# version 02
# baseline (speedup 1.0000x reference)
import numpy as np
import concourse.bass as bass
import concourse.mybir as mybir
from concourse.bass_utils import run_bass_kernel_spmd

F32 = mybir.dt.float32
BF16 = mybir.dt.bfloat16
I32 = mybir.dt.int32
U8 = mybir.dt.uint8
AF = mybir.ActivationFunctionType
ALU = mybir.AluOpType
AX = mybir.AxisListType

H = 2048
S = 4096
B = 4
NT = 16
NCORES = 8
EPS_LN = 1e-5
EPS_B = 1e-6
NEG = -1e30

ENGINES = ("sync", "tensor", "vector", "scalar", "gpsimd")


class Sem:
    def __init__(self, nc, name):
        self.h = nc.alloc_semaphore(name=name)
        self.c = 0
        self.name = name

    def inc(self, k):
        self.c += k
        return self.c


class Rec:
    def __init__(self, nc):
        self.nc = nc
        self.ops = {e: [] for e in ENGINES}

    def emit(self, eng, builder=None, waits=(), inc=None):
        waits = [(s.h, int(v)) for (s, v) in waits if v > 0]
        if inc is not None:
            incs = inc if isinstance(inc, list) else [inc]
            for s, k in incs:
                s.inc(k)
            incs = [(s.h, int(k)) for (s, k) in incs]
        else:
            incs = []

        def fn(e, waits=waits, incs=incs, builder=builder):
            for h, v in waits:
                e.wait_ge(h, v)
            if builder is not None:
                ins = builder(e)
                for h, k in incs:
                    ins.then_inc(h, k)

        self.ops[eng].append(fn)

    def dma(self, eng, out, in_, waits=(), inc=None, **kw):
        self.emit(eng, lambda e: e.dma_start(out=out, in_=in_, **kw),
                  waits=waits, inc=inc)

    def replay(self):
        with self.nc.Block() as block:
            for eng in ENGINES:
                fns = self.ops[eng]
                getattr(block, eng)(lambda engine, fns=fns: [f(engine) for f in fns])


class Stream:

    def __init__(self, nc, name, nslots=2):
        self.sems = [Sem(nc, f"{name}{i}") for i in range(nslots)]

    def batch(self, rec, eng, slot, pairs, waits=(), **kw):
        s = self.sems[slot]
        for i, (o, inp) in enumerate(pairs):
            rec.dma(eng, o, inp, waits=(waits if i == 0 else ()), inc=(s, 16), **kw)
        return (s, s.c)


def build_program(mode="full"):
    nc = bass.Bass("TRN2", target_bir_lowering=False)
    ap = lambda t: t.ap()

    x_my = ap(nc.dram_tensor("x_my", [NT, 128, H], F32, kind="ExternalInput"))
    masks = ap(nc.dram_tensor("masks", [NT, 128, 256], F32, kind="ExternalInput"))
    qkv = ap(nc.dram_tensor("qkv", [H, 3 * H], F32, kind="ExternalInput"))
    oproj = ap(nc.dram_tensor("oproj", [H, H], F32, kind="ExternalInput"))
    gamma = ap(nc.dram_tensor("gamma", [1, H], F32, kind="ExternalInput"))
    beta = ap(nc.dram_tensor("beta", [1, H], F32, kind="ExternalInput"))
    bw_in = {k: ap(nc.dram_tensor(f"b{k}w", [H, H], F32, kind="ExternalInput"))
             for k in ("q", "k", "v", "o")}
    bb_in = {k: ap(nc.dram_tensor(f"b{k}b", [1, H], F32, kind="ExternalInput"))
             for k in ("q", "k", "v", "o")}
    gpk = ap(nc.dram_tensor("gpk", [2048, 1024], I32, kind="ExternalInput"))
    gsc = ap(nc.dram_tensor("gsc", [2048, 64], F32, kind="ExternalInput"))
    upk = ap(nc.dram_tensor("upk", [2048, 1024], I32, kind="ExternalInput"))
    usc = ap(nc.dram_tensor("usc", [2048, 64], F32, kind="ExternalInput"))
    dpk = ap(nc.dram_tensor("dpk", [2048, 1024], I32, kind="ExternalInput"))
    dsc = ap(nc.dram_tensor("dsc", [2048, 64], F32, kind="ExternalInput"))

    out_my = ap(nc.dram_tensor("out_my", [NT, 128, H], F32, kind="ExternalOutput"))

    dr = lambda name, shape, dt=BF16: ap(nc.dram_tensor(name, shape, dt, kind="Internal"))
    wg_sl = dr("wg_sl", [16 * 128, 2048])
    wu_sl = dr("wu_sl", [16 * 128, 2048])
    wd_sl = dr("wd_sl", [16 * 128, 2048])
    drs = lambda name, shape, dt=BF16: ap(nc.dram_tensor(name, shape, dt, kind="Internal", addr_space="Shared"))
    wg_all = drs("wg_all", [NCORES * 16 * 128, 2048])
    wu_all = drs("wu_all", [NCORES * 16 * 128, 2048])
    wd_all = drs("wd_all", [NCORES * 16 * 128, 2048])
    inter_sc = dr("inter_sc", [128 * 128, 2048])
    xb1 = ap(nc.dram_tensor("xb1", [NT, 128, H], F32, kind="Internal"))
    xb2 = ap(nc.dram_tensor("xb2", [NT, 128, H], F32, kind="Internal"))
    xb3 = ap(nc.dram_tensor("xb3", [NT, 128, H], F32, kind="Internal"))
    qT_sc = dr("qT_sc", [16 * 128, 2048])
    kT_sc = dr("kT_sc", [16 * 128, 2048])
    v_sc = dr("v_sc", [16 * 128, 2048])
    kT_all = dr("kT_all", [2 * 16 * 128, 2048])
    v_all = dr("v_all", [2 * 16 * 128, 2048])
    w_bf = {k: dr(f"{k}w_bf", [H, H]) for k in ("q", "k", "v", "o")}
    kp_sc = dr("kp_sc", [16 * 128, 2048])
    vp_sc = dr("vp_sc", [16 * 128, 2048])
    kv_part = ap(nc.dram_tensor("kv_part", [128, 2064], F32, kind="Internal"))
    kv_sum = ap(nc.dram_tensor("kv_sum", [128, 2064], F32, kind="Internal"))

    rec = Rec(nc)
    sPE = Sem(nc, "sPE")
    sV = Sem(nc, "sV")
    sSC = Sem(nc, "sSC")
    sGP = Sem(nc, "sGP")
    sCC = Sem(nc, "sCC")
    stX = Stream(nc, "stX", 2)
    stW = Stream(nc, "stW", 2)
    stI = Stream(nc, "stI", 2)
    stP = Stream(nc, "stP", 2)
    stO = Stream(nc, "stO", 2)
    stD = Stream(nc, "stD", 1)
    stS = Stream(nc, "stS", 2)
    stEvB = Stream(nc, "stEvB", 2)

    def ostores():
        return [(stO.sems[0], stO.sems[0].c), (stO.sems[1], stO.sems[1].c)]

    P = [nc.alloc_psum_tensor(f"P{i}", [128, 1024], F32) for i in range(4)]
    TRA = P[3][:, 0:64].bitcast(BF16)
    TRB = P[3][:, 512:576].bitcast(BF16)

    alloc = nc.alloc_sbuf_tensor
    ident = alloc("ident", [128, 128], BF16)
    onesf = alloc("onesf", [1, 128], F32)

    A1 = alloc("A1", [128, 65536], U8)
    A2 = alloc("A2", [128, 49152], U8)
    A3 = alloc("A3", [128, 32768], U8)
    A4 = alloc("A4", [128, 16384], U8)
    NB = {F32: 4, BF16: 2, I32: 4, U8: 1}

    def V(ar, off, n, dt):
        return ar[:, off:off + n * NB[dt]].bitcast(dt)

    dq = {
        "pk": alloc("dq_pk", [128, 1024], I32),
        "sc": alloc("dq_sc", [128, 64], F32),
        "f": alloc("dq_f", [128, 1024], F32),
        "m": alloc("dq_m", [128, 1024], F32),
        "t0": alloc("dq_t0", [128, 1024], F32),
        "t1": alloc("dq_t1", [128, 1024], F32),
        "ge": alloc("dq_ge", [128, 1024], F32),
        "l4": alloc("dq_l4", [128, 1024], U8),
        "l6": alloc("dq_l6", [128, 1024], U8),
        "vals": alloc("dq_vals", [128, 2048], BF16),
        "stage": alloc("dq_stage", [128, 2048], BF16),
    }
    dq["ci"] = dq["t1"][:].bitcast(I32)
    lnb = {
        "mu": alloc("ln_mu", [128, 1], F32),
        "ss": alloc("ln_ss", [128, 1], F32),
        "rs": alloc("ln_rs", [128, 1], F32),
        "xc": alloc("ln_xc", [128, H], F32),
        "sq": V(A4, 0, H, F32),
    }

    def emit_identity():
        rec.emit("gpsimd", lambda g: g.memset(onesf[:], 1.0))
        rec.emit("gpsimd", lambda g: g.memset(ident[:], 0.0))
        rec.emit("gpsimd", lambda g: g.affine_select(
            out=ident[:], in_=ident[:], compare_op=ALU.not_equal, fill=1.0,
            base=0, pattern=[[-1, 128]], channel_multiplier=1), inc=(sGP, 1))

    def deq_tile_emitters(which, pk_d, sc_d, sl_d):
        out = []
        state = {}

        def mk(t):
            def vec_part():
                emit_deq_vec(which, pk_d, sc_d, t, state)

            def pe_part():
                emit_deq_pe(which, sl_d, t, state)
            return vec_part, pe_part
        for t in range(16):
            out.append(mk(t))
        return out

    def emit_dequant(which, pk_d, sc_d, sl_d):
        for v, p in deq_tile_emitters(which, pk_d, sc_d, sl_d):
            v()
            p()

    def emit_deq_vec(which, pk_d, sc_d, t, state):
        if True:
            lw = stP.batch(rec, "scalar", 0,
                           [(dq["pk"][:], pk_d[t * 128:(t + 1) * 128, :]),
                            (dq["sc"][:], sc_d[t * 128:(t + 1) * 128, :])],
                           waits=[(sV, sV.c), (sPE, sPE.c)])
            v = "vector"
            rec.emit(v, waits=[lw])
            for pl in (0, 1):
                if pl == 0:
                    rec.emit(v, lambda e: e.tensor_single_scalar(dq["ci"][:], dq["pk"][:], 15, ALU.bitwise_and))
                else:
                    rec.emit(v, lambda e: e.tensor_single_scalar(dq["ci"][:], dq["pk"][:], 4, ALU.logical_shift_right))
                    rec.emit(v, lambda e: e.tensor_single_scalar(dq["ci"][:], dq["ci"][:], 15, ALU.bitwise_and))
                rec.emit(v, lambda e: e.tensor_copy(dq["f"][:], dq["ci"][:]))
                rec.emit(v, lambda e: e.tensor_single_scalar(dq["ge"][:], dq["f"][:], 8.0, ALU.is_ge))
                rec.emit(v, lambda e: e.scalar_tensor_tensor(dq["m"][:], dq["ge"][:], -8.0, dq["f"][:], ALU.mult, ALU.add))
                rec.emit(v, lambda e: e.tensor_single_scalar(dq["l4"][:], dq["m"][:], 4.0, ALU.is_lt))
                rec.emit(v, lambda e: e.tensor_single_scalar(dq["l6"][:], dq["m"][:], 6.0, ALU.is_lt))
                rec.emit(v, lambda e: e.tensor_scalar(dq["t0"][:], dq["m"][:], 2.0, -8.0, ALU.mult, ALU.add))
                rec.emit(v, lambda e: e.tensor_scalar_sub(dq["t1"][:], dq["m"][:], 2.0))
                rec.emit(v, lambda e: e.select(dq["t0"][:], dq["l6"][:], dq["t1"][:], dq["t0"][:]))
                rec.emit(v, lambda e: e.tensor_scalar_mul(dq["t1"][:], dq["m"][:], 0.5))
                rec.emit(v, lambda e: e.select(dq["t0"][:], dq["l4"][:], dq["t1"][:], dq["t0"][:]))
                rec.emit(v, lambda e: e.tensor_scalar(dq["ge"][:], dq["ge"][:], -2.0, 1.0, ALU.mult, ALU.add))
                rec.emit(v, lambda e: e.tensor_mul(dq["t0"][:], dq["t0"][:], dq["ge"][:]))
                o3 = dq["vals"][:, pl::2].rearrange("p (b j) -> p b j", j=16)
                i3 = dq["t0"][:].rearrange("p (b j) -> p b j", j=16)
                s3 = dq["sc"][:, :, None].broadcast_to([128, 64, 16])
                rec.emit(v, lambda e, o3=o3, i3=i3, s3=s3: e.tensor_tensor(o3, i3, s3, ALU.mult),
                         inc=(sV, 1))
            state[t] = sV.c

    def emit_deq_pe(which, sl_d, t, state):
        if True:
            v_done = state[t]
            for c in range(16):
                tr = TRA if c % 2 == 0 else TRB
                rec.emit("tensor", lambda e, c=c, tr=tr: e.transpose(
                    tr, dq["vals"][:, c * 128:(c + 1) * 128], ident[:]),
                    waits=[(sV, v_done), (sGP, 1), (sSC, sSC.c)], inc=(sPE, 1))
                pe_now = sPE.c
                rec.emit("scalar", lambda e, c=c, tr=tr: e.copy(dq["stage"][:, c * 128:(c + 1) * 128], tr),
                         waits=[(sPE, pe_now)], inc=(sSC, 1))
            sc_done = sSC.c
            if which in ("g", "u"):
                stw = stD.batch(rec, "sync", 0, [(sl_d[t * 128:(t + 1) * 128, :], dq["stage"][:])],
                                waits=[(sSC, sc_done)])
            else:
                stw = stD.batch(rec, "sync", 0,
                                [(sl_d[c * 128:(c + 1) * 128, t * 128:(t + 1) * 128],
                                  dq["stage"][:, c * 128:(c + 1) * 128]) for c in range(16)],
                                waits=[(sSC, sc_done)])
            rec.emit("scalar", waits=[stw])

    def emit_deq_gather(sl_d, all_d):
        sD = stD.sems[0]
        rec.emit("gpsimd", lambda g, sl_d=sl_d, all_d=all_d: g.collective_compute(
            "AllGather", ALU.bypass, replica_groups=[list(range(NCORES))],
            ins=[sl_d], outs=[all_d]),
            waits=[(sD, sD.c)], inc=(sCC, 1))
        return sCC.c

    def emit_ln(x_sb, xn_out, stats_to=None, eps=EPS_LN, xc=None):
        b = lnb
        xc = xc if xc is not None else b["xc"]
        rec.emit("vector", lambda e: e.reduce_sum(b["ss"][:], x_sb[:], axis=AX.X), inc=(sV, 1))
        rec.emit("vector", lambda e: e.tensor_scalar_mul(b["mu"][:], b["ss"][:], 1.0 / H),
                 waits=[(sV, sV.c)], inc=(sV, 1))
        rec.emit("vector", lambda e, xc=xc: e.tensor_scalar(xc[:], x_sb[:], b["mu"][:, 0:1], None, ALU.subtract),
                 waits=[(sV, sV.c)], inc=(sV, 1))
        rec.emit("scalar", lambda e, xc=xc: e.activation(b["sq"][:], xc[:], AF.Square, accum_out=b["ss"][:]),
                 waits=[(sV, sV.c)], inc=(sSC, 1))
        rec.emit("vector", lambda e, eps=eps: e.tensor_scalar(b["ss"][:], b["ss"][:], 1.0 / H, float(eps), ALU.mult, ALU.add),
                 waits=[(sSC, sSC.c)], inc=(sV, 1))
        rec.emit("scalar", lambda e: e.activation(b["ss"][:], b["ss"][:], AF.Ln),
                 waits=[(sV, sV.c)], inc=(sSC, 1))
        rec.emit("vector", lambda e: e.tensor_scalar_mul(b["ss"][:], b["ss"][:], -0.5),
                 waits=[(sSC, sSC.c)], inc=(sV, 1))
        rec.emit("scalar", lambda e: e.activation(b["rs"][:], b["ss"][:], AF.Exp),
                 waits=[(sV, sV.c)], inc=(sSC, 1))
        rec.emit("vector", lambda e, xc=xc: e.tensor_scalar(xn_out[:], xc[:], b["rs"][:, 0:1], None, ALU.mult),
                 waits=[(sSC, sSC.c)], inc=(sV, 1))
        if stats_to is not None:
            rec.emit("vector", lambda e: e.tensor_scalar_add(stats_to[:, 0:1], b["rs"][:], 1.0))
            rec.emit("vector", lambda e: e.tensor_tensor(stats_to[:, 1:2], b["mu"][:], b["rs"][:], ALU.mult))
            rec.emit("vector", lambda e: e.tensor_scalar_mul(stats_to[:, 1:2], stats_to[:, 1:2], -1.0), inc=(sV, 1))

    ff = {
        "x": V(A3, 0, H, F32),
        "xn": V(A3, 8192, H, BF16),
        "xnT": V(A1, 0, 16 * H, BF16),
        "wg": [V(A2, 0, 2048, BF16), V(A2, 4096, 2048, BF16)],
        "wu": [V(A2, 8192, 2048, BF16), V(A2, 12288, 2048, BF16)],
        "gl": V(A2, 16384, 1024, F32),
        "int": [V(A2, 20480, 2048, BF16), V(A2, 24576, 2048, BF16)],
    "ic": [V(A2, 28672, 1024, BF16), V(A2, 30720, 1024, BF16)],
        "wd": [V(A2, 32768, 512, BF16), V(A2, 33792, 512, BF16)],
        "xp": [V(A2, 34816, 512, F32), V(A2, 36864, 512, F32)],
        "op": [V(A2, 38912, 512, F32), V(A2, 40960, 512, F32)],
        "ic4": [V(A2, 0, 4096, BF16), V(A2, 8192, 4096, BF16)],
        "wd4": [V(A2, 16384, 2048, BF16), V(A2, 20480, 2048, BF16)],
    }

    aa = {
        "xn1": V(A3, 0, H, F32),
        "xng": V(A3, 8192, H, BF16),
        "grep": V(A3, 12288, H, F32),
        "brep": V(A3, 20480, H, F32),
        "lnst": alloc("aa_lnst", [128, 2 * NT], F32),
        "qkvb": [V(A2, 16384, 2048, BF16), V(A2, 20480, 2048, BF16)],
        "vwb": [V(A2, 0, 16 * 512, BF16), V(A2, 16384 - 8192 + 0 * 8192 + 8192, 0, BF16)],
        "ev": [alloc("aa_ev0", [128, 512], BF16), alloc("aa_ev1", [128, 512], BF16)],
        "qsb": [V(A2, 40960, 16 * 128, BF16), V(A2, 45056, 16 * 128, BF16)],
        "ksb": [V(A2, 0, 16 * 256, BF16), V(A2, 8192, 16 * 256, BF16)],
        "vsb": [V(A2, 16384, 2 * 2048, BF16), V(A2, 24576, 2 * 2048, BF16)],
        "msk": V(A2, 32768, NT * 256, BF16),
        "srow": V(A3, 0, 4096, F32),
        "prow": V(A4, 0, 4096, BF16),
        "nmax": alloc("aa_nmax", [128, 1], F32),
        "z": alloc("aa_z", [128, 1], F32),
        "rz": alloc("aa_rz", [128, 1], F32),
        "pt": [alloc("aa_pt0", [128, 128], BF16), alloc("aa_pt1", [128, 128], BF16)],
        "pvn": V(A4, 8192, 2048, BF16),
        "pvt": V(A4, 12288, 16 * 128, BF16),
        "opb": V(A1, 0, 16 * 2048, BF16),
        "xf": V(A3, 16384, H, F32),
        "o1": V(A3, 24576, H, F32),
    }
    aa["vwb"] = [V(A2, 0, 16 * 512, BF16), V(A2, 8192, 16 * 512, BF16)]
    aa["xn2"] = aa["xn1"]

    def emit_attna(src, dst, par_dummy=None, adbg=None, weave=None):
        wv = list(weave) if weave else []
        wstate = {"vq": [], "k": 0}

        def weave_step():
            if wstate["k"] < len(wv):
                wv[wstate["k"]][0]()
                wstate["vq"].append(wv[wstate["k"]][1])
                wstate["k"] += 1
            if len(wstate["vq"]) > 1:
                wstate["vq"].pop(0)()

        def weave_flush():
            while wstate["k"] < len(wv):
                weave_step()
            while wstate["vq"]:
                wstate["vq"].pop(0)()
        xngT = ff["xnT"]
        lw_g = stD.batch(rec, "sync", 0, [(aa["grep"][:], gamma.partition_broadcast(128))])
        lw_g = stD.batch(rec, "sync", 0, [(aa["brep"][:], beta.partition_broadcast(128))])
        stMK = Stream(nc, "stMK", 1)
        lw_mk = stMK.batch(rec, "gpsimd", 0, [(aa["msk"][:].rearrange("p (i c) -> p i c", c=256), masks.transpose([1, 0, 2]))])
        stOP = Stream(nc, "stOP", 1)

        for i in range(NT):
            weave_step()
            lw = stX.batch(rec, "sync", i % 2, [(ff["x"][:], src[i])],
                           waits=[(sV, sV.c), (sSC, sSC.c)])
            rec.emit("vector", waits=[lw, lw_g])
            emit_ln(ff["x"], aa["xn1"], stats_to=aa["lnst"][:, 2 * i:2 * i + 2])
            emit_ln(aa["xn1"], aa["xn1"])
            rec.emit("vector", lambda e: e.tensor_mul(aa["xn1"][:], aa["xn1"][:], aa["grep"][:]))
            rec.emit("vector", lambda e: e.tensor_add(aa["xng"][:], aa["xn1"][:], aa["brep"][:]), inc=(sV, 1))
            vr = sV.c
            for c in range(16):
                tr = TRA if c % 2 == 0 else TRB
                rec.emit("tensor", lambda e, c=c, tr=tr: e.transpose(
                    tr, aa["xng"][:, c * 128:(c + 1) * 128], ident[:]),
                    waits=[(sV, vr), (sGP, 1), (sSC, sSC.c)], inc=(sPE, 1))
                pe = sPE.c
                rec.emit("scalar", lambda e, c=c, i=i, tr=tr: e.copy(
                    xngT[:, c * H + i * 128: c * H + (i + 1) * 128], tr),
                    waits=[(sPE, pe)], inc=(sSC, 1))
        xngT_sc = sSC.c

        qkT_sc = {0: qT_sc, 1: kT_sc}
        stEv = Stream(nc, "stEv", 2)
        ev_guard = [(sSC, 0), (sSC, 0)]
        for dch in range(32):
            weave_step()
            p = dch % 2
            src_cols = qkv[:, (dch % 16) * 128 + (dch // 16) * 2048:][:, 0:128]
            lw = stW.batch(rec, "gpsimd", p,
                           [(aa["qkvb"][p][:].rearrange("p (c d) -> p c d", d=128),
                             src_cols.rearrange("(c p) d -> p c d", p=128))],
                           waits=[(sPE, sPE.c)])
            for tch in range(4):
                bank = P[p] if tch < 2 else P[2 + p]
                reg = bank[:, (tch % 2) * 512:(tch % 2) * 512 + 512]
                w = [lw, (sSC, xngT_sc), ev_guard[p]]
                rec.emit("tensor", waits=w)
                for c in range(16):
                    rec.emit("tensor", lambda e, p=p, c=c, tch=tch, reg=reg: e.matmul(
                        reg, aa["qkvb"][p][:, c * 128:(c + 1) * 128],
                        xngT[:, c * H + tch * 512: c * H + (tch + 1) * 512],
                        start=(c == 0), stop=(c == 15)),
                        inc=(sPE, 1) if c == 15 else None)
                pe = sPE.c
                ep = tch % 2
                rec.emit("scalar", lambda e, reg=reg, ep=ep: e.copy(aa["ev"][ep][:], reg),
                         waits=[(sPE, pe), (stEv.sems[ep], stEv.sems[ep].c)], inc=(sSC, 1))
                stEv.batch(rec, "sync", ep,
                           [(qkT_sc[dch // 16][(dch % 16) * 128:(dch % 16 + 1) * 128, tch * 512:(tch + 1) * 512],
                             aa["ev"][ep][:])],
                           waits=[(sSC, sSC.c)])
            ev_guard[p] = (sSC, sSC.c)

        weave_flush()
        for half in range(2):
            for dch4 in range(4):
                p = dch4 % 2
                src_cols = qkv[:, 2 * H + dch4 * 512: 2 * H + (dch4 + 1) * 512]
                lw = stW.batch(rec, "gpsimd", p,
                               [(aa["vwb"][p][:].rearrange("p (c d) -> p c d", d=512),
                                 src_cols.rearrange("(c p) d -> p c d", p=128))],
                               waits=[(sPE, sPE.c)])
                rec.emit("tensor", waits=[lw, (sV, sV.c), (sSC, sSC.c)])
                for c in range(16):
                    for t in range(8):
                        rec.emit("tensor", lambda e, p=p, c=c, t=t, half=half: e.matmul(
                            P[t // 2][:, (t % 2) * 512:(t % 2) * 512 + 512],
                            xngT[:, c * H + half * 1024 + t * 128: c * H + half * 1024 + (t + 1) * 128],
                            aa["vwb"][p][:, c * 512:(c + 1) * 512],
                            start=(c == 0), stop=(c == 15)),
                            inc=(sPE, 1) if (c == 15 and t == 7) else None)
                pe = sPE.c
                for t in range(8):
                    ep = t % 2
                    rec.emit("scalar", lambda e, t=t, ep=ep: e.copy(
                        aa["ev"][ep][:], P[t // 2][:, (t % 2) * 512:(t % 2) * 512 + 512]),
                        waits=[(sPE, pe), (stEv.sems[ep], stEv.sems[ep].c)], inc=(sSC, 1))
                    stEv.batch(rec, "sync", ep,
                               [(v_sc[(half * 8 + t) * 128:(half * 8 + t + 1) * 128, dch4 * 512:(dch4 + 1) * 512],
                                 aa["ev"][ep][:])],
                               waits=[(sSC, sSC.c)])

        opw = stOP.batch(rec, "gpsimd", 0,
                         [(aa["opb"][:, c * 2048:(c + 1) * 2048],
                           oproj[c * 128:(c + 1) * 128, :]) for c in range(16)],
                         waits=[(sPE, sPE.c)])

        pair_groups = [[0, 1], [2, 3], [4, 5], [6, 7]]
        ev_all = [(sm, sm.c) for sm in stEv.sems]
        rec.emit("gpsimd", lambda g: g.collective_compute(
            "AllGather", ALU.bypass, replica_groups=pair_groups,
            ins=[kT_sc], outs=[kT_all]), waits=ev_all, inc=(sCC, 1))
        rec.emit("gpsimd", lambda g: g.collective_compute(
            "AllGather", ALU.bypass, replica_groups=pair_groups,
            ins=[v_sc], outs=[v_all]), inc=(sCC, 1))
        ag_cc = sCC.c
        if adbg is not None:
            rec.dma("sync", adbg["qT"], qT_sc, waits=[(sCC, ag_cc)], inc=(stO.sems[0], 16))
            rec.dma("sync", adbg["kTa"], kT_all, inc=(stO.sems[0], 16))
            rec.dma("sync", adbg["va"], v_all, inc=(stO.sems[0], 16))

        kT_r = kT_all.rearrange("(r c p) t -> r c p t", r=2, p=128)
        v_r = v_all.rearrange("(r i p) d -> r i p d", r=2, p=128)

        stQT = Stream(nc, "stQT", 2)
        stKP = Stream(nc, "stKP", 2)
        stVP = Stream(nc, "stVP", 2)
        S_reg = [P[0][:, 0:256], P[0][:, 512:768]]
        s_guard = [(sSC, 0), (sSC, 0)]
        o1_prev = (sV, 0)
        q_guard = [(sPE, 0), (sPE, 0)]
        k_guard = [(sPE, 0), (sPE, 0)]
        v_guard = [(sPE, 0), (sPE, 0)]
        for i in range(NT):
            ip = i % 2
            npan = i + 1
            lq = stQT.batch(rec, "sync", ip,
                            [(aa["qsb"][ip][:].rearrange("p (c t) -> p c t", t=128),
                              qT_sc.rearrange("(c p) t -> p c t", p=128)[:, :, i * 128:(i + 1) * 128])],
                            waits=[q_guard[ip], (sCC, ag_cc)])
            vmax_prev = sV.c
            for pan in range(npan):
                pp = pan % 2
                prs = []
                for gsub in range(2):
                    gc = 2 * pan + gsub
                    rr = gc % 2
                    lt = gc // 2
                    prs.append((aa["ksb"][pp][:].rearrange("p (c t) -> p c t", t=256)[:, :, gsub * 128:(gsub + 1) * 128],
                                kT_r[rr, :, :, lt * 128:(lt + 1) * 128].transpose([1, 0, 2])))
                lk = stKP.batch(rec, "sync", pp, prs, waits=[k_guard[pp], (sCC, ag_cc)])
                w = [lq, lk, s_guard[pp]]
                rec.emit("tensor", waits=w)
                for c in range(16):
                    rec.emit("tensor", lambda e, ip=ip, pp=pp, c=c: e.matmul(
                        S_reg[pp], aa["qsb"][ip][:, c * 128:(c + 1) * 128],
                        aa["ksb"][pp][:, c * 256:(c + 1) * 256],
                        start=(c == 0), stop=(c == 15)),
                        inc=(sPE, 1) if c == 15 else None)
                k_guard[pp] = (sPE, sPE.c)
                pe = sPE.c
                if pan == npan - 1:
                    rec.emit("vector", lambda e, i=i, pan=pan, pp=pp: e.scalar_tensor_tensor(
                        aa["srow"][:, pan * 256:(pan + 1) * 256], S_reg[pp], 1.0 / 45.254833995939045,
                        aa["msk"][:, i * 256:(i + 1) * 256], ALU.mult, ALU.add),
                        waits=[(sPE, pe), lw_mk], inc=(sV, 1))
                    s_guard[pp] = (sV, sV.c)
                else:
                    rec.emit("vector", lambda e, pan=pan, pp=pp: e.tensor_scalar_mul(
                        aa["srow"][:, pan * 256:(pan + 1) * 256], S_reg[pp], 1.0 / 45.254833995939045),
                        waits=[(sPE, pe)], inc=(sV, 1))
                    s_guard[pp] = (sV, sV.c)
            rec.emit("vector", lambda e, i=i, npan=npan: e.reduce_max(
                aa["nmax"][:], aa["srow"][:, 0:npan * 256], axis=AX.X, negate=True),
                waits=[(sSC, sSC.c), (sV, sV.c)], inc=(sV, 1))
            rec.emit("scalar", lambda e, npan=npan: e.activation(
                aa["prow"][:, 0:npan * 256], aa["srow"][:, 0:npan * 256], AF.Exp,
                bias=aa["nmax"][:, 0:1], accum_out=aa["z"][:]),
                waits=[(sV, sV.c)], inc=(sSC, 1))
            rec.emit("vector", lambda e: e.reciprocal(aa["rz"][:], aa["z"][:]),
                     waits=[(sSC, sSC.c)], inc=(sV, 1))
            rz_v = sV.c
            if adbg is not None and i == 3:
                rec.dma("sync", adbg["srow"], aa["srow"][:], waits=[(sV, rz_v)], inc=(stO.sems[0], 16))
                rec.dma("sync", adbg["prow"], aa["prow"][:], inc=(stO.sems[0], 16))
                rec.dma("sync", adbg["z"], aa["z"][:], inc=(stO.sems[0], 16))
                rec.emit("scalar", waits=[(stO.sems[0], stO.sems[0].c)])
                rec.emit("vector", waits=[(stO.sems[0], stO.sems[0].c)])
            prow_sc = sSC.c
            for pan in range(npan):
                pp = pan % 2
                prs = []
                for gsub in range(2):
                    gc = 2 * pan + gsub
                    rr = gc % 2
                    lt = gc // 2
                    prs.append((aa["vsb"][pp][:, gsub * 2048:(gsub + 1) * 2048],
                                v_r[rr, lt]))
                lv = stVP.batch(rec, "sync", pp, prs, waits=[v_guard[pp], (sCC, ag_cc)])
                for gsub in range(2):
                    b = 2 * pan + gsub
                    tp = b % 2
                    tr = TRA if tp == 0 else TRB
                    rec.emit("tensor", lambda e, b=b, tr=tr: e.transpose(
                        tr, aa["prow"][:, b * 128:(b + 1) * 128], ident[:]),
                        waits=[(sSC, sSC.c), (sGP, 1)], inc=(sPE, 1))
                    pe = sPE.c
                    rec.emit("scalar", lambda e, tp=tp, tr=tr: e.copy(aa["pt"][tp][:], tr),
                             waits=[(sPE, pe)], inc=(sSC, 1))
                    ptsc = sSC.c
                    w = [lv, (sSC, ptsc)]
                    if b == 0:
                        w.append(o1_prev)
                    rec.emit("tensor", waits=w)
                    for d4 in range(4):
                        rec.emit("tensor", lambda e, tp=tp, gsub=gsub, d4=d4, b=b, npan=npan: e.matmul(
                            P[1 + d4 // 2][:, (d4 % 2) * 512:(d4 % 2) * 512 + 512],
                            aa["pt"][tp][:],
                            aa["vsb"][(b // 2) % 2][:, gsub * 2048 + d4 * 512: gsub * 2048 + (d4 + 1) * 512],
                            start=(b == 0), stop=(b == 2 * npan - 1)),
                            inc=(sPE, 1) if d4 == 3 else None)
                    v_guard[pp] = (sPE, sPE.c)
            pv_pe = sPE.c
            rec.emit("vector", lambda e: e.tensor_scalar(
                aa["pvn"][:, 0:1024], P[1][:], aa["rz"][:, 0:1], None, ALU.mult),
                waits=[(sPE, pv_pe), (sV, rz_v)], inc=(sV, 1))
            rec.emit("vector", lambda e: e.tensor_scalar(
                aa["pvn"][:, 1024:2048], P[2][:], aa["rz"][:, 0:1], None, ALU.mult),
                inc=(sV, 1))
            pvn_v = sV.c
            for c in range(16):
                tr = TRA if c % 2 == 0 else TRB
                rec.emit("tensor", lambda e, c=c, tr=tr: e.transpose(
                    tr, aa["pvn"][:, c * 128:(c + 1) * 128], ident[:]),
                    waits=[(sV, pvn_v), (sSC, sSC.c)], inc=(sPE, 1))
                pe = sPE.c
                rec.emit("scalar", lambda e, c=c, tr=tr: e.copy(
                    aa["pvt"][:, c * 128:(c + 1) * 128], tr),
                    waits=[(sPE, pe)], inc=(sSC, 1))
            pvt_sc = sSC.c
            lx = stX.batch(rec, "sync", ip, [(aa["xf"][:], src[i])], waits=[(sV, sV.c)])
            rec.emit("tensor", waits=[(sSC, pvt_sc), opw])
            for oc in range(4):
                for c in range(16):
                    rec.emit("tensor", lambda e, oc=oc, c=c: e.matmul(
                        P[1 + oc // 2][:, (oc % 2) * 512:(oc % 2) * 512 + 512],
                        aa["pvt"][:, c * 128:(c + 1) * 128],
                        aa["opb"][:, c * 2048 + oc * 512: c * 2048 + (oc + 1) * 512],
                        start=(c == 0), stop=(c == 15)),
                        inc=(sPE, 1) if c == 15 else None)
            o_pe = sPE.c
            rec.emit("vector", lambda e, i=i: e.tensor_scalar(
                aa["o1"][:], aa["xf"][:], aa["lnst"][:, 2 * i:2 * i + 1], aa["lnst"][:, 2 * i + 1:2 * i + 2],
                ALU.mult, ALU.add),
                waits=[lx, (sV, sV.c), (stO.sems[0], stO.sems[0].c), (stO.sems[1], stO.sems[1].c)],
                inc=(sV, 1))
            rec.emit("vector", lambda e: e.tensor_add(aa["o1"][:, 0:1024], aa["o1"][:, 0:1024], P[1][:]),
                     waits=[(sPE, o_pe)], inc=(sV, 1))
            rec.emit("vector", lambda e: e.tensor_add(aa["o1"][:, 1024:2048], aa["o1"][:, 1024:2048], P[2][:]),
                     inc=(sV, 1))
            o1_prev = (sV, sV.c)
            q_guard[ip] = (sPE, o_pe)
            ow = stO.sems[ip]
            rec.emit("vector", waits=[(ow, ow.c)])
            stO.batch(rec, "sync", ip, [(dst[i], aa["o1"][:])], waits=[(sV, sV.c)])

    ab = {
        "xn3": V(A3, 8192, H, BF16),
        "kbr": V(A3, 12288, H, F32),
        "vbr": V(A3, 20480, H, F32),
        "obr": V(A3, 12288, H, F32),
        "x": V(A3, 0, H, F32),
        "o1": V(A3, 20480, H, F32),
        "wc": [V(A2, 0, 512, BF16), V(A2, 1024, 512, BF16)],
        "pb": [V(A2, 0, H, BF16), V(A2, 4096, H, BF16)],
        "kpt": V(A2, 4096, H, BF16),
        "vpt": V(A2, 8192, H, BF16),
        "kvacc": V(A2, 12288, 2064, F32),
        "kvsb": V(A2, 20736, 2064, BF16),
        "qwc": [V(A2, 25088, 512, BF16), V(A2, 26112, 512, BF16)],
        "owc": [V(A2, 27136, 512, BF16), V(A2, 28160, 512, BF16)],
        "rzb": V(A2, 29184, 512, F32),
        "qpT": V(A2, 31232, 16 * 512, BF16),
        "atn": [V(A4, 8192, 512, BF16), V(A4, 9216, 512, BF16)],
        "el0": V(A4, 10240, 512, F32),
        "el1": V(A4, 12288, 512, F32),
        "msk8": V(A4, 14336, 512, U8),
        "qbc": alloc("ab_qbc", [128, 16], F32),
        "ones": alloc("ab_ones", [128, 1], BF16),
    }
    ab["atnall"] = None

    pst_o = [[[P[(tt * 4 + eo) // 2][:, ((tt * 4 + eo) % 2) * 512:((tt * 4 + eo) % 2) * 512 + 512]
               for eo in range(4)] for tt in range(2)] for _ttp in range(2)]

    def emit_attnb(src, dst):
        stB = Stream(nc, "stB", 2)
        stKV = Stream(nc, "stKV", 2)
        xn3T = ff["xnT"]
        pb_guard = [(stB.sems[0], 0), (stB.sems[1], 0)]
        entry = [(sPE, sPE.c), (sV, sV.c)]
        for mi, mk in enumerate(("q", "k", "v", "o")):
            for st_ in range(16):
                p = st_ % 2
                w = [pb_guard[p]] + (entry if (mi == 0 and st_ < 2) else [])
                lw = stW.batch(rec, "gpsimd", p,
                               [(ab["pb"][p][:], bw_in[mk][st_ * 128:(st_ + 1) * 128, :])],
                               waits=w)
                pb_guard[p] = stB.batch(rec, "sync", p,
                                        [(w_bf[mk][st_ * 128:(st_ + 1) * 128, :], ab["pb"][p][:])],
                                        waits=[lw])
        wbf_done = [pb_guard[0], pb_guard[1]]
        lw_b = stD.batch(rec, "sync", 0, [(ab["kbr"][:], bb_in["k"].partition_broadcast(128))])
        lw_b = stD.batch(rec, "sync", 0, [(ab["vbr"][:], bb_in["v"].partition_broadcast(128))])
        rec.dma("gpsimd", ab["qbc"][:],
                bb_in["q"].rearrange("o (c p) -> o c p", p=128).transpose([0, 2, 1])[0],
                inc=(sGP, 16), allow_slow_non_contiguous=True)
        qbc_gp = sGP.c
        rec.emit("gpsimd", lambda g: g.memset(ab["ones"][:], 1.0), inc=(sGP, 1))
        ones_gp = sGP.c
        for i in range(NT):
            w = [(sV, sV.c), (sSC, sSC.c)] + (ostores() if i == 0 else [])
            lw = stX.batch(rec, "sync", i % 2, [(ab["x"][:], src[i])], waits=w)
            rec.emit("vector", waits=[lw])
            emit_ln(ab["x"], ab["xn3"])
            vr = sV.c
            for c in range(16):
                tr = TRA if c % 2 == 0 else TRB
                rec.emit("tensor", lambda e, c=c, tr=tr: e.transpose(
                    tr, ab["xn3"][:, c * 128:(c + 1) * 128], ident[:]),
                    waits=[(sV, vr), (sGP, 1), (sSC, sSC.c)], inc=(sPE, 1))
                pe = sPE.c
                rec.emit("scalar", lambda e, c=c, i=i, tr=tr: e.copy(
                    xn3T[:, c * H + i * 128: c * H + (i + 1) * 128], tr),
                    waits=[(sPE, pe)], inc=(sSC, 1))
        xn3T_sc = sSC.c

        pst = [P[t // 2][:, (t % 2) * 512:(t % 2 + 1) * 512] for t in range(8)]
        first_bd = True
        for mk, biasr, do_elu, dstp in (("k", ab["kbr"], True, kp_sc), ("v", ab["vbr"], False, vp_sc)):
            for half in range(2):
                for dch4 in range(4):
                    pe_base = sPE.c
                    for c in range(16):
                        p = c % 2
                        if c < 2:
                            w = [(sPE, pe_base)]
                            if first_bd:
                                w += [wbf_done[0], wbf_done[1], (sSC, xn3T_sc)]
                        else:
                            w = [(sPE, pe_base + (c - 1))]
                        lw = stI.batch(rec, "scalar", p,
                                       [(ab["wc"][p][:], w_bf[mk][dch4 * 512:(dch4 + 1) * 512, c * 128:(c + 1) * 128])],
                                       waits=w, transpose=True)
                        w2 = [lw]
                        if c == 0:
                            w2.append((sV, sV.c))
                        rec.emit("tensor", waits=w2)
                        for t in range(8):
                            rec.emit("tensor", lambda e, p=p, c=c, t=t, half=half: e.matmul(
                                pst[t], xn3T[:, c * H + half * 1024 + t * 128: c * H + half * 1024 + (t + 1) * 128],
                                ab["wc"][p][:],
                                start=(c == 0), stop=(c == 15)),
                                inc=(sPE, 1) if t == 7 else None)
                    first_bd = False
                    pe_done = sPE.c
                    for t in range(8):
                        ep = t % 2
                        sEv = stEvB.sems[ep]
                        rec.emit("vector", lambda e, t=t, ep=ep, biasr=biasr, dch4=dch4: e.tensor_tensor(
                            ab["el0"][:] if False else ab["el0"][:], pst[t], biasr[:, dch4 * 512:(dch4 + 1) * 512], ALU.add),
                            waits=[(sPE, pe_done), lw_b, (sEv, sEv.c), (sV, sV.c)], inc=(sV, 1))
                        if do_elu:
                            rec.emit("vector", lambda e: e.tensor_single_scalar(ab["msk8"][:], ab["el0"][:], 0.0, ALU.is_gt))
                            rec.emit("scalar", lambda e: e.activation(ab["el1"][:], ab["el0"][:], AF.Exp),
                                     waits=[(sV, sV.c)], inc=(sSC, 1))
                            rec.emit("vector", lambda e: e.tensor_scalar_add(ab["el0"][:], ab["el0"][:], 1.0),
                                     waits=[(sSC, sSC.c)])
                            rec.emit("vector", lambda e, ep=ep: e.select(aa["ev"][ep][:], ab["msk8"][:], ab["el0"][:], ab["el1"][:]),
                                     inc=(sV, 1))
                        else:
                            rec.emit("vector", lambda e, ep=ep: e.tensor_copy(aa["ev"][ep][:], ab["el0"][:]), inc=(sV, 1))
                        stEvB.batch(rec, "sync", ep,
                                    [(dstp[(half * 8 + t) * 128:(half * 8 + t + 1) * 128, dch4 * 512:(dch4 + 1) * 512],
                                      aa["ev"][ep][:])],
                                    waits=[(sV, sV.c)])
        evb_all = [(sm, sm.c) for sm in stEvB.sems]

        rec.emit("vector", lambda e: e.memset(ab["kvacc"][:], 0.0), inc=(sV, 1))
        for i in range(NT):
            p = i % 2
            lw = stKV.batch(rec, "sync", p,
                            [(ab["kpt"][:], kp_sc[i * 128:(i + 1) * 128, :])],
                            waits=(evb_all if i == 0 else [(sPE, kv_pe)]))
            lw2 = stKV.batch(rec, "sync", p, [(ab["vpt"][:], vp_sc[i * 128:(i + 1) * 128, :])])
            w = [lw, lw2, (sGP, ones_gp)]
            if i > 0:
                w.append((sV, sV.c))
            rec.emit("tensor", waits=w)
            for h in range(16):
                rec.emit("tensor", lambda e, h=h: e.matmul(
                    P[h // 8][:, (h % 8) * 128:(h % 8) * 128 + 128],
                    ab["kpt"][:, h * 128:(h + 1) * 128],
                    ab["vpt"][:, h * 128:(h + 1) * 128],
                    start=True, stop=True))
                rec.emit("tensor", lambda e, h=h: e.matmul(
                    P[2][:, h:h + 1],
                    ab["kpt"][:, h * 128:(h + 1) * 128],
                    ab["ones"][:],
                    start=True, stop=True),
                    inc=(sPE, 1) if h == 15 else None)
            kv_pe = sPE.c
            rec.emit("vector", lambda e: e.tensor_add(ab["kvacc"][:, 0:1024], ab["kvacc"][:, 0:1024], P[0][:]),
                     waits=[(sPE, kv_pe)], inc=(sV, 1))
            rec.emit("vector", lambda e: e.tensor_add(ab["kvacc"][:, 1024:2048], ab["kvacc"][:, 1024:2048], P[1][:]))
            rec.emit("vector", lambda e: e.tensor_add(ab["kvacc"][:, 2048:2064], ab["kvacc"][:, 2048:2064], P[2][:, 0:16]),
                     inc=(sV, 1))
        kv_v = sV.c

        lw = stD.batch(rec, "sync", 0, [(kv_part, ab["kvacc"][:])], waits=[(sV, kv_v)])
        rec.emit("gpsimd", lambda g: g.collective_compute(
            "AllReduce", ALU.add, replica_groups=[[0, 1], [2, 3], [4, 5], [6, 7]],
            ins=[kv_part], outs=[kv_sum]), waits=[lw], inc=(sCC, 1))
        cc = sCC.c
        lw_kv = stD.batch(rec, "gpsimd", 0, [(ab["kvsb"][:], kv_sum)], waits=[(sCC, cc)])

        obw = stD.batch(rec, "sync", 0, [(ab["obr"][:], bb_in["o"].partition_broadcast(128))],
                        waits=[(sV, sV.c)])
        QP = [P[0][:, 0:512], P[0][:, 512:1024], P[1][:, 0:512], P[1][:, 512:1024]]
        AT = [P[1][:, 0:512], P[1][:, 512:1024]]
        ZR = P[2][:, 0:4]
        RB = P[2][:, 512:1024]
        qp_guard = (sSC, 0)
        at_guard = [(sV, 0), (sV, 0)]
        o2_guard = (sV, 0)
        for tch in range(4):
            qp_guard = (sV, sV.c)
            for dch4 in range(4):
                pe_base = sPE.c
                for c in range(16):
                    pc = c % 2
                    if c < 2:
                        w = [(sPE, pe_base)]
                        if tch == 0 and dch4 == 0:
                            w.append((sSC, xn3T_sc))
                    else:
                        w = [(sPE, pe_base + (c - 1))]
                    lw = stI.batch(rec, "scalar", pc,
                                   [(ab["qwc"][pc][:], w_bf["q"][dch4 * 512:(dch4 + 1) * 512, c * 128:(c + 1) * 128])],
                                   waits=w, transpose=True)
                    rec.emit("tensor", waits=[lw] + ([qp_guard] if c == 0 else []))
                    for di in range(4):
                        rec.emit("tensor", lambda e, pc=pc, c=c, di=di, tch=tch: e.matmul(
                            QP[di], ab["qwc"][pc][:, di * 128:(di + 1) * 128],
                            xn3T[:, c * H + tch * 512: c * H + (tch + 1) * 512],
                            start=(c == 0), stop=(c == 15)),
                            inc=(sPE, 1) if di == 3 else None)
                pe_q = sPE.c
                for di in range(4):
                    dch = dch4 * 4 + di
                    rec.emit("vector", lambda e, di=di, dch=dch: e.tensor_scalar(
                        ab["el0"][:], QP[di], ab["qbc"][:, dch:dch + 1], None, ALU.add),
                        waits=[(sPE, pe_q), (sGP, qbc_gp), (sV, sV.c)], inc=(sV, 1))
                    rec.emit("vector", lambda e: e.tensor_single_scalar(ab["msk8"][:], ab["el0"][:], 0.0, ALU.is_gt))
                    rec.emit("scalar", lambda e: e.activation(ab["el1"][:], ab["el0"][:], AF.Exp),
                             waits=[(sV, sV.c)], inc=(sSC, 1))
                    rec.emit("vector", lambda e: e.tensor_scalar_add(ab["el0"][:], ab["el0"][:], 1.0),
                             waits=[(sSC, sSC.c)])
                    rec.emit("vector", lambda e, dch=dch: e.select(ab["qpT"][:, dch * 512:(dch + 1) * 512],
                                                                   ab["msk8"][:], ab["el0"][:], ab["el1"][:]),
                             inc=(sV, 1))
                qp_guard = (sV, sV.c)
            qp_v = sV.c
            qp_guard = (sV, qp_v)
            for h in range(16):
                hp = h % 2
                rec.emit("tensor", waits=[(sV, sV.c), lw_kv])
                rec.emit("tensor", lambda e, h=h, hp=hp: e.matmul(
                    AT[hp], ab["kvsb"][:, h * 128:(h + 1) * 128],
                    ab["qpT"][:, h * 512:(h + 1) * 512],
                    start=True, stop=True))
                rec.emit("tensor", lambda e, h=h: e.matmul(
                    P[2][0:1, 0:512],
                    ab["kvsb"][:, 2048 + h:2048 + h + 1],
                    ab["qpT"][:, h * 512:(h + 1) * 512],
                    start=True, stop=True), inc=(sPE, 1))
                pe_at = sPE.c
                rec.emit("vector", lambda e: e.tensor_scalar_add(ab["rzb"][0:1, 0:512], P[2][0:1, 0:512], EPS_B),
                         waits=[(sPE, pe_at), o2_guard], inc=(sV, 1))
                rec.emit("vector", lambda e: e.reciprocal(ab["rzb"][0:1, 0:512], ab["rzb"][0:1, 0:512]),
                         waits=[(sV, sV.c)], inc=(sV, 1))
                rz_v = sV.c
                rec.emit("tensor", lambda e: e.matmul(
                    RB, onesf[0:1, :], ab["rzb"][0:1, 0:512], start=True, stop=True),
                    waits=[(sV, rz_v)], inc=(sPE, 1))
                pe_rb = sPE.c
                rec.emit("scalar", lambda e: e.copy(ab["el1"][:], RB),
                         waits=[(sPE, pe_rb), (sSC, sSC.c)], inc=(sSC, 1))
                rec.emit("vector", lambda e, h=h, hp=hp: e.tensor_tensor(
                    ab["qpT"][:, h * 512:(h + 1) * 512], AT[hp], ab["el1"][:], ALU.mult),
                    waits=[(sSC, sSC.c)], inc=(sV, 1))
                at_guard[hp] = (sV, sV.c)
            atn_v = sV.c
            for ttp in range(2):
                pe_base = sPE.c
                for eo in range(4):
                    for c in range(16):
                        ci = eo * 16 + c
                        pc = ci % 2
                        w = [(sPE, pe_base)] if ci < 2 else [(sPE, pe_base + (ci - 1))]
                        lw = stI.batch(rec, "scalar", pc,
                                       [(ab["owc"][pc][:], w_bf["o"][eo * 512:(eo + 1) * 512, c * 128:(c + 1) * 128])],
                                       waits=w, transpose=True)
                        w2 = [lw, (sV, atn_v)]
                        if eo == 0 and c == 0:
                            w2.append(o2_guard)
                        rec.emit("tensor", waits=w2)
                        for tt in range(2):
                            rec.emit("tensor", lambda e, pc=pc, c=c, eo=eo, tt=tt, ttp=ttp: e.matmul(
                                pst_o[ttp][tt][eo],
                                ab["qpT"][:, c * 512 + ttp * 256 + tt * 128: c * 512 + ttp * 256 + (tt + 1) * 128],
                                ab["owc"][pc][:],
                                start=(c == 0), stop=(c == 15)),
                                inc=(sPE, 1) if tt == 1 else None)
                pe_o = sPE.c
                for tt in range(2):
                    gt = tch * 4 + ttp * 2 + tt
                    ip = gt % 2
                    lx = stX.batch(rec, "sync", ip, [(ab["x"][:], src[gt])],
                                   waits=[(sV, sV.c)])
                    ow = stO.sems[ip]
                    for eo in range(4):
                        rec.emit("vector", lambda e, tt=tt, ttp=ttp, eo=eo: e.tensor_tensor(
                            ab["o1"][:, eo * 512:(eo + 1) * 512], pst_o[ttp][tt][eo],
                            ab["obr"][:, eo * 512:(eo + 1) * 512], ALU.add),
                            waits=([(sPE, pe_o), obw,
                                    (stO.sems[0], stO.sems[0].c), (stO.sems[1], stO.sems[1].c)]
                                   if eo == 0 else []),
                            inc=(sV, 1) if eo == 3 else None)
                    rec.emit("vector", lambda e: e.tensor_add(ab["o1"][:], ab["o1"][:], ab["x"][:]),
                             waits=[lx], inc=(sV, 1))
                    o2_guard = (sV, sV.c)
                    stO.batch(rec, "sync", ip, [(dst[gt], ab["o1"][:])], waits=[(sV, sV.c)])

    def emit_ffn(src, dst, deq_cc, dbg=None):
        for i in range(NT):
            w = [(sV, sV.c), (sSC, sSC.c)] + (ostores() if i == 0 else [])
            lw = stX.batch(rec, "sync", i % 2, [(ff["x"][:], src[i])], waits=w)
            rec.emit("vector", waits=[lw])
            emit_ln(ff["x"], ff["xn"])
            vr = sV.c
            if dbg is not None and i == 0:
                rec.dma("sync", dbg["xn"], ff["xn"][:], waits=[(sV, vr)], inc=(stO.sems[0], 16))
                rec.dma("sync", dbg["xc"], lnb["xc"][:], inc=(stO.sems[0], 16))
                rec.dma("sync", dbg["mu"], lnb["mu"][:], inc=(stO.sems[0], 16))
                rec.dma("sync", dbg["rs"], lnb["rs"][:], inc=(stO.sems[0], 16))
                rec.dma("sync", dbg["xin"], ff["x"][:], inc=(stO.sems[0], 16))
                rec.emit("vector", waits=[(stO.sems[0], stO.sems[0].c)])
            for c in range(16):
                tr = TRA if c % 2 == 0 else TRB
                rec.emit("tensor", lambda e, c=c, tr=tr: e.transpose(
                    tr, ff["xn"][:, c * 128:(c + 1) * 128], ident[:]),
                    waits=[(sV, vr), (sGP, 1), (sSC, sSC.c)], inc=(sPE, 1))
                pe = sPE.c
                rec.emit("scalar", lambda e, c=c, i=i, tr=tr: e.copy(
                    ff["xnT"][:, c * H + i * 128: c * H + (i + 1) * 128], tr),
                    waits=[(sPE, pe)], inc=(sSC, 1))
        xnT_sc = sSC.c

        pe_base = sPE.c
        v_evac = {}
        st_strip = {}
        for j in range(128):
            p = j % 2
            w = [(sCC, deq_cc), (sPE, pe_base)] if j < 2 else [(sPE, pe_base + 4 * j - 4)]
            lw = stW.batch(rec, "scalar", p,
                           [(ff["wg"][p][:], wg_all[j * 128:(j + 1) * 128, :]),
                            (ff["wu"][p][:], wu_all[j * 128:(j + 1) * 128, :])],
                           waits=w)
            for th in range(2):
                u = 2 * j + th
                q = u % 2
                w = [lw, (sSC, xnT_sc)]
                if u >= 2:
                    w.append((sV, v_evac[u - 2]))
                rec.emit("tensor", waits=w)
                for tc in range(2):
                    for c in range(16):
                        last = tc == 1 and c == 15
                        rec.emit("tensor", lambda e, p=p, q=q, th=th, tc=tc, c=c: e.matmul(
                            P[q][:, tc * 512:(tc + 1) * 512],
                            ff["wg"][p][:, c * 128:(c + 1) * 128],
                            ff["xnT"][:, c * H + th * 1024 + tc * 512: c * H + th * 1024 + (tc + 1) * 512],
                            start=(c == 0), stop=(c == 15)),
                            inc=(sPE, 1) if last else None)
                pe_gate = sPE.c
                for tc in range(2):
                    for c in range(16):
                        last = tc == 1 and c == 15
                        rec.emit("tensor", lambda e, p=p, q=q, th=th, tc=tc, c=c: e.matmul(
                            P[2 + q][:, tc * 512:(tc + 1) * 512],
                            ff["wu"][p][:, c * 128:(c + 1) * 128],
                            ff["xnT"][:, c * H + th * 1024 + tc * 512: c * H + th * 1024 + (tc + 1) * 512],
                            start=(c == 0), stop=(c == 15)),
                            inc=(sPE, 1) if last else None)
                pe_up = sPE.c
                rec.emit("scalar", lambda e, q=q: e.activation(ff["gl"][:], P[q][:], AF.Gelu_apprx_tanh),
                         waits=[(sPE, pe_gate)], inc=(sSC, 1))
                scd = sSC.c
                w = [(sSC, scd), (sPE, pe_up)]
                if th == 0 and j >= 2:
                    w.append(st_strip[j - 2])
                rec.emit("vector", lambda e, p=p, q=q, th=th: e.tensor_tensor(
                    ff["int"][p][:, th * 1024:(th + 1) * 1024], ff["gl"][:], P[2 + q][:], ALU.mult),
                    waits=w, inc=(sV, 1))
                v_evac[u] = sV.c
            st_strip[j] = stS.batch(rec, "sync", p,
                                    [(inter_sc[j * 128:(j + 1) * 128, :], ff["int"][p][:])],
                                    waits=[(sV, v_evac[2 * j + 1])])
        strips_all = (stS.sems[1], stS.sems[1].c)
        strips_all0 = (stS.sems[0], stS.sems[0].c)

        pst = [P[t // 2][:, (t % 2) * 512:(t % 2 + 1) * 512] for t in range(8)]
        for tg in range(2):
            for hc in range(4):
                pe_base2 = sPE.c
                v_prev = sV.c
                for g in range(32):
                    p = g % 2
                    if g < 2:
                        w = [strips_all, strips_all0, (sCC, deq_cc)] if (tg == 0 and hc == 0) \
                            else [(sPE, pe_base2)]
                    else:
                        w = [(sPE, pe_base2 + 4 * (g - 1))]
                    pairs = [(ff["ic4"][p][:, jj * 1024:(jj + 1) * 1024],
                              inter_sc[(g * 4 + jj) * 128:(g * 4 + jj + 1) * 128, tg * 1024:(tg + 1) * 1024])
                             for jj in range(4)]
                    pairs += [(ff["wd4"][p][:, jj * 512:(jj + 1) * 512],
                               wd_all[(g * 4 + jj) * 128:(g * 4 + jj + 1) * 128, hc * 512:(hc + 1) * 512])
                              for jj in range(4)]
                    lw = stI.batch(rec, "scalar", p, pairs, waits=w)
                    w = [lw]
                    if g == 0:
                        w.append((sV, v_prev))
                    rec.emit("tensor", waits=w)
                    for jj in range(4):
                        j = g * 4 + jj
                        for t in range(8):
                            rec.emit("tensor", lambda e, p=p, t=t, j=j, jj=jj: e.matmul(
                                pst[t], ff["ic4"][p][:, jj * 1024 + t * 128: jj * 1024 + (t + 1) * 128],
                                ff["wd4"][p][:, jj * 512:(jj + 1) * 512],
                                start=(j == 0), stop=(j == 127)),
                                inc=(sPE, 1) if t == 7 else None)
                pe_done = sPE.c
                for t in range(8):
                    p = t % 2
                    lw = stX.batch(rec, "sync", p,
                                   [(ff["xp"][p][:], src[tg * 8 + t][:, hc * 512:(hc + 1) * 512])],
                                   waits=[(sV, sV.c)])
                    ow = stO.sems[p]
                    rec.emit("vector", lambda e, p=p, t=t: e.tensor_tensor(
                        ff["op"][p][:], pst[t], ff["xp"][p][:], ALU.add),
                        waits=[(sPE, pe_done), lw, (ow, ow.c)], inc=(sV, 1))
                    stO.batch(rec, "sync", p,
                              [(dst[tg * 8 + t][:, hc * 512:(hc + 1) * 512], ff["op"][p][:])],
                              waits=[(sV, sV.c)])
        return

    emit_identity()

    if mode == "full":
        weave = (deq_tile_emitters("g", gpk, gsc, wg_sl)
                 + deq_tile_emitters("u", upk, usc, wu_sl)
                 + deq_tile_emitters("d", dpk, dsc, wd_sl))
        emit_attna(x_my, xb1, weave=weave)
        emit_deq_gather(wg_sl, wg_all)
        emit_deq_gather(wu_sl, wu_all)
        cc = emit_deq_gather(wd_sl, wd_all)
        emit_ffn(xb1, xb2, cc)
        emit_attnb(xb2, xb3)
        emit_ffn(xb3, out_my, cc)
        if False:
            fd = {
                "x1": ap(nc.dram_tensor("fdbg_x1", [NT, 128, H], F32, kind="ExternalOutput")),
                "x2": ap(nc.dram_tensor("fdbg_x2", [NT, 128, H], F32, kind="ExternalOutput")),
                "x3": ap(nc.dram_tensor("fdbg_x3", [NT, 128, H], F32, kind="ExternalOutput")),
            }
            w = [(sm, sm.c) for sm in stO.sems] + [(sV, sV.c), (sSC, sSC.c), (sPE, sPE.c)]
            rec.dma("sync", fd["x1"], xb1, waits=w, inc=(stO.sems[0], 16))
            rec.dma("sync", fd["x2"], xb2, inc=(stO.sems[0], 16))
            rec.dma("sync", fd["x3"], xb3, inc=(stO.sems[0], 16))
        for sm in stO.sems:
            rec.emit("sync", waits=[(sm, sm.c)])
    elif mode in ("ffn", "ffn_dbg"):
        emit_dequant("g", gpk, gsc, wg_sl)
        emit_dequant("u", upk, usc, wu_sl)
        emit_dequant("d", dpk, dsc, wd_sl)
        emit_deq_gather(wg_sl, wg_all)
        emit_deq_gather(wu_sl, wu_all)
        cc = emit_deq_gather(wd_sl, wd_all)
        dbg = None
        if mode == "ffn_dbg":
            dbg = {
                "xn": ap(nc.dram_tensor("dbg_xn", [128, H], BF16, kind="ExternalOutput")),
                "xc": ap(nc.dram_tensor("dbg_xc", [128, H], F32, kind="ExternalOutput")),
                "mu": ap(nc.dram_tensor("dbg_mu", [128, 1], F32, kind="ExternalOutput")),
                "rs": ap(nc.dram_tensor("dbg_rs", [128, 1], F32, kind="ExternalOutput")),
                "xin": ap(nc.dram_tensor("dbg_xin", [128, H], F32, kind="ExternalOutput")),
            }
        emit_ffn(x_my, out_my, cc, dbg=dbg)
        if mode == "ffn_dbg":
            dbg_wg = ap(nc.dram_tensor("dbg_wg", [NCORES * 16 * 128, 2048], BF16, kind="ExternalOutput"))
            dbg_wd = ap(nc.dram_tensor("dbg_wd", [NCORES * 16 * 128, 2048], BF16, kind="ExternalOutput"))
            dbg_int = ap(nc.dram_tensor("dbg_int", [128 * 128, 2048], BF16, kind="ExternalOutput"))
            dbg_xnT = ap(nc.dram_tensor("dbg_xnT", [128, 16 * H], BF16, kind="ExternalOutput"))
            w = [(sm, sm.c) for sm in stO.sems] + [(sV, sV.c), (sSC, sSC.c), (sPE, sPE.c)]
            rec.dma("sync", dbg_wg, wg_all, waits=w, inc=(stO.sems[0], 16))
            rec.dma("sync", dbg_wd, wd_all, inc=(stO.sems[0], 16))
            rec.dma("sync", dbg_int, inter_sc, inc=(stO.sems[0], 16))
            rec.dma("sync", dbg_xnT, ff["xnT"][:], inc=(stO.sems[0], 16))
        for sm in stO.sems:
            rec.emit("sync", waits=[(sm, sm.c)])
    elif mode in ("attnb", "attnb_dbg"):
        emit_attnb(x_my, out_my)
        if mode == "attnb_dbg":
            bd = {
                "kp": ap(nc.dram_tensor("bdbg_kp", [16 * 128, 2048], BF16, kind="ExternalOutput")),
                "vp": ap(nc.dram_tensor("bdbg_vp", [16 * 128, 2048], BF16, kind="ExternalOutput")),
                "kvp": ap(nc.dram_tensor("bdbg_kvp", [128, 2064], F32, kind="ExternalOutput")),
                "kvs": ap(nc.dram_tensor("bdbg_kvs", [128, 2064], F32, kind="ExternalOutput")),
            }
            w = [(sm, sm.c) for sm in stO.sems] + [(sV, sV.c), (sSC, sSC.c), (sPE, sPE.c)]
            rec.dma("sync", bd["kp"], kp_sc, waits=w, inc=(stO.sems[0], 16))
            rec.dma("sync", bd["vp"], vp_sc, inc=(stO.sems[0], 16))
            rec.dma("sync", bd["kvp"], kv_part, inc=(stO.sems[0], 16))
            rec.dma("sync", bd["kvs"], kv_sum, inc=(stO.sems[0], 16))
        for sm in stO.sems:
            rec.emit("sync", waits=[(sm, sm.c)])
    elif mode in ("attna", "attna_dbg"):
        adbg = None
        if mode == "attna_dbg":
            adbg = {
                "qT": ap(nc.dram_tensor("adbg_qT", [16 * 128, 2048], BF16, kind="ExternalOutput")),
                "kTa": ap(nc.dram_tensor("adbg_kTa", [2 * 16 * 128, 2048], BF16, kind="ExternalOutput")),
                "va": ap(nc.dram_tensor("adbg_va", [2 * 16 * 128, 2048], BF16, kind="ExternalOutput")),
                "srow": ap(nc.dram_tensor("adbg_srow", [128, 4096], F32, kind="ExternalOutput")),
                "prow": ap(nc.dram_tensor("adbg_prow", [128, 4096], BF16, kind="ExternalOutput")),
                "z": ap(nc.dram_tensor("adbg_z", [128, 1], F32, kind="ExternalOutput")),
            }
        emit_attna(x_my, out_my, adbg=adbg)
        for sm in stO.sems:
            rec.emit("sync", waits=[(sm, sm.c)])
    else:
        raise NotImplementedError(mode)

    rec.replay()
    return nc


def shard_inputs(inputs):
    x = np.ascontiguousarray(np.asarray(inputs["L0_attn_a_x"], dtype=np.float32))
    qkv_w = np.ascontiguousarray(np.asarray(inputs["L0_attn_a_qkv"], dtype=np.float32))
    oproj = np.ascontiguousarray(np.asarray(inputs["L0_attn_a_o_proj"], dtype=np.float32))
    gamma = np.ascontiguousarray(np.asarray(inputs["L0_attn_a_gamma"], dtype=np.float32).reshape(1, H))
    beta = np.ascontiguousarray(np.asarray(inputs["L0_attn_a_beta"], dtype=np.float32).reshape(1, H))
    gpk = np.ascontiguousarray(np.asarray(inputs["L0_ffn_gate_W_packed"], dtype=np.int32))
    gsc = np.ascontiguousarray(np.asarray(inputs["L0_ffn_gate_scales"], dtype=np.float32))
    upk = np.ascontiguousarray(np.asarray(inputs["L0_ffn_up_W_packed"], dtype=np.int32))
    usc = np.ascontiguousarray(np.asarray(inputs["L0_ffn_up_scales"], dtype=np.float32))
    dpk = np.ascontiguousarray(np.asarray(inputs["L0_ffn_down_W_packed"], dtype=np.int32))
    dsc = np.ascontiguousarray(np.asarray(inputs["L0_ffn_down_scales"], dtype=np.float32))
    wb = {k: np.ascontiguousarray(np.asarray(inputs[f"L0_attn_b_{k}_w"], dtype=np.float32))
          for k in ("q", "k", "v", "o")}
    bb = {k: np.ascontiguousarray(np.asarray(inputs[f"L0_attn_b_{k}_b"], dtype=np.float32).reshape(1, H))
          for k in ("q", "k", "v", "o")}

    in_maps = []
    for c in range(NCORES):
        b, par = c // 2, c % 2
        xt = np.ascontiguousarray(x[b].reshape(32, 128, H)[par::2])
        mk = np.zeros((NT, 128, 256), dtype=np.float32)
        for i in range(NT):
            g = 2 * i + par
            r0 = 128 * g
            np_pan = g // 2 + 1
            c0 = 256 * (np_pan - 1)
            rr = r0 + np.arange(128)[:, None]
            cc = c0 + np.arange(256)[None, :]
            mk[i] = np.where(cc <= rr, 0.0, NEG).astype(np.float32)
        nsl = slice(c * 2048, (c + 1) * 2048)
        m = {
            "x_my": xt, "masks": mk,
            "qkv": qkv_w, "oproj": oproj, "gamma": gamma, "beta": beta,
            "bqw": wb["q"], "bkw": wb["k"], "bvw": wb["v"], "bow": wb["o"],
            "bqb": bb["q"], "bkb": bb["k"], "bvb": bb["v"], "bob": bb["o"],
            "gpk": np.ascontiguousarray(gpk[nsl]), "gsc": np.ascontiguousarray(gsc[nsl]),
            "upk": np.ascontiguousarray(upk[nsl]), "usc": np.ascontiguousarray(usc[nsl]),
            "dpk": np.ascontiguousarray(dpk[:, c * 1024:(c + 1) * 1024]),
            "dsc": np.ascontiguousarray(dsc[:, c * 64:(c + 1) * 64]),
        }
        in_maps.append(m)
    return in_maps


def gather_output(results):
    out = np.zeros((B, S, H), dtype=np.float32)
    for c in range(NCORES):
        b, par = c // 2, c % 2
        out[b].reshape(32, 128, H)[par::2] = np.asarray(results[c]["out_my"], dtype=np.float32)
    return out


_NC_CACHE = {}


def kernel(**inputs):
    mode = "full"
    if mode not in _NC_CACHE:
        _NC_CACHE[mode] = build_program(mode)
    nc = _NC_CACHE[mode]
    in_maps = shard_inputs(inputs)
    res = run_bass_kernel_spmd(nc, in_maps, core_ids=list(range(NCORES)))
    return gather_output(res.results)


# version 03
# speedup vs baseline: 1.0006x; 1.0006x over previous
import numpy as np
import concourse.bass as bass
import concourse.mybir as mybir
from concourse.bass_utils import run_bass_kernel_spmd

F32 = mybir.dt.float32
BF16 = mybir.dt.bfloat16
I32 = mybir.dt.int32
U8 = mybir.dt.uint8
AF = mybir.ActivationFunctionType
ALU = mybir.AluOpType
AX = mybir.AxisListType

H = 2048
S = 4096
B = 4
NT = 16
NCORES = 8
EPS_LN = 1e-5
EPS_B = 1e-6
NEG = -1e30

ENGINES = ("sync", "tensor", "vector", "scalar", "gpsimd")


class Sem:
    def __init__(self, nc, name):
        self.h = nc.alloc_semaphore(name=name)
        self.c = 0
        self.name = name

    def inc(self, k):
        self.c += k
        return self.c


class Rec:
    def __init__(self, nc):
        self.nc = nc
        self.ops = {e: [] for e in ENGINES}

    def emit(self, eng, builder=None, waits=(), inc=None):
        waits = [(s.h, int(v)) for (s, v) in waits if v > 0]
        if inc is not None:
            incs = inc if isinstance(inc, list) else [inc]
            for s, k in incs:
                s.inc(k)
            incs = [(s.h, int(k)) for (s, k) in incs]
        else:
            incs = []

        def fn(e, waits=waits, incs=incs, builder=builder):
            for h, v in waits:
                e.wait_ge(h, v)
            if builder is not None:
                ins = builder(e)
                for h, k in incs:
                    ins.then_inc(h, k)

        self.ops[eng].append(fn)

    def dma(self, eng, out, in_, waits=(), inc=None, **kw):
        self.emit(eng, lambda e: e.dma_start(out=out, in_=in_, **kw),
                  waits=waits, inc=inc)

    def replay(self):
        with self.nc.Block() as block:
            for eng in ENGINES:
                fns = self.ops[eng]
                getattr(block, eng)(lambda engine, fns=fns: [f(engine) for f in fns])


class Stream:

    def __init__(self, nc, name, nslots=2):
        self.sems = [Sem(nc, f"{name}{i}") for i in range(nslots)]

    def batch(self, rec, eng, slot, pairs, waits=(), **kw):
        s = self.sems[slot]
        for i, (o, inp) in enumerate(pairs):
            rec.dma(eng, o, inp, waits=(waits if i == 0 else ()), inc=(s, 16), **kw)
        return (s, s.c)


def build_program(mode="full"):
    nc = bass.Bass("TRN2", target_bir_lowering=False)
    ap = lambda t: t.ap()

    x_my = ap(nc.dram_tensor("x_my", [NT, 128, H], F32, kind="ExternalInput"))
    masks = ap(nc.dram_tensor("masks", [NT, 128, 256], F32, kind="ExternalInput"))
    qkv = ap(nc.dram_tensor("qkv", [H, 3 * H], F32, kind="ExternalInput"))
    oproj = ap(nc.dram_tensor("oproj", [H, H], F32, kind="ExternalInput"))
    gamma = ap(nc.dram_tensor("gamma", [1, H], F32, kind="ExternalInput"))
    beta = ap(nc.dram_tensor("beta", [1, H], F32, kind="ExternalInput"))
    bw_in = {k: ap(nc.dram_tensor(f"b{k}w", [H, H], F32, kind="ExternalInput"))
             for k in ("q", "k", "v", "o")}
    bb_in = {k: ap(nc.dram_tensor(f"b{k}b", [1, H], F32, kind="ExternalInput"))
             for k in ("q", "k", "v", "o")}
    gpk = ap(nc.dram_tensor("gpk", [2048, 1024], I32, kind="ExternalInput"))
    gsc = ap(nc.dram_tensor("gsc", [2048, 64], F32, kind="ExternalInput"))
    upk = ap(nc.dram_tensor("upk", [2048, 1024], I32, kind="ExternalInput"))
    usc = ap(nc.dram_tensor("usc", [2048, 64], F32, kind="ExternalInput"))
    dpk = ap(nc.dram_tensor("dpk", [2048, 1024], I32, kind="ExternalInput"))
    dsc = ap(nc.dram_tensor("dsc", [2048, 64], F32, kind="ExternalInput"))

    out_my = ap(nc.dram_tensor("out_my", [NT, 128, H], F32, kind="ExternalOutput"))

    dr = lambda name, shape, dt=BF16: ap(nc.dram_tensor(name, shape, dt, kind="Internal"))
    wg_sl = dr("wg_sl", [16 * 128, 2048])
    wu_sl = dr("wu_sl", [16 * 128, 2048])
    wd_sl = dr("wd_sl", [16 * 128, 2048])
    drs = lambda name, shape, dt=BF16: ap(nc.dram_tensor(name, shape, dt, kind="Internal", addr_space="Shared"))
    wg_all = drs("wg_all", [NCORES * 16 * 128, 2048])
    wu_all = drs("wu_all", [NCORES * 16 * 128, 2048])
    wd_all = drs("wd_all", [NCORES * 16 * 128, 2048])
    inter_sc = dr("inter_sc", [128 * 128, 2048])
    xb1 = ap(nc.dram_tensor("xb1", [NT, 128, H], F32, kind="Internal"))
    xb2 = ap(nc.dram_tensor("xb2", [NT, 128, H], F32, kind="Internal"))
    xb3 = ap(nc.dram_tensor("xb3", [NT, 128, H], F32, kind="Internal"))
    qT_sc = dr("qT_sc", [16 * 128, 2048])
    kT_sc = dr("kT_sc", [16 * 128, 2048])
    v_sc = dr("v_sc", [16 * 128, 2048])
    kT_all = dr("kT_all", [2 * 16 * 128, 2048])
    v_all = dr("v_all", [2 * 16 * 128, 2048])
    w_bf = {k: dr(f"{k}w_bf", [H, H]) for k in ("q", "k", "v", "o")}
    kp_sc = dr("kp_sc", [16 * 128, 2048])
    vp_sc = dr("vp_sc", [16 * 128, 2048])
    kv_part = ap(nc.dram_tensor("kv_part", [128, 2064], F32, kind="Internal"))
    kv_sum = ap(nc.dram_tensor("kv_sum", [128, 2064], F32, kind="Internal"))

    rec = Rec(nc)
    sPE = Sem(nc, "sPE")
    sV = Sem(nc, "sV")
    sSC = Sem(nc, "sSC")
    sGP = Sem(nc, "sGP")
    sCC = Sem(nc, "sCC")
    stX = Stream(nc, "stX", 2)
    stW = Stream(nc, "stW", 2)
    stI = Stream(nc, "stI", 2)
    stP = Stream(nc, "stP", 2)
    stO = Stream(nc, "stO", 2)
    stD = Stream(nc, "stD", 1)
    stS = Stream(nc, "stS", 2)
    stEvB = Stream(nc, "stEvB", 2)

    def ostores():
        return [(stO.sems[0], stO.sems[0].c), (stO.sems[1], stO.sems[1].c)]

    P = [nc.alloc_psum_tensor(f"P{i}", [128, 1024], F32) for i in range(4)]
    TRA = P[3][:, 0:64].bitcast(BF16)
    TRB = P[3][:, 512:576].bitcast(BF16)

    alloc = nc.alloc_sbuf_tensor
    ident = alloc("ident", [128, 128], BF16)
    onesf = alloc("onesf", [1, 128], F32)

    A1 = alloc("A1", [128, 65536], U8)
    A2 = alloc("A2", [128, 49152], U8)
    A3 = alloc("A3", [128, 32768], U8)
    A4 = alloc("A4", [128, 16384], U8)
    NB = {F32: 4, BF16: 2, I32: 4, U8: 1}

    def V(ar, off, n, dt):
        return ar[:, off:off + n * NB[dt]].bitcast(dt)

    dq = {
        "pk": alloc("dq_pk", [128, 1024], I32),
        "sc": alloc("dq_sc", [128, 64], F32),
        "f": alloc("dq_f", [128, 1024], F32),
        "m": alloc("dq_m", [128, 1024], F32),
        "t0": alloc("dq_t0", [128, 1024], F32),
        "t1": alloc("dq_t1", [128, 1024], F32),
        "ge": alloc("dq_ge", [128, 1024], F32),
        "l4": alloc("dq_l4", [128, 1024], U8),
        "l6": alloc("dq_l6", [128, 1024], U8),
        "vals": alloc("dq_vals", [128, 2048], BF16),
        "stage": alloc("dq_stage", [128, 2048], BF16),
    }
    dq["ci"] = dq["t1"][:].bitcast(I32)
    lnb = {
        "mu": alloc("ln_mu", [128, 1], F32),
        "ss": alloc("ln_ss", [128, 1], F32),
        "rs": alloc("ln_rs", [128, 1], F32),
        "xc": alloc("ln_xc", [128, H], F32),
        "sq": V(A4, 0, H, F32),
    }

    def emit_identity():
        rec.emit("gpsimd", lambda g: g.memset(onesf[:], 1.0))
        rec.emit("gpsimd", lambda g: g.memset(ident[:], 0.0))
        rec.emit("gpsimd", lambda g: g.affine_select(
            out=ident[:], in_=ident[:], compare_op=ALU.not_equal, fill=1.0,
            base=0, pattern=[[-1, 128]], channel_multiplier=1), inc=(sGP, 1))

    def deq_tile_emitters(which, pk_d, sc_d, sl_d):
        out = []
        state = {}

        def mk(t):
            def vec_part():
                emit_deq_vec(which, pk_d, sc_d, t, state)

            def pe_part():
                emit_deq_pe(which, sl_d, t, state)
            return vec_part, pe_part
        for t in range(16):
            out.append(mk(t))
        return out

    def emit_dequant(which, pk_d, sc_d, sl_d):
        for v, p in deq_tile_emitters(which, pk_d, sc_d, sl_d):
            v()
            p()

    def emit_deq_vec(which, pk_d, sc_d, t, state):
        if True:
            lw = stP.batch(rec, "scalar", 0,
                           [(dq["pk"][:], pk_d[t * 128:(t + 1) * 128, :]),
                            (dq["sc"][:], sc_d[t * 128:(t + 1) * 128, :])],
                           waits=[(sV, sV.c), (sPE, sPE.c)])
            v = "vector"
            rec.emit(v, waits=[lw])
            for pl in (0, 1):
                if pl == 0:
                    rec.emit(v, lambda e: e.tensor_single_scalar(dq["ci"][:], dq["pk"][:], 15, ALU.bitwise_and))
                else:
                    rec.emit(v, lambda e: e.tensor_single_scalar(dq["ci"][:], dq["pk"][:], 4, ALU.logical_shift_right))
                    rec.emit(v, lambda e: e.tensor_single_scalar(dq["ci"][:], dq["ci"][:], 15, ALU.bitwise_and))
                rec.emit(v, lambda e: e.tensor_copy(dq["f"][:], dq["ci"][:]))
                rec.emit(v, lambda e: e.tensor_single_scalar(dq["ge"][:], dq["f"][:], 8.0, ALU.is_ge))
                rec.emit(v, lambda e: e.scalar_tensor_tensor(dq["m"][:], dq["ge"][:], -8.0, dq["f"][:], ALU.mult, ALU.add))
                rec.emit(v, lambda e: e.tensor_single_scalar(dq["l4"][:], dq["m"][:], 4.0, ALU.is_lt))
                rec.emit(v, lambda e: e.tensor_single_scalar(dq["l6"][:], dq["m"][:], 6.0, ALU.is_lt))
                rec.emit(v, lambda e: e.tensor_scalar(dq["t0"][:], dq["m"][:], 2.0, -8.0, ALU.mult, ALU.add))
                rec.emit(v, lambda e: e.tensor_scalar_sub(dq["t1"][:], dq["m"][:], 2.0))
                rec.emit(v, lambda e: e.select(dq["t0"][:], dq["l6"][:], dq["t1"][:], dq["t0"][:]))
                rec.emit(v, lambda e: e.tensor_scalar_mul(dq["t1"][:], dq["m"][:], 0.5))
                rec.emit(v, lambda e: e.select(dq["t0"][:], dq["l4"][:], dq["t1"][:], dq["t0"][:]))
                rec.emit(v, lambda e: e.tensor_scalar(dq["ge"][:], dq["ge"][:], -2.0, 1.0, ALU.mult, ALU.add))
                rec.emit(v, lambda e: e.tensor_mul(dq["t0"][:], dq["t0"][:], dq["ge"][:]))
                o3 = dq["vals"][:, pl::2].rearrange("p (b j) -> p b j", j=16)
                i3 = dq["t0"][:].rearrange("p (b j) -> p b j", j=16)
                s3 = dq["sc"][:, :, None].broadcast_to([128, 64, 16])
                rec.emit(v, lambda e, o3=o3, i3=i3, s3=s3: e.tensor_tensor(o3, i3, s3, ALU.mult),
                         inc=(sV, 1))
            state[t] = sV.c

    def emit_deq_pe(which, sl_d, t, state):
        if True:
            v_done = state[t]
            for c in range(16):
                tr = TRA if c % 2 == 0 else TRB
                rec.emit("tensor", lambda e, c=c, tr=tr: e.transpose(
                    tr, dq["vals"][:, c * 128:(c + 1) * 128], ident[:]),
                    waits=[(sV, v_done), (sGP, 1), (sSC, sSC.c)], inc=(sPE, 1))
                pe_now = sPE.c
                rec.emit("scalar", lambda e, c=c, tr=tr: e.copy(dq["stage"][:, c * 128:(c + 1) * 128], tr),
                         waits=[(sPE, pe_now)], inc=(sSC, 1))
            sc_done = sSC.c
            if which in ("g", "u"):
                stw = stD.batch(rec, "sync", 0, [(sl_d[t * 128:(t + 1) * 128, :], dq["stage"][:])],
                                waits=[(sSC, sc_done)])
            else:
                stw = stD.batch(rec, "sync", 0,
                                [(sl_d[c * 128:(c + 1) * 128, t * 128:(t + 1) * 128],
                                  dq["stage"][:, c * 128:(c + 1) * 128]) for c in range(16)],
                                waits=[(sSC, sc_done)])
            rec.emit("scalar", waits=[stw])

    def emit_deq_gather(sl_d, all_d):
        sD = stD.sems[0]
        rec.emit("gpsimd", lambda g, sl_d=sl_d, all_d=all_d: g.collective_compute(
            "AllGather", ALU.bypass, replica_groups=[list(range(NCORES))],
            ins=[sl_d], outs=[all_d]),
            waits=[(sD, sD.c)], inc=(sCC, 1))
        return sCC.c

    def emit_ln(x_sb, xn_out, stats_to=None, eps=EPS_LN, xc=None):
        b = lnb
        xc = xc if xc is not None else b["xc"]
        rec.emit("vector", lambda e: e.reduce_sum(b["ss"][:], x_sb[:], axis=AX.X), inc=(sV, 1))
        rec.emit("vector", lambda e: e.tensor_scalar_mul(b["mu"][:], b["ss"][:], 1.0 / H),
                 waits=[(sV, sV.c)], inc=(sV, 1))
        rec.emit("vector", lambda e, xc=xc: e.tensor_scalar(xc[:], x_sb[:], b["mu"][:, 0:1], None, ALU.subtract),
                 waits=[(sV, sV.c)], inc=(sV, 1))
        rec.emit("scalar", lambda e, xc=xc: e.activation(b["sq"][:], xc[:], AF.Square, accum_out=b["ss"][:]),
                 waits=[(sV, sV.c)], inc=(sSC, 1))
        rec.emit("vector", lambda e, eps=eps: e.tensor_scalar(b["ss"][:], b["ss"][:], 1.0 / H, float(eps), ALU.mult, ALU.add),
                 waits=[(sSC, sSC.c)], inc=(sV, 1))
        rec.emit("scalar", lambda e: e.activation(b["ss"][:], b["ss"][:], AF.Ln),
                 waits=[(sV, sV.c)], inc=(sSC, 1))
        rec.emit("vector", lambda e: e.tensor_scalar_mul(b["ss"][:], b["ss"][:], -0.5),
                 waits=[(sSC, sSC.c)], inc=(sV, 1))
        rec.emit("scalar", lambda e: e.activation(b["rs"][:], b["ss"][:], AF.Exp),
                 waits=[(sV, sV.c)], inc=(sSC, 1))
        rec.emit("vector", lambda e, xc=xc: e.tensor_scalar(xn_out[:], xc[:], b["rs"][:, 0:1], None, ALU.mult),
                 waits=[(sSC, sSC.c)], inc=(sV, 1))
        if stats_to is not None:
            rec.emit("vector", lambda e: e.tensor_scalar_add(stats_to[:, 0:1], b["rs"][:], 1.0))
            rec.emit("vector", lambda e: e.tensor_tensor(stats_to[:, 1:2], b["mu"][:], b["rs"][:], ALU.mult))
            rec.emit("vector", lambda e: e.tensor_scalar_mul(stats_to[:, 1:2], stats_to[:, 1:2], -1.0), inc=(sV, 1))

    ff = {
        "x": V(A3, 0, H, F32),
        "xn": V(A3, 8192, H, BF16),
        "xnT": V(A1, 0, 16 * H, BF16),
        "wg": [V(A2, 0, 2048, BF16), V(A2, 4096, 2048, BF16)],
        "wu": [V(A2, 8192, 2048, BF16), V(A2, 12288, 2048, BF16)],
        "gl": V(A2, 16384, 1024, F32),
        "int": [V(A2, 20480, 2048, BF16), V(A2, 24576, 2048, BF16)],
    "ic": [V(A2, 28672, 1024, BF16), V(A2, 30720, 1024, BF16)],
        "wd": [V(A2, 32768, 512, BF16), V(A2, 33792, 512, BF16)],
        "xp": [V(A2, 34816, 512, F32), V(A2, 36864, 512, F32)],
        "op": [V(A2, 38912, 512, F32), V(A2, 40960, 512, F32)],
        "ic4": [V(A2, 0, 4096, BF16), V(A2, 8192, 4096, BF16)],
        "wd4": [V(A2, 16384, 2048, BF16), V(A2, 20480, 2048, BF16)],
    }

    aa = {
        "xn1": V(A3, 0, H, F32),
        "xng": V(A3, 8192, H, BF16),
        "grep": V(A3, 12288, H, F32),
        "brep": V(A3, 20480, H, F32),
        "lnst": alloc("aa_lnst", [128, 2 * NT], F32),
        "qkvb": [V(A2, 16384, 2048, BF16), V(A2, 20480, 2048, BF16)],
        "vwb": [V(A2, 0, 16 * 512, BF16), V(A2, 16384 - 8192 + 0 * 8192 + 8192, 0, BF16)],
        "ev": [alloc("aa_ev0", [128, 512], BF16), alloc("aa_ev1", [128, 512], BF16)],
        "qsb": [V(A2, 40960, 16 * 128, BF16), V(A2, 45056, 16 * 128, BF16)],
        "ksb": [V(A2, 0, 16 * 256, BF16), V(A2, 8192, 16 * 256, BF16)],
        "vsb": [V(A2, 16384, 2 * 2048, BF16), V(A2, 24576, 2 * 2048, BF16)],
        "msk": V(A2, 32768, NT * 256, BF16),
        "srow": V(A3, 0, 4096, F32),
        "prow": V(A4, 0, 4096, BF16),
        "nmax": alloc("aa_nmax", [128, 1], F32),
        "z": alloc("aa_z", [128, 1], F32),
        "rz": alloc("aa_rz", [128, 1], F32),
        "pt": [alloc("aa_pt0", [128, 128], BF16), alloc("aa_pt1", [128, 128], BF16)],
        "pvn": V(A4, 8192, 2048, BF16),
        "pvt": V(A4, 12288, 16 * 128, BF16),
        "opb": V(A1, 0, 16 * 2048, BF16),
        "xf": V(A3, 16384, H, F32),
        "o1": V(A3, 24576, H, F32),
    }
    aa["vwb"] = [V(A2, 0, 16 * 512, BF16), V(A2, 8192, 16 * 512, BF16)]
    aa["xn2"] = aa["xn1"]

    def emit_attna(src, dst, par_dummy=None, adbg=None, weave=None):
        wv = list(weave) if weave else []
        wstate = {"vq": [], "k": 0}

        def weave_step():
            if wstate["k"] < len(wv):
                wv[wstate["k"]][0]()
                wstate["vq"].append(wv[wstate["k"]][1])
                wstate["k"] += 1
            if len(wstate["vq"]) > 1:
                wstate["vq"].pop(0)()

        def weave_flush():
            while wstate["k"] < len(wv):
                weave_step()
            while wstate["vq"]:
                wstate["vq"].pop(0)()
        xngT = ff["xnT"]
        lw_g = stD.batch(rec, "sync", 0, [(aa["grep"][:], gamma.partition_broadcast(128))])
        lw_g = stD.batch(rec, "sync", 0, [(aa["brep"][:], beta.partition_broadcast(128))])
        stMK = Stream(nc, "stMK", 1)
        lw_mk = stMK.batch(rec, "gpsimd", 0, [(aa["msk"][:].rearrange("p (i c) -> p i c", c=256), masks.transpose([1, 0, 2]))])
        stOP = Stream(nc, "stOP", 1)

        for i in range(NT):
            weave_step()
            lw = stX.batch(rec, "sync", i % 2, [(ff["x"][:], src[i])],
                           waits=[(sV, sV.c), (sSC, sSC.c)])
            rec.emit("vector", waits=[lw, lw_g])
            emit_ln(ff["x"], aa["xn1"], stats_to=aa["lnst"][:, 2 * i:2 * i + 2])
            emit_ln(aa["xn1"], aa["xn1"])
            rec.emit("vector", lambda e: e.tensor_mul(aa["xn1"][:], aa["xn1"][:], aa["grep"][:]))
            rec.emit("vector", lambda e: e.tensor_add(aa["xng"][:], aa["xn1"][:], aa["brep"][:]), inc=(sV, 1))
            vr = sV.c
            for c in range(16):
                tr = TRA if c % 2 == 0 else TRB
                rec.emit("tensor", lambda e, c=c, tr=tr: e.transpose(
                    tr, aa["xng"][:, c * 128:(c + 1) * 128], ident[:]),
                    waits=[(sV, vr), (sGP, 1), (sSC, sSC.c)], inc=(sPE, 1))
                pe = sPE.c
                rec.emit("scalar", lambda e, c=c, i=i, tr=tr: e.copy(
                    xngT[:, c * H + i * 128: c * H + (i + 1) * 128], tr),
                    waits=[(sPE, pe)], inc=(sSC, 1))
        xngT_sc = sSC.c

        qkT_sc = {0: qT_sc, 1: kT_sc}
        stEv = Stream(nc, "stEv", 2)
        ev_guard = [(sSC, 0), (sSC, 0)]
        for dch in range(32):
            weave_step()
            p = dch % 2
            src_cols = qkv[:, (dch % 16) * 128 + (dch // 16) * 2048:][:, 0:128]
            lw = stW.batch(rec, "gpsimd", p,
                           [(aa["qkvb"][p][:].rearrange("p (c d) -> p c d", d=128),
                             src_cols.rearrange("(c p) d -> p c d", p=128))],
                           waits=[(sPE, sPE.c)])
            for tch in range(4):
                bank = P[p] if tch < 2 else P[2 + p]
                reg = bank[:, (tch % 2) * 512:(tch % 2) * 512 + 512]
                w = [lw, (sSC, xngT_sc), ev_guard[p]]
                rec.emit("tensor", waits=w)
                for c in range(16):
                    rec.emit("tensor", lambda e, p=p, c=c, tch=tch, reg=reg: e.matmul(
                        reg, aa["qkvb"][p][:, c * 128:(c + 1) * 128],
                        xngT[:, c * H + tch * 512: c * H + (tch + 1) * 512],
                        start=(c == 0), stop=(c == 15)),
                        inc=(sPE, 1) if c == 15 else None)
                pe = sPE.c
                ep = tch % 2
                rec.emit("scalar", lambda e, reg=reg, ep=ep: e.copy(aa["ev"][ep][:], reg),
                         waits=[(sPE, pe), (stEv.sems[ep], stEv.sems[ep].c)], inc=(sSC, 1))
                stEv.batch(rec, "sync", ep,
                           [(qkT_sc[dch // 16][(dch % 16) * 128:(dch % 16 + 1) * 128, tch * 512:(tch + 1) * 512],
                             aa["ev"][ep][:])],
                           waits=[(sSC, sSC.c)])
            ev_guard[p] = (sSC, sSC.c)

        weave_flush()
        for half in range(2):
            for dch4 in range(4):
                p = dch4 % 2
                src_cols = qkv[:, 2 * H + dch4 * 512: 2 * H + (dch4 + 1) * 512]
                lw = stW.batch(rec, "gpsimd", p,
                               [(aa["vwb"][p][:].rearrange("p (c d) -> p c d", d=512),
                                 src_cols.rearrange("(c p) d -> p c d", p=128))],
                               waits=[(sPE, sPE.c)])
                rec.emit("tensor", waits=[lw, (sV, sV.c), (sSC, sSC.c)])
                for c in range(16):
                    for t in range(8):
                        rec.emit("tensor", lambda e, p=p, c=c, t=t, half=half: e.matmul(
                            P[t // 2][:, (t % 2) * 512:(t % 2) * 512 + 512],
                            xngT[:, c * H + half * 1024 + t * 128: c * H + half * 1024 + (t + 1) * 128],
                            aa["vwb"][p][:, c * 512:(c + 1) * 512],
                            start=(c == 0), stop=(c == 15)),
                            inc=(sPE, 1) if (c == 15 and t == 7) else None)
                pe = sPE.c
                for t in range(8):
                    ep = t % 2
                    rec.emit("scalar", lambda e, t=t, ep=ep: e.copy(
                        aa["ev"][ep][:], P[t // 2][:, (t % 2) * 512:(t % 2) * 512 + 512]),
                        waits=[(sPE, pe), (stEv.sems[ep], stEv.sems[ep].c)], inc=(sSC, 1))
                    stEv.batch(rec, "sync", ep,
                               [(v_sc[(half * 8 + t) * 128:(half * 8 + t + 1) * 128, dch4 * 512:(dch4 + 1) * 512],
                                 aa["ev"][ep][:])],
                               waits=[(sSC, sSC.c)])

        opw = stOP.batch(rec, "gpsimd", 0,
                         [(aa["opb"][:, c * 2048:(c + 1) * 2048],
                           oproj[c * 128:(c + 1) * 128, :]) for c in range(16)],
                         waits=[(sPE, sPE.c)])

        pair_groups = [[0, 1], [2, 3], [4, 5], [6, 7]]
        ev_all = [(sm, sm.c) for sm in stEv.sems]
        rec.emit("gpsimd", lambda g: g.collective_compute(
            "AllGather", ALU.bypass, replica_groups=pair_groups,
            ins=[kT_sc], outs=[kT_all]), waits=ev_all, inc=(sCC, 1))
        rec.emit("gpsimd", lambda g: g.collective_compute(
            "AllGather", ALU.bypass, replica_groups=pair_groups,
            ins=[v_sc], outs=[v_all]), inc=(sCC, 1))
        ag_cc = sCC.c
        if adbg is not None:
            rec.dma("sync", adbg["qT"], qT_sc, waits=[(sCC, ag_cc)], inc=(stO.sems[0], 16))
            rec.dma("sync", adbg["kTa"], kT_all, inc=(stO.sems[0], 16))
            rec.dma("sync", adbg["va"], v_all, inc=(stO.sems[0], 16))

        kT_r = kT_all.rearrange("(r c p) t -> r c p t", r=2, p=128)
        v_r = v_all.rearrange("(r i p) d -> r i p d", r=2, p=128)

        stQT = Stream(nc, "stQT", 2)
        stKP = Stream(nc, "stKP", 2)
        stVP = Stream(nc, "stVP", 2)
        S_reg = [P[0][:, 0:256], P[0][:, 512:768]]
        prow_s = [aa["prow"], lnb["xc"][:].bitcast(BF16)]
        nmax_s = [aa["nmax"], alloc("aa_nmax1", [128, 1], F32)]
        z_s = [aa["z"], alloc("aa_z1", [128, 1], F32)]
        rz_s = [aa["rz"], alloc("aa_rz1", [128, 1], F32)]
        st = {"s_guard": [(sSC, 0), (sSC, 0)], "o1_prev": (sV, 0),
              "q_guard": [(sPE, 0), (sPE, 0)], "k_guard": [(sPE, 0), (sPE, 0)],
              "v_guard": [(sPE, 0), (sPE, 0)], "pt_done": [(sPE, 0), (sPE, 0)],
              "exp_sc": (sSC, 0), "lq": {}, "rz_v": {}, "exp_of": {}}

        def aa_pass1(i):
            ip = i % 2
            npan = i + 1
            st["lq"][i] = stQT.batch(rec, "sync", ip,
                            [(aa["qsb"][ip][:].rearrange("p (c t) -> p c t", t=128),
                              qT_sc.rearrange("(c p) t -> p c t", p=128)[:, :, i * 128:(i + 1) * 128])],
                            waits=[st["q_guard"][ip], (sCC, ag_cc)])
            for pan in range(npan):
                pp = pan % 2
                prs = []
                for gsub in range(2):
                    gc = 2 * pan + gsub
                    rr = gc % 2
                    lt = gc // 2
                    prs.append((aa["ksb"][pp][:].rearrange("p (c t) -> p c t", t=256)[:, :, gsub * 128:(gsub + 1) * 128],
                                kT_r[rr, :, :, lt * 128:(lt + 1) * 128].transpose([1, 0, 2])))
                lk = stKP.batch(rec, "sync", pp, prs, waits=[st["k_guard"][pp], (sCC, ag_cc)])
                rec.emit("tensor", waits=[st["lq"][i], lk, st["s_guard"][pp]])
                for c in range(16):
                    rec.emit("tensor", lambda e, ip=ip, pp=pp, c=c: e.matmul(
                        S_reg[pp], aa["qsb"][ip][:, c * 128:(c + 1) * 128],
                        aa["ksb"][pp][:, c * 256:(c + 1) * 256],
                        start=(c == 0), stop=(c == 15)),
                        inc=(sPE, 1) if c == 15 else None)
                st["k_guard"][pp] = (sPE, sPE.c)
                pe = sPE.c
                w = [(sPE, pe)]
                if pan == 0:
                    w.append(st["exp_sc"])
                if pan == npan - 1:
                    rec.emit("vector", lambda e, i=i, pan=pan, pp=pp: e.scalar_tensor_tensor(
                        aa["srow"][:, pan * 256:(pan + 1) * 256], S_reg[pp], 1.0 / 45.254833995939045,
                        aa["msk"][:, i * 256:(i + 1) * 256], ALU.mult, ALU.add),
                        waits=w + [lw_mk], inc=(sV, 1))
                else:
                    rec.emit("vector", lambda e, pan=pan, pp=pp: e.tensor_scalar_mul(
                        aa["srow"][:, pan * 256:(pan + 1) * 256], S_reg[pp], 1.0 / 45.254833995939045),
                        waits=w, inc=(sV, 1))
                st["s_guard"][pp] = (sV, sV.c)

        def aa_softmax(i):
            ip = i % 2
            npan = i + 1
            rec.emit("vector", lambda e, i=i, npan=npan, ip=ip: e.reduce_max(
                nmax_s[ip][:], aa["srow"][:, 0:npan * 256], axis=AX.X, negate=True),
                waits=[(sV, sV.c)], inc=(sV, 1))
            rec.emit("scalar", lambda e, npan=npan, ip=ip: e.activation(
                prow_s[ip][:, 0:npan * 256], aa["srow"][:, 0:npan * 256], AF.Exp,
                bias=nmax_s[ip][:, 0:1], accum_out=z_s[ip][:]),
                waits=[(sV, sV.c), st["pt_done"][ip]], inc=(sSC, 1))
            st["exp_sc"] = (sSC, sSC.c)
            st["exp_of"][i] = (sSC, sSC.c)
            rec.emit("vector", lambda e, ip=ip: e.reciprocal(rz_s[ip][:], z_s[ip][:]),
                     waits=[(sSC, sSC.c)], inc=(sV, 1))
            st["rz_v"][i] = sV.c
            if adbg is not None and i == 3:
                rec.dma("sync", adbg["srow"], aa["srow"][:], waits=[(sV, sV.c)], inc=(stO.sems[0], 16))
                rec.dma("sync", adbg["prow"], prow_s[ip][:], inc=(stO.sems[0], 16))
                rec.dma("sync", adbg["z"], z_s[ip][:], inc=(stO.sems[0], 16))
                rec.emit("scalar", waits=[(stO.sems[0], stO.sems[0].c)])
                rec.emit("vector", waits=[(stO.sems[0], stO.sems[0].c)])

        def aa_pass2(i):
            ip = i % 2
            npan = i + 1
            for pan in range(npan):
                pp = pan % 2
                prs = []
                for gsub in range(2):
                    gc = 2 * pan + gsub
                    rr = gc % 2
                    lt = gc // 2
                    prs.append((aa["vsb"][pp][:, gsub * 2048:(gsub + 1) * 2048], v_r[rr, lt]))
                lv = stVP.batch(rec, "sync", pp, prs, waits=[st["v_guard"][pp], (sCC, ag_cc)])
                for gsub in range(2):
                    b = 2 * pan + gsub
                    tp = b % 2
                    tr = TRA if tp == 0 else TRB
                    rec.emit("tensor", lambda e, b=b, tr=tr, ip=ip: e.transpose(
                        tr, prow_s[ip][:, b * 128:(b + 1) * 128], ident[:]),
                        waits=[st["exp_of"][i], (sSC, sSC.c), (sGP, 1)], inc=(sPE, 1))
                    pe = sPE.c
                    rec.emit("scalar", lambda e, tp=tp, tr=tr: e.copy(aa["pt"][tp][:], tr),
                             waits=[(sPE, pe)], inc=(sSC, 1))
                    ptsc = sSC.c
                    w = [lv, (sSC, ptsc)]
                    if b == 0:
                        w.append(st["o1_prev"])
                    rec.emit("tensor", waits=w)
                    for d4 in range(4):
                        rec.emit("tensor", lambda e, tp=tp, gsub=gsub, d4=d4, b=b, npan=npan: e.matmul(
                            P[1 + d4 // 2][:, (d4 % 2) * 512:(d4 % 2) * 512 + 512],
                            aa["pt"][tp][:],
                            aa["vsb"][(b // 2) % 2][:, gsub * 2048 + d4 * 512: gsub * 2048 + (d4 + 1) * 512],
                            start=(b == 0), stop=(b == 2 * npan - 1)),
                            inc=(sPE, 1) if d4 == 3 else None)
                    st["v_guard"][pp] = (sPE, sPE.c)
            st["pt_done"][ip] = (sPE, sPE.c)
            pv_pe = sPE.c
            rec.emit("vector", lambda e, ip=ip: e.tensor_scalar(
                aa["pvn"][:, 0:1024], P[1][:], rz_s[ip][:, 0:1], None, ALU.mult),
                waits=[(sPE, pv_pe), (sV, st["rz_v"][i])], inc=(sV, 1))
            rec.emit("vector", lambda e, ip=ip: e.tensor_scalar(
                aa["pvn"][:, 1024:2048], P[2][:], rz_s[ip][:, 0:1], None, ALU.mult),
                inc=(sV, 1))
            pvn_v = sV.c
            for c in range(16):
                tr = TRA if c % 2 == 0 else TRB
                rec.emit("tensor", lambda e, c=c, tr=tr: e.transpose(
                    tr, aa["pvn"][:, c * 128:(c + 1) * 128], ident[:]),
                    waits=[(sV, pvn_v), (sSC, sSC.c)], inc=(sPE, 1))
                pe = sPE.c
                rec.emit("scalar", lambda e, c=c, tr=tr: e.copy(
                    aa["pvt"][:, c * 128:(c + 1) * 128], tr),
                    waits=[(sPE, pe)], inc=(sSC, 1))
            pvt_sc = sSC.c
            lx = stX.batch(rec, "sync", ip, [(aa["xf"][:], src[i])], waits=[(sV, sV.c)])
            rec.emit("tensor", waits=[(sSC, pvt_sc), opw])
            for oc in range(4):
                for c in range(16):
                    rec.emit("tensor", lambda e, oc=oc, c=c: e.matmul(
                        P[1 + oc // 2][:, (oc % 2) * 512:(oc % 2) * 512 + 512],
                        aa["pvt"][:, c * 128:(c + 1) * 128],
                        aa["opb"][:, c * 2048 + oc * 512: c * 2048 + (oc + 1) * 512],
                        start=(c == 0), stop=(c == 15)),
                        inc=(sPE, 1) if c == 15 else None)
            o_pe = sPE.c
            rec.emit("vector", lambda e, i=i: e.tensor_scalar(
                aa["o1"][:], aa["xf"][:], aa["lnst"][:, 2 * i:2 * i + 1], aa["lnst"][:, 2 * i + 1:2 * i + 2],
                ALU.mult, ALU.add),
                waits=[lx, (sV, sV.c), (stO.sems[0], stO.sems[0].c), (stO.sems[1], stO.sems[1].c)],
                inc=(sV, 1))
            rec.emit("vector", lambda e: e.tensor_add(aa["o1"][:, 0:1024], aa["o1"][:, 0:1024], P[1][:]),
                     waits=[(sPE, o_pe)], inc=(sV, 1))
            rec.emit("vector", lambda e: e.tensor_add(aa["o1"][:, 1024:2048], aa["o1"][:, 1024:2048], P[2][:]),
                     inc=(sV, 1))
            st["o1_prev"] = (sV, sV.c)
            st["q_guard"][ip] = (sPE, o_pe)
            ow = stO.sems[ip]
            rec.emit("vector", waits=[(ow, ow.c)])
            stO.batch(rec, "sync", ip, [(dst[i], aa["o1"][:])], waits=[(sV, sV.c)])

        import os as _os
        if _os.environ.get("AA_PIPE") != "1":
            for i in range(NT):
                aa_pass1(i)
                aa_softmax(i)
                aa_pass2(i)
        else:
            for i in range(NT):
                aa_pass1(i)
                if i >= 1:
                    aa_pass2(i - 1)
                aa_softmax(i)
            aa_pass2(NT - 1)

    ab = {
        "xn3": V(A3, 8192, H, BF16),
        "kbr": V(A3, 12288, H, F32),
        "vbr": V(A3, 20480, H, F32),
        "obr": V(A3, 12288, H, F32),
        "x": V(A3, 0, H, F32),
        "o1": V(A3, 20480, H, F32),
        "wc": [V(A2, 0, 512, BF16), V(A2, 1024, 512, BF16)],
        "pb": [V(A2, 0, H, BF16), V(A2, 4096, H, BF16)],
        "kpt": V(A2, 4096, H, BF16),
        "vpt": V(A2, 8192, H, BF16),
        "kvacc": V(A2, 12288, 2064, F32),
        "kvsb": V(A2, 20736, 2064, BF16),
        "qwc": [V(A2, 25088, 512, BF16), V(A2, 26112, 512, BF16)],
        "owc": [V(A2, 27136, 512, BF16), V(A2, 28160, 512, BF16)],
        "rzb": V(A2, 29184, 512, F32),
        "qpT": V(A2, 31232, 16 * 512, BF16),
        "atn": [V(A4, 8192, 512, BF16), V(A4, 9216, 512, BF16)],
        "el0": V(A4, 10240, 512, F32),
        "el1": V(A4, 12288, 512, F32),
        "msk8": V(A4, 14336, 512, U8),
        "qbc": alloc("ab_qbc", [128, 16], F32),
        "ones": alloc("ab_ones", [128, 1], BF16),
    }

    pst_o = [[[P[(tt * 4 + eo) // 2][:, ((tt * 4 + eo) % 2) * 512:((tt * 4 + eo) % 2) * 512 + 512]
               for eo in range(4)] for tt in range(2)] for _ttp in range(2)]

    def emit_attnb(src, dst):
        stB = Stream(nc, "stB", 2)
        stKV = Stream(nc, "stKV", 2)
        xn3T = ff["xnT"]
        pb_guard = [(stB.sems[0], 0), (stB.sems[1], 0)]
        entry = [(sPE, sPE.c), (sV, sV.c)]
        for mi, mk in enumerate(("q", "k", "v", "o")):
            for st_ in range(16):
                p = st_ % 2
                w = [pb_guard[p]] + (entry if (mi == 0 and st_ < 2) else [])
                lw = stW.batch(rec, "gpsimd", p,
                               [(ab["pb"][p][:], bw_in[mk][st_ * 128:(st_ + 1) * 128, :])],
                               waits=w)
                pb_guard[p] = stB.batch(rec, "sync", p,
                                        [(w_bf[mk][st_ * 128:(st_ + 1) * 128, :], ab["pb"][p][:])],
                                        waits=[lw])
        wbf_done = [pb_guard[0], pb_guard[1]]
        lw_b = stD.batch(rec, "sync", 0, [(ab["kbr"][:], bb_in["k"].partition_broadcast(128))])
        lw_b = stD.batch(rec, "sync", 0, [(ab["vbr"][:], bb_in["v"].partition_broadcast(128))])
        rec.dma("gpsimd", ab["qbc"][:],
                bb_in["q"].rearrange("o (c p) -> o c p", p=128).transpose([0, 2, 1])[0],
                inc=(sGP, 16), allow_slow_non_contiguous=True)
        qbc_gp = sGP.c
        rec.emit("gpsimd", lambda g: g.memset(ab["ones"][:], 1.0), inc=(sGP, 1))
        ones_gp = sGP.c
        for i in range(NT):
            w = [(sV, sV.c), (sSC, sSC.c)] + (ostores() if i == 0 else [])
            lw = stX.batch(rec, "sync", i % 2, [(ab["x"][:], src[i])], waits=w)
            rec.emit("vector", waits=[lw])
            emit_ln(ab["x"], ab["xn3"])
            vr = sV.c
            for c in range(16):
                tr = TRA if c % 2 == 0 else TRB
                rec.emit("tensor", lambda e, c=c, tr=tr: e.transpose(
                    tr, ab["xn3"][:, c * 128:(c + 1) * 128], ident[:]),
                    waits=[(sV, vr), (sGP, 1), (sSC, sSC.c)], inc=(sPE, 1))
                pe = sPE.c
                rec.emit("scalar", lambda e, c=c, i=i, tr=tr: e.copy(
                    xn3T[:, c * H + i * 128: c * H + (i + 1) * 128], tr),
                    waits=[(sPE, pe)], inc=(sSC, 1))
        xn3T_sc = sSC.c

        pst = [P[t // 2][:, (t % 2) * 512:(t % 2 + 1) * 512] for t in range(8)]
        first_bd = True
        for mk, biasr, do_elu, dstp in (("k", ab["kbr"], True, kp_sc), ("v", ab["vbr"], False, vp_sc)):
            for half in range(2):
                for dch4 in range(4):
                    pe_base = sPE.c
                    for c in range(16):
                        p = c % 2
                        if c < 2:
                            w = [(sPE, pe_base)]
                            if first_bd:
                                w += [wbf_done[0], wbf_done[1], (sSC, xn3T_sc)]
                        else:
                            w = [(sPE, pe_base + (c - 1))]
                        lw = stI.batch(rec, "scalar", p,
                                       [(ab["wc"][p][:], w_bf[mk][dch4 * 512:(dch4 + 1) * 512, c * 128:(c + 1) * 128])],
                                       waits=w, transpose=True)
                        w2 = [lw]
                        if c == 0:
                            w2.append((sV, sV.c))
                        rec.emit("tensor", waits=w2)
                        for t in range(8):
                            rec.emit("tensor", lambda e, p=p, c=c, t=t, half=half: e.matmul(
                                pst[t], xn3T[:, c * H + half * 1024 + t * 128: c * H + half * 1024 + (t + 1) * 128],
                                ab["wc"][p][:],
                                start=(c == 0), stop=(c == 15)),
                                inc=(sPE, 1) if t == 7 else None)
                    first_bd = False
                    pe_done = sPE.c
                    for t in range(8):
                        ep = t % 2
                        sEv = stEvB.sems[ep]
                        rec.emit("vector", lambda e, t=t, ep=ep, biasr=biasr, dch4=dch4: e.tensor_tensor(
                            ab["el0"][:] if False else ab["el0"][:], pst[t], biasr[:, dch4 * 512:(dch4 + 1) * 512], ALU.add),
                            waits=[(sPE, pe_done), lw_b, (sEv, sEv.c), (sV, sV.c)], inc=(sV, 1))
                        if do_elu:
                            rec.emit("vector", lambda e: e.tensor_single_scalar(ab["msk8"][:], ab["el0"][:], 0.0, ALU.is_gt))
                            rec.emit("scalar", lambda e: e.activation(ab["el1"][:], ab["el0"][:], AF.Exp),
                                     waits=[(sV, sV.c)], inc=(sSC, 1))
                            rec.emit("vector", lambda e: e.tensor_scalar_add(ab["el0"][:], ab["el0"][:], 1.0),
                                     waits=[(sSC, sSC.c)])
                            rec.emit("vector", lambda e, ep=ep: e.select(aa["ev"][ep][:], ab["msk8"][:], ab["el0"][:], ab["el1"][:]),
                                     inc=(sV, 1))
                        else:
                            rec.emit("vector", lambda e, ep=ep: e.tensor_copy(aa["ev"][ep][:], ab["el0"][:]), inc=(sV, 1))
                        stEvB.batch(rec, "sync", ep,
                                    [(dstp[(half * 8 + t) * 128:(half * 8 + t + 1) * 128, dch4 * 512:(dch4 + 1) * 512],
                                      aa["ev"][ep][:])],
                                    waits=[(sV, sV.c)])
        evb_all = [(sm, sm.c) for sm in stEvB.sems]

        rec.emit("vector", lambda e: e.memset(ab["kvacc"][:], 0.0), inc=(sV, 1))
        for i in range(NT):
            p = i % 2
            lw = stKV.batch(rec, "sync", p,
                            [(ab["kpt"][:], kp_sc[i * 128:(i + 1) * 128, :])],
                            waits=(evb_all if i == 0 else [(sPE, kv_pe)]))
            lw2 = stKV.batch(rec, "sync", p, [(ab["vpt"][:], vp_sc[i * 128:(i + 1) * 128, :])])
            w = [lw, lw2, (sGP, ones_gp)]
            if i > 0:
                w.append((sV, sV.c))
            rec.emit("tensor", waits=w)
            for h in range(16):
                rec.emit("tensor", lambda e, h=h: e.matmul(
                    P[h // 8][:, (h % 8) * 128:(h % 8) * 128 + 128],
                    ab["kpt"][:, h * 128:(h + 1) * 128],
                    ab["vpt"][:, h * 128:(h + 1) * 128],
                    start=True, stop=True))
                rec.emit("tensor", lambda e, h=h: e.matmul(
                    P[2][:, h:h + 1],
                    ab["kpt"][:, h * 128:(h + 1) * 128],
                    ab["ones"][:],
                    start=True, stop=True),
                    inc=(sPE, 1) if h == 15 else None)
            kv_pe = sPE.c
            rec.emit("vector", lambda e: e.tensor_add(ab["kvacc"][:, 0:1024], ab["kvacc"][:, 0:1024], P[0][:]),
                     waits=[(sPE, kv_pe)], inc=(sV, 1))
            rec.emit("vector", lambda e: e.tensor_add(ab["kvacc"][:, 1024:2048], ab["kvacc"][:, 1024:2048], P[1][:]))
            rec.emit("vector", lambda e: e.tensor_add(ab["kvacc"][:, 2048:2064], ab["kvacc"][:, 2048:2064], P[2][:, 0:16]),
                     inc=(sV, 1))
        kv_v = sV.c

        lw = stD.batch(rec, "sync", 0, [(kv_part, ab["kvacc"][:])], waits=[(sV, kv_v)])
        rec.emit("gpsimd", lambda g: g.collective_compute(
            "AllReduce", ALU.add, replica_groups=[[0, 1], [2, 3], [4, 5], [6, 7]],
            ins=[kv_part], outs=[kv_sum]), waits=[lw], inc=(sCC, 1))
        cc = sCC.c
        lw_kv = stD.batch(rec, "gpsimd", 0, [(ab["kvsb"][:], kv_sum)], waits=[(sCC, cc)])

        obw = stD.batch(rec, "sync", 0, [(ab["obr"][:], bb_in["o"].partition_broadcast(128))],
                        waits=[(sV, sV.c)])
        QP = [P[0][:, 0:512], P[0][:, 512:1024], P[1][:, 0:512], P[1][:, 512:1024]]
        AT = [P[1][:, 0:512], P[1][:, 512:1024]]
        ZR = P[2][:, 0:4]
        RB = P[2][:, 512:1024]
        qp_guard = (sSC, 0)
        at_guard = [(sV, 0), (sV, 0)]
        o2_guard = (sV, 0)
        for tch in range(4):
            qp_guard = (sV, sV.c)
            for dch4 in range(4):
                pe_base = sPE.c
                for c in range(16):
                    pc = c % 2
                    if c < 2:
                        w = [(sPE, pe_base)]
                        if tch == 0 and dch4 == 0:
                            w.append((sSC, xn3T_sc))
                    else:
                        w = [(sPE, pe_base + (c - 1))]
                    lw = stI.batch(rec, "scalar", pc,
                                   [(ab["qwc"][pc][:], w_bf["q"][dch4 * 512:(dch4 + 1) * 512, c * 128:(c + 1) * 128])],
                                   waits=w, transpose=True)
                    rec.emit("tensor", waits=[lw] + ([qp_guard] if c == 0 else []))
                    for di in range(4):
                        rec.emit("tensor", lambda e, pc=pc, c=c, di=di, tch=tch: e.matmul(
                            QP[di], ab["qwc"][pc][:, di * 128:(di + 1) * 128],
                            xn3T[:, c * H + tch * 512: c * H + (tch + 1) * 512],
                            start=(c == 0), stop=(c == 15)),
                            inc=(sPE, 1) if di == 3 else None)
                pe_q = sPE.c
                for di in range(4):
                    dch = dch4 * 4 + di
                    rec.emit("vector", lambda e, di=di, dch=dch: e.tensor_scalar(
                        ab["el0"][:], QP[di], ab["qbc"][:, dch:dch + 1], None, ALU.add),
                        waits=[(sPE, pe_q), (sGP, qbc_gp), (sV, sV.c)], inc=(sV, 1))
                    rec.emit("vector", lambda e: e.tensor_single_scalar(ab["msk8"][:], ab["el0"][:], 0.0, ALU.is_gt))
                    rec.emit("scalar", lambda e: e.activation(ab["el1"][:], ab["el0"][:], AF.Exp),
                             waits=[(sV, sV.c)], inc=(sSC, 1))
                    rec.emit("vector", lambda e: e.tensor_scalar_add(ab["el0"][:], ab["el0"][:], 1.0),
                             waits=[(sSC, sSC.c)])
                    rec.emit("vector", lambda e, dch=dch: e.select(ab["qpT"][:, dch * 512:(dch + 1) * 512],
                                                                   ab["msk8"][:], ab["el0"][:], ab["el1"][:]),
                             inc=(sV, 1))
                qp_guard = (sV, sV.c)
            qp_v = sV.c
            qp_guard = (sV, qp_v)
            for h in range(16):
                hp = h % 2
                rec.emit("tensor", waits=[(sV, sV.c), lw_kv])
                rec.emit("tensor", lambda e, h=h, hp=hp: e.matmul(
                    AT[hp], ab["kvsb"][:, h * 128:(h + 1) * 128],
                    ab["qpT"][:, h * 512:(h + 1) * 512],
                    start=True, stop=True))
                rec.emit("tensor", lambda e, h=h: e.matmul(
                    P[2][0:1, 0:512],
                    ab["kvsb"][:, 2048 + h:2048 + h + 1],
                    ab["qpT"][:, h * 512:(h + 1) * 512],
                    start=True, stop=True), inc=(sPE, 1))
                pe_at = sPE.c
                rec.emit("vector", lambda e: e.tensor_scalar_add(ab["rzb"][0:1, 0:512], P[2][0:1, 0:512], EPS_B),
                         waits=[(sPE, pe_at), o2_guard], inc=(sV, 1))
                rec.emit("vector", lambda e: e.reciprocal(ab["rzb"][0:1, 0:512], ab["rzb"][0:1, 0:512]),
                         waits=[(sV, sV.c)], inc=(sV, 1))
                rz_v = sV.c
                rec.emit("tensor", lambda e: e.matmul(
                    RB, onesf[0:1, :], ab["rzb"][0:1, 0:512], start=True, stop=True),
                    waits=[(sV, rz_v)], inc=(sPE, 1))
                pe_rb = sPE.c
                rec.emit("scalar", lambda e: e.copy(ab["el1"][:], RB),
                         waits=[(sPE, pe_rb), (sSC, sSC.c)], inc=(sSC, 1))
                rec.emit("vector", lambda e, h=h, hp=hp: e.tensor_tensor(
                    ab["qpT"][:, h * 512:(h + 1) * 512], AT[hp], ab["el1"][:], ALU.mult),
                    waits=[(sSC, sSC.c)], inc=(sV, 1))
                at_guard[hp] = (sV, sV.c)
            atn_v = sV.c
            for ttp in range(2):
                pe_base = sPE.c
                for eo in range(4):
                    for c in range(16):
                        ci = eo * 16 + c
                        pc = ci % 2
                        w = [(sPE, pe_base)] if ci < 2 else [(sPE, pe_base + (ci - 1))]
                        lw = stI.batch(rec, "scalar", pc,
                                       [(ab["owc"][pc][:], w_bf["o"][eo * 512:(eo + 1) * 512, c * 128:(c + 1) * 128])],
                                       waits=w, transpose=True)
                        w2 = [lw, (sV, atn_v)]
                        if eo == 0 and c == 0:
                            w2.append(o2_guard)
                        rec.emit("tensor", waits=w2)
                        for tt in range(2):
                            rec.emit("tensor", lambda e, pc=pc, c=c, eo=eo, tt=tt, ttp=ttp: e.matmul(
                                pst_o[ttp][tt][eo],
                                ab["qpT"][:, c * 512 + ttp * 256 + tt * 128: c * 512 + ttp * 256 + (tt + 1) * 128],
                                ab["owc"][pc][:],
                                start=(c == 0), stop=(c == 15)),
                                inc=(sPE, 1) if tt == 1 else None)
                pe_o = sPE.c
                for tt in range(2):
                    gt = tch * 4 + ttp * 2 + tt
                    ip = gt % 2
                    lx = stX.batch(rec, "sync", ip, [(ab["x"][:], src[gt])],
                                   waits=[(sV, sV.c)])
                    ow = stO.sems[ip]
                    for eo in range(4):
                        rec.emit("vector", lambda e, tt=tt, ttp=ttp, eo=eo: e.tensor_tensor(
                            ab["o1"][:, eo * 512:(eo + 1) * 512], pst_o[ttp][tt][eo],
                            ab["obr"][:, eo * 512:(eo + 1) * 512], ALU.add),
                            waits=([(sPE, pe_o), obw,
                                    (stO.sems[0], stO.sems[0].c), (stO.sems[1], stO.sems[1].c)]
                                   if eo == 0 else []),
                            inc=(sV, 1) if eo == 3 else None)
                    rec.emit("vector", lambda e: e.tensor_add(ab["o1"][:], ab["o1"][:], ab["x"][:]),
                             waits=[lx], inc=(sV, 1))
                    o2_guard = (sV, sV.c)
                    stO.batch(rec, "sync", ip, [(dst[gt], ab["o1"][:])], waits=[(sV, sV.c)])

    def emit_ffn(src, dst, deq_cc, dbg=None):
        for i in range(NT):
            w = [(sV, sV.c), (sSC, sSC.c)] + (ostores() if i == 0 else [])
            lw = stX.batch(rec, "sync", i % 2, [(ff["x"][:], src[i])], waits=w)
            rec.emit("vector", waits=[lw])
            emit_ln(ff["x"], ff["xn"])
            vr = sV.c
            if dbg is not None and i == 0:
                rec.dma("sync", dbg["xn"], ff["xn"][:], waits=[(sV, vr)], inc=(stO.sems[0], 16))
                rec.dma("sync", dbg["xc"], lnb["xc"][:], inc=(stO.sems[0], 16))
                rec.dma("sync", dbg["mu"], lnb["mu"][:], inc=(stO.sems[0], 16))
                rec.dma("sync", dbg["rs"], lnb["rs"][:], inc=(stO.sems[0], 16))
                rec.dma("sync", dbg["xin"], ff["x"][:], inc=(stO.sems[0], 16))
                rec.emit("vector", waits=[(stO.sems[0], stO.sems[0].c)])
            for c in range(16):
                tr = TRA if c % 2 == 0 else TRB
                rec.emit("tensor", lambda e, c=c, tr=tr: e.transpose(
                    tr, ff["xn"][:, c * 128:(c + 1) * 128], ident[:]),
                    waits=[(sV, vr), (sGP, 1), (sSC, sSC.c)], inc=(sPE, 1))
                pe = sPE.c
                rec.emit("scalar", lambda e, c=c, i=i, tr=tr: e.copy(
                    ff["xnT"][:, c * H + i * 128: c * H + (i + 1) * 128], tr),
                    waits=[(sPE, pe)], inc=(sSC, 1))
        xnT_sc = sSC.c

        pe_base = sPE.c
        v_evac = {}
        st_strip = {}
        for j in range(128):
            p = j % 2
            w = [(sCC, deq_cc), (sPE, pe_base)] if j < 2 else [(sPE, pe_base + 4 * j - 4)]
            lw = stW.batch(rec, "scalar", p,
                           [(ff["wg"][p][:], wg_all[j * 128:(j + 1) * 128, :]),
                            (ff["wu"][p][:], wu_all[j * 128:(j + 1) * 128, :])],
                           waits=w)
            for th in range(2):
                u = 2 * j + th
                q = u % 2
                w = [lw, (sSC, xnT_sc)]
                if u >= 2:
                    w.append((sV, v_evac[u - 2]))
                rec.emit("tensor", waits=w)
                for tc in range(2):
                    for c in range(16):
                        last = tc == 1 and c == 15
                        rec.emit("tensor", lambda e, p=p, q=q, th=th, tc=tc, c=c: e.matmul(
                            P[q][:, tc * 512:(tc + 1) * 512],
                            ff["wg"][p][:, c * 128:(c + 1) * 128],
                            ff["xnT"][:, c * H + th * 1024 + tc * 512: c * H + th * 1024 + (tc + 1) * 512],
                            start=(c == 0), stop=(c == 15)),
                            inc=(sPE, 1) if last else None)
                pe_gate = sPE.c
                for tc in range(2):
                    for c in range(16):
                        last = tc == 1 and c == 15
                        rec.emit("tensor", lambda e, p=p, q=q, th=th, tc=tc, c=c: e.matmul(
                            P[2 + q][:, tc * 512:(tc + 1) * 512],
                            ff["wu"][p][:, c * 128:(c + 1) * 128],
                            ff["xnT"][:, c * H + th * 1024 + tc * 512: c * H + th * 1024 + (tc + 1) * 512],
                            start=(c == 0), stop=(c == 15)),
                            inc=(sPE, 1) if last else None)
                pe_up = sPE.c
                rec.emit("scalar", lambda e, q=q: e.activation(ff["gl"][:], P[q][:], AF.Gelu_apprx_tanh),
                         waits=[(sPE, pe_gate)], inc=(sSC, 1))
                scd = sSC.c
                w = [(sSC, scd), (sPE, pe_up)]
                if th == 0 and j >= 2:
                    w.append(st_strip[j - 2])
                rec.emit("vector", lambda e, p=p, q=q, th=th: e.tensor_tensor(
                    ff["int"][p][:, th * 1024:(th + 1) * 1024], ff["gl"][:], P[2 + q][:], ALU.mult),
                    waits=w, inc=(sV, 1))
                v_evac[u] = sV.c
            st_strip[j] = stS.batch(rec, "sync", p,
                                    [(inter_sc[j * 128:(j + 1) * 128, :], ff["int"][p][:])],
                                    waits=[(sV, v_evac[2 * j + 1])])
        strips_all = (stS.sems[1], stS.sems[1].c)
        strips_all0 = (stS.sems[0], stS.sems[0].c)

        pst = [P[t // 2][:, (t % 2) * 512:(t % 2 + 1) * 512] for t in range(8)]
        for tg in range(2):
            for hc in range(4):
                pe_base2 = sPE.c
                v_prev = sV.c
                for g in range(32):
                    p = g % 2
                    if g < 2:
                        w = [strips_all, strips_all0, (sCC, deq_cc)] if (tg == 0 and hc == 0) \
                            else [(sPE, pe_base2)]
                    else:
                        w = [(sPE, pe_base2 + 4 * (g - 1))]
                    pairs = [(ff["ic4"][p][:, jj * 1024:(jj + 1) * 1024],
                              inter_sc[(g * 4 + jj) * 128:(g * 4 + jj + 1) * 128, tg * 1024:(tg + 1) * 1024])
                             for jj in range(4)]
                    pairs += [(ff["wd4"][p][:, jj * 512:(jj + 1) * 512],
                               wd_all[(g * 4 + jj) * 128:(g * 4 + jj + 1) * 128, hc * 512:(hc + 1) * 512])
                              for jj in range(4)]
                    lw = stI.batch(rec, "scalar", p, pairs, waits=w)
                    w = [lw]
                    if g == 0:
                        w.append((sV, v_prev))
                    rec.emit("tensor", waits=w)
                    for jj in range(4):
                        j = g * 4 + jj
                        for t in range(8):
                            rec.emit("tensor", lambda e, p=p, t=t, j=j, jj=jj: e.matmul(
                                pst[t], ff["ic4"][p][:, jj * 1024 + t * 128: jj * 1024 + (t + 1) * 128],
                                ff["wd4"][p][:, jj * 512:(jj + 1) * 512],
                                start=(j == 0), stop=(j == 127)),
                                inc=(sPE, 1) if t == 7 else None)
                pe_done = sPE.c
                for t in range(8):
                    p = t % 2
                    lw = stX.batch(rec, "sync", p,
                                   [(ff["xp"][p][:], src[tg * 8 + t][:, hc * 512:(hc + 1) * 512])],
                                   waits=[(sV, sV.c)])
                    ow = stO.sems[p]
                    rec.emit("vector", lambda e, p=p, t=t: e.tensor_tensor(
                        ff["op"][p][:], pst[t], ff["xp"][p][:], ALU.add),
                        waits=[(sPE, pe_done), lw, (ow, ow.c)], inc=(sV, 1))
                    stO.batch(rec, "sync", p,
                              [(dst[tg * 8 + t][:, hc * 512:(hc + 1) * 512], ff["op"][p][:])],
                              waits=[(sV, sV.c)])
        return

    emit_identity()

    if mode == "full":
        weave = (deq_tile_emitters("g", gpk, gsc, wg_sl)
                 + deq_tile_emitters("u", upk, usc, wu_sl)
                 + deq_tile_emitters("d", dpk, dsc, wd_sl))
        emit_attna(x_my, xb1, weave=weave)
        emit_deq_gather(wg_sl, wg_all)
        emit_deq_gather(wu_sl, wu_all)
        cc = emit_deq_gather(wd_sl, wd_all)
        emit_ffn(xb1, xb2, cc)
        emit_attnb(xb2, xb3)
        emit_ffn(xb3, out_my, cc)
        if False:
            fd = {
                "x1": ap(nc.dram_tensor("fdbg_x1", [NT, 128, H], F32, kind="ExternalOutput")),
                "x2": ap(nc.dram_tensor("fdbg_x2", [NT, 128, H], F32, kind="ExternalOutput")),
                "x3": ap(nc.dram_tensor("fdbg_x3", [NT, 128, H], F32, kind="ExternalOutput")),
            }
            w = [(sm, sm.c) for sm in stO.sems] + [(sV, sV.c), (sSC, sSC.c), (sPE, sPE.c)]
            rec.dma("sync", fd["x1"], xb1, waits=w, inc=(stO.sems[0], 16))
            rec.dma("sync", fd["x2"], xb2, inc=(stO.sems[0], 16))
            rec.dma("sync", fd["x3"], xb3, inc=(stO.sems[0], 16))
        for sm in stO.sems:
            rec.emit("sync", waits=[(sm, sm.c)])
    elif mode in ("ffn", "ffn_dbg"):
        emit_dequant("g", gpk, gsc, wg_sl)
        emit_dequant("u", upk, usc, wu_sl)
        emit_dequant("d", dpk, dsc, wd_sl)
        emit_deq_gather(wg_sl, wg_all)
        emit_deq_gather(wu_sl, wu_all)
        cc = emit_deq_gather(wd_sl, wd_all)
        dbg = None
        if mode == "ffn_dbg":
            dbg = {
                "xn": ap(nc.dram_tensor("dbg_xn", [128, H], BF16, kind="ExternalOutput")),
                "xc": ap(nc.dram_tensor("dbg_xc", [128, H], F32, kind="ExternalOutput")),
                "mu": ap(nc.dram_tensor("dbg_mu", [128, 1], F32, kind="ExternalOutput")),
                "rs": ap(nc.dram_tensor("dbg_rs", [128, 1], F32, kind="ExternalOutput")),
                "xin": ap(nc.dram_tensor("dbg_xin", [128, H], F32, kind="ExternalOutput")),
            }
        emit_ffn(x_my, out_my, cc, dbg=dbg)
        if mode == "ffn_dbg":
            dbg_wg = ap(nc.dram_tensor("dbg_wg", [NCORES * 16 * 128, 2048], BF16, kind="ExternalOutput"))
            dbg_wd = ap(nc.dram_tensor("dbg_wd", [NCORES * 16 * 128, 2048], BF16, kind="ExternalOutput"))
            dbg_int = ap(nc.dram_tensor("dbg_int", [128 * 128, 2048], BF16, kind="ExternalOutput"))
            dbg_xnT = ap(nc.dram_tensor("dbg_xnT", [128, 16 * H], BF16, kind="ExternalOutput"))
            w = [(sm, sm.c) for sm in stO.sems] + [(sV, sV.c), (sSC, sSC.c), (sPE, sPE.c)]
            rec.dma("sync", dbg_wg, wg_all, waits=w, inc=(stO.sems[0], 16))
            rec.dma("sync", dbg_wd, wd_all, inc=(stO.sems[0], 16))
            rec.dma("sync", dbg_int, inter_sc, inc=(stO.sems[0], 16))
            rec.dma("sync", dbg_xnT, ff["xnT"][:], inc=(stO.sems[0], 16))
        for sm in stO.sems:
            rec.emit("sync", waits=[(sm, sm.c)])
    elif mode in ("attnb", "attnb_dbg"):
        emit_attnb(x_my, out_my)
        if mode == "attnb_dbg":
            bd = {
                "kp": ap(nc.dram_tensor("bdbg_kp", [16 * 128, 2048], BF16, kind="ExternalOutput")),
                "vp": ap(nc.dram_tensor("bdbg_vp", [16 * 128, 2048], BF16, kind="ExternalOutput")),
                "kvp": ap(nc.dram_tensor("bdbg_kvp", [128, 2064], F32, kind="ExternalOutput")),
                "kvs": ap(nc.dram_tensor("bdbg_kvs", [128, 2064], F32, kind="ExternalOutput")),
            }
            w = [(sm, sm.c) for sm in stO.sems] + [(sV, sV.c), (sSC, sSC.c), (sPE, sPE.c)]
            rec.dma("sync", bd["kp"], kp_sc, waits=w, inc=(stO.sems[0], 16))
            rec.dma("sync", bd["vp"], vp_sc, inc=(stO.sems[0], 16))
            rec.dma("sync", bd["kvp"], kv_part, inc=(stO.sems[0], 16))
            rec.dma("sync", bd["kvs"], kv_sum, inc=(stO.sems[0], 16))
        for sm in stO.sems:
            rec.emit("sync", waits=[(sm, sm.c)])
    elif mode in ("attna", "attna_dbg"):
        adbg = None
        if mode == "attna_dbg":
            adbg = {
                "qT": ap(nc.dram_tensor("adbg_qT", [16 * 128, 2048], BF16, kind="ExternalOutput")),
                "kTa": ap(nc.dram_tensor("adbg_kTa", [2 * 16 * 128, 2048], BF16, kind="ExternalOutput")),
                "va": ap(nc.dram_tensor("adbg_va", [2 * 16 * 128, 2048], BF16, kind="ExternalOutput")),
                "srow": ap(nc.dram_tensor("adbg_srow", [128, 4096], F32, kind="ExternalOutput")),
                "prow": ap(nc.dram_tensor("adbg_prow", [128, 4096], BF16, kind="ExternalOutput")),
                "z": ap(nc.dram_tensor("adbg_z", [128, 1], F32, kind="ExternalOutput")),
            }
        emit_attna(x_my, out_my, adbg=adbg)
        for sm in stO.sems:
            rec.emit("sync", waits=[(sm, sm.c)])
    else:
        raise NotImplementedError(mode)

    rec.replay()
    return nc


def shard_inputs(inputs):
    x = np.ascontiguousarray(np.asarray(inputs["L0_attn_a_x"], dtype=np.float32))
    qkv_w = np.ascontiguousarray(np.asarray(inputs["L0_attn_a_qkv"], dtype=np.float32))
    oproj = np.ascontiguousarray(np.asarray(inputs["L0_attn_a_o_proj"], dtype=np.float32))
    gamma = np.ascontiguousarray(np.asarray(inputs["L0_attn_a_gamma"], dtype=np.float32).reshape(1, H))
    beta = np.ascontiguousarray(np.asarray(inputs["L0_attn_a_beta"], dtype=np.float32).reshape(1, H))
    gpk = np.ascontiguousarray(np.asarray(inputs["L0_ffn_gate_W_packed"], dtype=np.int32))
    gsc = np.ascontiguousarray(np.asarray(inputs["L0_ffn_gate_scales"], dtype=np.float32))
    upk = np.ascontiguousarray(np.asarray(inputs["L0_ffn_up_W_packed"], dtype=np.int32))
    usc = np.ascontiguousarray(np.asarray(inputs["L0_ffn_up_scales"], dtype=np.float32))
    dpk = np.ascontiguousarray(np.asarray(inputs["L0_ffn_down_W_packed"], dtype=np.int32))
    dsc = np.ascontiguousarray(np.asarray(inputs["L0_ffn_down_scales"], dtype=np.float32))
    wb = {k: np.ascontiguousarray(np.asarray(inputs[f"L0_attn_b_{k}_w"], dtype=np.float32))
          for k in ("q", "k", "v", "o")}
    bb = {k: np.ascontiguousarray(np.asarray(inputs[f"L0_attn_b_{k}_b"], dtype=np.float32).reshape(1, H))
          for k in ("q", "k", "v", "o")}

    in_maps = []
    for c in range(NCORES):
        b, par = c // 2, c % 2
        xt = np.ascontiguousarray(x[b].reshape(32, 128, H)[par::2])
        mk = np.zeros((NT, 128, 256), dtype=np.float32)
        for i in range(NT):
            g = 2 * i + par
            r0 = 128 * g
            np_pan = g // 2 + 1
            c0 = 256 * (np_pan - 1)
            rr = r0 + np.arange(128)[:, None]
            cc = c0 + np.arange(256)[None, :]
            mk[i] = np.where(cc <= rr, 0.0, NEG).astype(np.float32)
        nsl = slice(c * 2048, (c + 1) * 2048)
        m = {
            "x_my": xt, "masks": mk,
            "qkv": qkv_w, "oproj": oproj, "gamma": gamma, "beta": beta,
            "bqw": wb["q"], "bkw": wb["k"], "bvw": wb["v"], "bow": wb["o"],
            "bqb": bb["q"], "bkb": bb["k"], "bvb": bb["v"], "bob": bb["o"],
            "gpk": np.ascontiguousarray(gpk[nsl]), "gsc": np.ascontiguousarray(gsc[nsl]),
            "upk": np.ascontiguousarray(upk[nsl]), "usc": np.ascontiguousarray(usc[nsl]),
            "dpk": np.ascontiguousarray(dpk[:, c * 1024:(c + 1) * 1024]),
            "dsc": np.ascontiguousarray(dsc[:, c * 64:(c + 1) * 64]),
        }
        in_maps.append(m)
    return in_maps


def gather_output(results):
    out = np.zeros((B, S, H), dtype=np.float32)
    for c in range(NCORES):
        b, par = c // 2, c % 2
        out[b].reshape(32, 128, H)[par::2] = np.asarray(results[c]["out_my"], dtype=np.float32)
    return out


_NC_CACHE = {}


def kernel(**inputs):
    mode = "full"
    if mode not in _NC_CACHE:
        _NC_CACHE[mode] = build_program(mode)
    nc = _NC_CACHE[mode]
    in_maps = shard_inputs(inputs)
    res = run_bass_kernel_spmd(nc, in_maps, core_ids=list(range(NCORES)))
    return gather_output(res.results)


# version 04
# speedup vs baseline: 1.0201x; 1.0194x over previous
import numpy as np
import concourse.bass as bass
import concourse.mybir as mybir
from concourse.bass_utils import run_bass_kernel_spmd

F32 = mybir.dt.float32
BF16 = mybir.dt.bfloat16
I32 = mybir.dt.int32
U8 = mybir.dt.uint8
AF = mybir.ActivationFunctionType
ALU = mybir.AluOpType
AX = mybir.AxisListType

H = 2048
S = 4096
B = 4
NT = 16
NCORES = 8
EPS_LN = 1e-5
EPS_B = 1e-6
NEG = -1e30

ENGINES = ("sync", "tensor", "vector", "scalar", "gpsimd")


class Sem:
    def __init__(self, nc, name):
        self.h = nc.alloc_semaphore(name=name)
        self.c = 0
        self.name = name

    def inc(self, k):
        self.c += k
        return self.c


class Rec:
    def __init__(self, nc):
        self.nc = nc
        self.ops = {e: [] for e in ENGINES}

    def emit(self, eng, builder=None, waits=(), inc=None):
        waits = [(s.h, int(v)) for (s, v) in waits if v > 0]
        if inc is not None:
            incs = inc if isinstance(inc, list) else [inc]
            for s, k in incs:
                s.inc(k)
            incs = [(s.h, int(k)) for (s, k) in incs]
        else:
            incs = []

        def fn(e, waits=waits, incs=incs, builder=builder):
            for h, v in waits:
                e.wait_ge(h, v)
            if builder is not None:
                ins = builder(e)
                for h, k in incs:
                    ins.then_inc(h, k)

        self.ops[eng].append(fn)

    def dma(self, eng, out, in_, waits=(), inc=None, **kw):
        self.emit(eng, lambda e: e.dma_start(out=out, in_=in_, **kw),
                  waits=waits, inc=inc)

    def replay(self):
        with self.nc.Block() as block:
            for eng in ENGINES:
                fns = self.ops[eng]
                getattr(block, eng)(lambda engine, fns=fns: [f(engine) for f in fns])


class Stream:

    def __init__(self, nc, name, nslots=2):
        self.sems = [Sem(nc, f"{name}{i}") for i in range(nslots)]

    def batch(self, rec, eng, slot, pairs, waits=(), **kw):
        s = self.sems[slot]
        for i, (o, inp) in enumerate(pairs):
            rec.dma(eng, o, inp, waits=(waits if i == 0 else ()), inc=(s, 16), **kw)
        return (s, s.c)


def build_program(mode="full"):
    nc = bass.Bass("TRN2", target_bir_lowering=False)
    ap = lambda t: t.ap()

    x_my = ap(nc.dram_tensor("x_my", [NT, 128, H], F32, kind="ExternalInput"))
    masks = ap(nc.dram_tensor("masks", [NT, 128, 256], F32, kind="ExternalInput"))
    qkv = ap(nc.dram_tensor("qkv", [H, 3 * H], F32, kind="ExternalInput"))
    oproj = ap(nc.dram_tensor("oproj", [H, H], F32, kind="ExternalInput"))
    gamma = ap(nc.dram_tensor("gamma", [1, H], F32, kind="ExternalInput"))
    beta = ap(nc.dram_tensor("beta", [1, H], F32, kind="ExternalInput"))
    bw_in = {k: ap(nc.dram_tensor(f"b{k}w", [H, H], F32, kind="ExternalInput"))
             for k in ("q", "k", "v", "o")}
    bb_in = {k: ap(nc.dram_tensor(f"b{k}b", [1, H], F32, kind="ExternalInput"))
             for k in ("q", "k", "v", "o")}
    gpk = ap(nc.dram_tensor("gpk", [2048, 1024], I32, kind="ExternalInput"))
    gsc = ap(nc.dram_tensor("gsc", [2048, 64], F32, kind="ExternalInput"))
    upk = ap(nc.dram_tensor("upk", [2048, 1024], I32, kind="ExternalInput"))
    usc = ap(nc.dram_tensor("usc", [2048, 64], F32, kind="ExternalInput"))
    dpk = ap(nc.dram_tensor("dpk", [2048, 1024], I32, kind="ExternalInput"))
    dsc = ap(nc.dram_tensor("dsc", [2048, 64], F32, kind="ExternalInput"))

    out_my = ap(nc.dram_tensor("out_my", [NT, 128, H], F32, kind="ExternalOutput"))

    dr = lambda name, shape, dt=BF16: ap(nc.dram_tensor(name, shape, dt, kind="Internal"))
    wg_sl = dr("wg_sl", [16 * 128, 2048])
    wu_sl = dr("wu_sl", [16 * 128, 2048])
    wd_sl = dr("wd_sl", [16 * 128, 2048])
    drs = lambda name, shape, dt=BF16: ap(nc.dram_tensor(name, shape, dt, kind="Internal", addr_space="Shared"))
    wg_all = drs("wg_all", [NCORES * 16 * 128, 2048])
    wu_all = drs("wu_all", [NCORES * 16 * 128, 2048])
    wd_all = drs("wd_all", [NCORES * 16 * 128, 2048])
    inter_sc = dr("inter_sc", [128 * 128, 2048])
    xb1 = ap(nc.dram_tensor("xb1", [NT, 128, H], F32, kind="Internal"))
    xb2 = ap(nc.dram_tensor("xb2", [NT, 128, H], F32, kind="Internal"))
    xb3 = ap(nc.dram_tensor("xb3", [NT, 128, H], F32, kind="Internal"))
    qT_sc = dr("qT_sc", [16 * 128, 2048])
    kT_sc = dr("kT_sc", [16 * 128, 2048])
    v_sc = dr("v_sc", [16 * 128, 2048])
    kT_all = dr("kT_all", [2 * 16 * 128, 2048])
    v_all = dr("v_all", [2 * 16 * 128, 2048])
    w_bf = {k: dr(f"{k}w_bf", [H, H]) for k in ("q", "k", "v", "o")}
    qwT_sc = dr("qwT_sc", [H, H])
    kwT_sc = dr("kwT_sc", [H, H])
    vwT_sc = dr("vwT_sc", [H, H])
    owT_sc = dr("owT_sc", [H, H])
    kp_sc = dr("kp_sc", [16 * 128, 2048])
    vp_sc = dr("vp_sc", [16 * 128, 2048])
    kv_part = ap(nc.dram_tensor("kv_part", [128, 2064], F32, kind="Internal"))
    kv_sum = ap(nc.dram_tensor("kv_sum", [128, 2064], F32, kind="Internal"))

    rec = Rec(nc)
    sPE = Sem(nc, "sPE")
    sV = Sem(nc, "sV")
    sSC = Sem(nc, "sSC")
    sGP = Sem(nc, "sGP")
    sCC = Sem(nc, "sCC")
    stX = Stream(nc, "stX", 2)
    stW = Stream(nc, "stW", 2)
    stI = Stream(nc, "stI", 2)
    stP = Stream(nc, "stP", 2)
    stO = Stream(nc, "stO", 2)
    stD = Stream(nc, "stD", 1)
    stS = Stream(nc, "stS", 2)
    stEvB = Stream(nc, "stEvB", 2)

    def ostores():
        return [(stO.sems[0], stO.sems[0].c), (stO.sems[1], stO.sems[1].c)]

    P = [nc.alloc_psum_tensor(f"P{i}", [128, 1024], F32) for i in range(4)]
    TRA = P[3][:, 0:64].bitcast(BF16)
    TRB = P[3][:, 512:576].bitcast(BF16)

    alloc = nc.alloc_sbuf_tensor
    ident = alloc("ident", [128, 128], BF16)
    onesf = alloc("onesf", [1, 128], F32)

    A1 = alloc("A1", [128, 65536], U8)
    A2 = alloc("A2", [128, 49152], U8)
    A3 = alloc("A3", [128, 32768], U8)
    A4 = alloc("A4", [128, 16384], U8)
    NB = {F32: 4, BF16: 2, I32: 4, U8: 1}

    def V(ar, off, n, dt):
        return ar[:, off:off + n * NB[dt]].bitcast(dt)

    dq = {
        "pk": alloc("dq_pk", [128, 1024], I32),
        "sc": alloc("dq_sc", [128, 64], F32),
        "f": alloc("dq_f", [128, 1024], F32),
        "m": alloc("dq_m", [128, 1024], F32),
        "t0": alloc("dq_t0", [128, 1024], F32),
        "t1": alloc("dq_t1", [128, 1024], F32),
        "ge": alloc("dq_ge", [128, 1024], F32),
        "l4": alloc("dq_l4", [128, 1024], U8),
        "l6": alloc("dq_l6", [128, 1024], U8),
        "vals": alloc("dq_vals", [128, 2048], BF16),
        "stage": alloc("dq_stage", [128, 2048], BF16),
    }
    dq["ci"] = dq["t1"][:].bitcast(I32)
    lnb = {
        "mu": alloc("ln_mu", [128, 1], F32),
        "ss": alloc("ln_ss", [128, 1], F32),
        "rs": alloc("ln_rs", [128, 1], F32),
        "xc": alloc("ln_xc", [128, H], F32),
        "sq": V(A4, 0, H, F32),
    }

    def emit_identity():
        rec.emit("gpsimd", lambda g: g.memset(onesf[:], 1.0))
        rec.emit("gpsimd", lambda g: g.memset(ident[:], 0.0))
        rec.emit("gpsimd", lambda g: g.affine_select(
            out=ident[:], in_=ident[:], compare_op=ALU.not_equal, fill=1.0,
            base=0, pattern=[[-1, 128]], channel_multiplier=1), inc=(sGP, 1))

    def deq_tile_emitters(which, pk_d, sc_d, sl_d):
        out = []
        state = {}

        def mk(t):
            def vec_part():
                emit_deq_vec(which, pk_d, sc_d, t, state)

            def pe_part():
                emit_deq_pe(which, sl_d, t, state)
            return vec_part, pe_part
        for t in range(16):
            out.append(mk(t))
        return out

    def emit_dequant(which, pk_d, sc_d, sl_d):
        for v, p in deq_tile_emitters(which, pk_d, sc_d, sl_d):
            v()
            p()

    def emit_deq_vec(which, pk_d, sc_d, t, state):
        if True:
            lw = stP.batch(rec, "scalar", 0,
                           [(dq["pk"][:], pk_d[t * 128:(t + 1) * 128, :]),
                            (dq["sc"][:], sc_d[t * 128:(t + 1) * 128, :])],
                           waits=[(sV, sV.c), (sPE, sPE.c)])
            v = "vector"
            rec.emit(v, waits=[lw])
            for pl in (0, 1):
                if pl == 0:
                    rec.emit(v, lambda e: e.tensor_single_scalar(dq["ci"][:], dq["pk"][:], 15, ALU.bitwise_and))
                else:
                    rec.emit(v, lambda e: e.tensor_single_scalar(dq["ci"][:], dq["pk"][:], 4, ALU.logical_shift_right))
                    rec.emit(v, lambda e: e.tensor_single_scalar(dq["ci"][:], dq["ci"][:], 15, ALU.bitwise_and))
                rec.emit(v, lambda e: e.tensor_copy(dq["f"][:], dq["ci"][:]))
                rec.emit(v, lambda e: e.tensor_single_scalar(dq["ge"][:], dq["f"][:], 8.0, ALU.is_ge))
                rec.emit(v, lambda e: e.scalar_tensor_tensor(dq["m"][:], dq["ge"][:], -8.0, dq["f"][:], ALU.mult, ALU.add))
                rec.emit(v, lambda e: e.tensor_single_scalar(dq["l4"][:], dq["m"][:], 4.0, ALU.is_lt))
                rec.emit(v, lambda e: e.tensor_single_scalar(dq["l6"][:], dq["m"][:], 6.0, ALU.is_lt))
                rec.emit(v, lambda e: e.tensor_scalar(dq["t0"][:], dq["m"][:], 2.0, -8.0, ALU.mult, ALU.add))
                rec.emit(v, lambda e: e.tensor_scalar_sub(dq["t1"][:], dq["m"][:], 2.0))
                rec.emit(v, lambda e: e.select(dq["t0"][:], dq["l6"][:], dq["t1"][:], dq["t0"][:]))
                rec.emit(v, lambda e: e.tensor_scalar_mul(dq["t1"][:], dq["m"][:], 0.5))
                rec.emit(v, lambda e: e.select(dq["t0"][:], dq["l4"][:], dq["t1"][:], dq["t0"][:]))
                rec.emit(v, lambda e: e.tensor_scalar(dq["ge"][:], dq["ge"][:], -2.0, 1.0, ALU.mult, ALU.add))
                rec.emit(v, lambda e: e.tensor_mul(dq["t0"][:], dq["t0"][:], dq["ge"][:]))
                o3 = dq["vals"][:, pl::2].rearrange("p (b j) -> p b j", j=16)
                i3 = dq["t0"][:].rearrange("p (b j) -> p b j", j=16)
                s3 = dq["sc"][:, :, None].broadcast_to([128, 64, 16])
                rec.emit(v, lambda e, o3=o3, i3=i3, s3=s3: e.tensor_tensor(o3, i3, s3, ALU.mult),
                         inc=(sV, 1))
            state[t] = sV.c

    def emit_deq_pe(which, sl_d, t, state):
        if True:
            v_done = state[t]
            for c in range(16):
                tr = TRA if c % 2 == 0 else TRB
                rec.emit("tensor", lambda e, c=c, tr=tr: e.transpose(
                    tr, dq["vals"][:, c * 128:(c + 1) * 128], ident[:]),
                    waits=[(sV, v_done), (sGP, 1), (sSC, sSC.c)], inc=(sPE, 1))
                pe_now = sPE.c
                rec.emit("scalar", lambda e, c=c, tr=tr: e.copy(dq["stage"][:, c * 128:(c + 1) * 128], tr),
                         waits=[(sPE, pe_now)], inc=(sSC, 1))
            sc_done = sSC.c
            if which in ("g", "u"):
                stw = stD.batch(rec, "sync", 0, [(sl_d[t * 128:(t + 1) * 128, :], dq["stage"][:])],
                                waits=[(sSC, sc_done)])
            else:
                stw = stD.batch(rec, "sync", 0,
                                [(sl_d[c * 128:(c + 1) * 128, t * 128:(t + 1) * 128],
                                  dq["stage"][:, c * 128:(c + 1) * 128]) for c in range(16)],
                                waits=[(sSC, sc_done)])
            rec.emit("scalar", waits=[stw])

    def emit_deq_gather(sl_d, all_d):
        sD = stD.sems[0]
        rec.emit("gpsimd", lambda g, sl_d=sl_d, all_d=all_d: g.collective_compute(
            "AllGather", ALU.bypass, replica_groups=[list(range(NCORES))],
            ins=[sl_d], outs=[all_d]),
            waits=[(sD, sD.c)], inc=(sCC, 1))
        return sCC.c

    def emit_ln(x_sb, xn_out, stats_to=None, eps=EPS_LN, xc=None):
        b = lnb
        xc = xc if xc is not None else b["xc"]
        rec.emit("vector", lambda e: e.reduce_sum(b["ss"][:], x_sb[:], axis=AX.X), inc=(sV, 1))
        rec.emit("vector", lambda e: e.tensor_scalar_mul(b["mu"][:], b["ss"][:], 1.0 / H),
                 waits=[(sV, sV.c)], inc=(sV, 1))
        rec.emit("vector", lambda e, xc=xc: e.tensor_scalar(xc[:], x_sb[:], b["mu"][:, 0:1], None, ALU.subtract),
                 waits=[(sV, sV.c)], inc=(sV, 1))
        rec.emit("scalar", lambda e, xc=xc: e.activation(b["sq"][:], xc[:], AF.Square, accum_out=b["ss"][:]),
                 waits=[(sV, sV.c)], inc=(sSC, 1))
        rec.emit("vector", lambda e, eps=eps: e.tensor_scalar(b["ss"][:], b["ss"][:], 1.0 / H, float(eps), ALU.mult, ALU.add),
                 waits=[(sSC, sSC.c)], inc=(sV, 1))
        rec.emit("scalar", lambda e: e.activation(b["ss"][:], b["ss"][:], AF.Ln),
                 waits=[(sV, sV.c)], inc=(sSC, 1))
        rec.emit("vector", lambda e: e.tensor_scalar_mul(b["ss"][:], b["ss"][:], -0.5),
                 waits=[(sSC, sSC.c)], inc=(sV, 1))
        rec.emit("scalar", lambda e: e.activation(b["rs"][:], b["ss"][:], AF.Exp),
                 waits=[(sV, sV.c)], inc=(sSC, 1))
        rec.emit("vector", lambda e, xc=xc: e.tensor_scalar(xn_out[:], xc[:], b["rs"][:, 0:1], None, ALU.mult),
                 waits=[(sSC, sSC.c)], inc=(sV, 1))
        if stats_to is not None:
            rec.emit("vector", lambda e: e.tensor_scalar_add(stats_to[:, 0:1], b["rs"][:], 1.0))
            rec.emit("vector", lambda e: e.tensor_tensor(stats_to[:, 1:2], b["mu"][:], b["rs"][:], ALU.mult))
            rec.emit("vector", lambda e: e.tensor_scalar_mul(stats_to[:, 1:2], stats_to[:, 1:2], -1.0), inc=(sV, 1))

    ff = {
        "x": V(A3, 0, H, F32),
        "xn": V(A3, 8192, H, BF16),
        "xnT": V(A1, 0, 16 * H, BF16),
        "wg": [V(A2, 0, 2048, BF16), V(A2, 4096, 2048, BF16)],
        "wu": [V(A2, 8192, 2048, BF16), V(A2, 12288, 2048, BF16)],
        "gl": V(A2, 16384, 1024, F32),
        "int": [V(A2, 20480, 2048, BF16), V(A2, 24576, 2048, BF16)],
    "ic": [V(A2, 28672, 1024, BF16), V(A2, 30720, 1024, BF16)],
        "wd": [V(A2, 32768, 512, BF16), V(A2, 33792, 512, BF16)],
        "xp": [V(A2, 34816, 512, F32), V(A2, 36864, 512, F32)],
        "op": [V(A2, 38912, 512, F32), V(A2, 40960, 512, F32)],
        "ic4": [V(A2, 0, 4096, BF16), V(A2, 8192, 4096, BF16)],
        "wd4": [V(A2, 16384, 2048, BF16), V(A2, 20480, 2048, BF16)],
    }

    aa = {
        "xn1": V(A3, 0, H, F32),
        "xng": V(A3, 8192, H, BF16),
        "grep": V(A3, 12288, H, F32),
        "brep": V(A3, 20480, H, F32),
        "lnst": alloc("aa_lnst", [128, 2 * NT], F32),
        "qkvb": [V(A2, 16384, 2048, BF16), V(A2, 20480, 2048, BF16)],
        "vwb": [V(A2, 0, 16 * 512, BF16), V(A2, 16384 - 8192 + 0 * 8192 + 8192, 0, BF16)],
        "ev": [alloc("aa_ev0", [128, 512], BF16), alloc("aa_ev1", [128, 512], BF16)],
        "qsb": [V(A2, 40960, 16 * 128, BF16), V(A2, 45056, 16 * 128, BF16)],
        "ksb": [V(A2, 0, 16 * 256, BF16), V(A2, 8192, 16 * 256, BF16)],
        "vsb": [V(A2, 16384, 2 * 2048, BF16), V(A2, 24576, 2 * 2048, BF16)],
        "msk": V(A2, 32768, NT * 256, BF16),
        "srow": V(A3, 0, 4096, F32),
        "prow": V(A4, 0, 4096, BF16),
        "nmax": alloc("aa_nmax", [128, 1], F32),
        "z": alloc("aa_z", [128, 1], F32),
        "rz": alloc("aa_rz", [128, 1], F32),
        "pt": [alloc("aa_pt0", [128, 128], BF16), alloc("aa_pt1", [128, 128], BF16)],
        "pvn": V(A4, 8192, 2048, BF16),
        "pvt": V(A4, 12288, 16 * 128, BF16),
        "opb": V(A1, 0, 16 * 2048, BF16),
        "xf": V(A3, 16384, H, F32),
        "o1": V(A3, 24576, H, F32),
    }
    aa["vwb"] = [V(A2, 0, 16 * 512, BF16), V(A2, 8192, 16 * 512, BF16)]
    aa["xn2"] = aa["xn1"]

    def emit_attna(src, dst, par_dummy=None, adbg=None, weave=None):
        wv = list(weave) if weave else []
        wstate = {"vq": [], "k": 0}

        def weave_step():
            if wstate["k"] < len(wv):
                wv[wstate["k"]][0]()
                wstate["vq"].append(wv[wstate["k"]][1])
                wstate["k"] += 1
            if len(wstate["vq"]) > 1:
                wstate["vq"].pop(0)()

        def weave_flush():
            while wstate["k"] < len(wv):
                weave_step()
            while wstate["vq"]:
                wstate["vq"].pop(0)()
        xngT = ff["xnT"]
        lw_g = stD.batch(rec, "sync", 0, [(aa["grep"][:], gamma.partition_broadcast(128))])
        lw_g = stD.batch(rec, "sync", 0, [(aa["brep"][:], beta.partition_broadcast(128))])
        stMK = Stream(nc, "stMK", 1)
        lw_mk = stMK.batch(rec, "gpsimd", 0, [(aa["msk"][:].rearrange("p (i c) -> p i c", c=256), masks.transpose([1, 0, 2]))])
        stOP = Stream(nc, "stOP", 1)

        for i in range(NT):
            weave_step()
            lw = stX.batch(rec, "sync", i % 2, [(ff["x"][:], src[i])],
                           waits=[(sV, sV.c), (sSC, sSC.c)])
            rec.emit("vector", waits=[lw, lw_g])
            emit_ln(ff["x"], aa["xn1"], stats_to=aa["lnst"][:, 2 * i:2 * i + 2])
            emit_ln(aa["xn1"], aa["xn1"])
            rec.emit("vector", lambda e: e.tensor_mul(aa["xn1"][:], aa["xn1"][:], aa["grep"][:]))
            rec.emit("vector", lambda e: e.tensor_add(aa["xng"][:], aa["xn1"][:], aa["brep"][:]), inc=(sV, 1))
            vr = sV.c
            for c in range(16):
                tr = TRA if c % 2 == 0 else TRB
                rec.emit("tensor", lambda e, c=c, tr=tr: e.transpose(
                    tr, aa["xng"][:, c * 128:(c + 1) * 128], ident[:]),
                    waits=[(sV, vr), (sGP, 1), (sSC, sSC.c)], inc=(sPE, 1))
                pe = sPE.c
                rec.emit("scalar", lambda e, c=c, i=i, tr=tr: e.copy(
                    xngT[:, c * H + i * 128: c * H + (i + 1) * 128], tr),
                    waits=[(sPE, pe)], inc=(sSC, 1))
        xngT_sc = sSC.c

        qkT_sc = {0: qT_sc, 1: kT_sc}
        stEv = Stream(nc, "stEv", 2)
        ev_guard = [(sSC, 0), (sSC, 0)]
        for dch in range(32):
            weave_step()
            p = dch % 2
            src_cols = qkv[:, (dch % 16) * 128 + (dch // 16) * 2048:][:, 0:128]
            lw = stW.batch(rec, "gpsimd", p,
                           [(aa["qkvb"][p][:].rearrange("p (c d) -> p c d", d=128),
                             src_cols.rearrange("(c p) d -> p c d", p=128))],
                           waits=[(sPE, sPE.c)])
            for tch in range(4):
                bank = P[p] if tch < 2 else P[2 + p]
                reg = bank[:, (tch % 2) * 512:(tch % 2) * 512 + 512]
                w = [lw, (sSC, xngT_sc), ev_guard[p]]
                rec.emit("tensor", waits=w)
                for c in range(16):
                    rec.emit("tensor", lambda e, p=p, c=c, tch=tch, reg=reg: e.matmul(
                        reg, aa["qkvb"][p][:, c * 128:(c + 1) * 128],
                        xngT[:, c * H + tch * 512: c * H + (tch + 1) * 512],
                        start=(c == 0), stop=(c == 15)),
                        inc=(sPE, 1) if c == 15 else None)
                pe = sPE.c
                ep = tch % 2
                rec.emit("scalar", lambda e, reg=reg, ep=ep: e.copy(aa["ev"][ep][:], reg),
                         waits=[(sPE, pe), (stEv.sems[ep], stEv.sems[ep].c)], inc=(sSC, 1))
                stEv.batch(rec, "sync", ep,
                           [(qkT_sc[dch // 16][(dch % 16) * 128:(dch % 16 + 1) * 128, tch * 512:(tch + 1) * 512],
                             aa["ev"][ep][:])],
                           waits=[(sSC, sSC.c)])
            ev_guard[p] = (sSC, sSC.c)

        weave_flush()
        for half in range(2):
            for dch4 in range(4):
                p = dch4 % 2
                src_cols = qkv[:, 2 * H + dch4 * 512: 2 * H + (dch4 + 1) * 512]
                lw = stW.batch(rec, "gpsimd", p,
                               [(aa["vwb"][p][:].rearrange("p (c d) -> p c d", d=512),
                                 src_cols.rearrange("(c p) d -> p c d", p=128))],
                               waits=[(sPE, sPE.c)])
                rec.emit("tensor", waits=[lw, (sV, sV.c), (sSC, sSC.c)])
                for c in range(16):
                    for t in range(8):
                        rec.emit("tensor", lambda e, p=p, c=c, t=t, half=half: e.matmul(
                            P[t // 2][:, (t % 2) * 512:(t % 2) * 512 + 512],
                            xngT[:, c * H + half * 1024 + t * 128: c * H + half * 1024 + (t + 1) * 128],
                            aa["vwb"][p][:, c * 512:(c + 1) * 512],
                            start=(c == 0), stop=(c == 15)),
                            inc=(sPE, 1) if (c == 15 and t == 7) else None)
                pe = sPE.c
                for t in range(8):
                    ep = t % 2
                    rec.emit("scalar", lambda e, t=t, ep=ep: e.copy(
                        aa["ev"][ep][:], P[t // 2][:, (t % 2) * 512:(t % 2) * 512 + 512]),
                        waits=[(sPE, pe), (stEv.sems[ep], stEv.sems[ep].c)], inc=(sSC, 1))
                    stEv.batch(rec, "sync", ep,
                               [(v_sc[(half * 8 + t) * 128:(half * 8 + t + 1) * 128, dch4 * 512:(dch4 + 1) * 512],
                                 aa["ev"][ep][:])],
                               waits=[(sSC, sSC.c)])

        opw = stOP.batch(rec, "gpsimd", 0,
                         [(aa["opb"][:, c * 2048:(c + 1) * 2048],
                           oproj[c * 128:(c + 1) * 128, :]) for c in range(16)],
                         waits=[(sPE, sPE.c)])

        pair_groups = [[0, 1], [2, 3], [4, 5], [6, 7]]
        ev_all = [(sm, sm.c) for sm in stEv.sems]
        rec.emit("gpsimd", lambda g: g.collective_compute(
            "AllGather", ALU.bypass, replica_groups=pair_groups,
            ins=[kT_sc], outs=[kT_all]), waits=ev_all, inc=(sCC, 1))
        rec.emit("gpsimd", lambda g: g.collective_compute(
            "AllGather", ALU.bypass, replica_groups=pair_groups,
            ins=[v_sc], outs=[v_all]), inc=(sCC, 1))
        ag_cc = sCC.c
        if adbg is not None:
            rec.dma("sync", adbg["qT"], qT_sc, waits=[(sCC, ag_cc)], inc=(stO.sems[0], 16))
            rec.dma("sync", adbg["kTa"], kT_all, inc=(stO.sems[0], 16))
            rec.dma("sync", adbg["va"], v_all, inc=(stO.sems[0], 16))

        kT_r = kT_all.rearrange("(r c p) t -> r c p t", r=2, p=128)
        v_r = v_all.rearrange("(r i p) d -> r i p d", r=2, p=128)

        stQT = Stream(nc, "stQT", 2)
        stKP = Stream(nc, "stKP", 2)
        stVP = Stream(nc, "stVP", 2)
        S_reg = [P[0][:, 0:256], P[0][:, 512:768]]
        prow_s = [aa["prow"], lnb["xc"][:].bitcast(BF16)]
        nmax_s = [aa["nmax"], alloc("aa_nmax1", [128, 1], F32)]
        z_s = [aa["z"], alloc("aa_z1", [128, 1], F32)]
        rz_s = [aa["rz"], alloc("aa_rz1", [128, 1], F32)]
        st = {"s_guard": [(sSC, 0), (sSC, 0)], "o1_prev": (sV, 0),
              "q_guard": [(sPE, 0), (sPE, 0)], "k_guard": [(sPE, 0), (sPE, 0)],
              "v_guard": [(sPE, 0), (sPE, 0)], "pt_done": [(sPE, 0), (sPE, 0)],
              "exp_sc": (sSC, 0), "lq": {}, "rz_v": {}, "exp_of": {}}

        def aa_pass1(i):
            ip = i % 2
            npan = i + 1
            st["lq"][i] = stQT.batch(rec, "sync", ip,
                            [(aa["qsb"][ip][:].rearrange("p (c t) -> p c t", t=128),
                              qT_sc.rearrange("(c p) t -> p c t", p=128)[:, :, i * 128:(i + 1) * 128])],
                            waits=[st["q_guard"][ip], (sCC, ag_cc)])
            for pan in range(npan):
                pp = pan % 2
                prs = []
                for gsub in range(2):
                    gc = 2 * pan + gsub
                    rr = gc % 2
                    lt = gc // 2
                    prs.append((aa["ksb"][pp][:].rearrange("p (c t) -> p c t", t=256)[:, :, gsub * 128:(gsub + 1) * 128],
                                kT_r[rr, :, :, lt * 128:(lt + 1) * 128].transpose([1, 0, 2])))
                lk = stKP.batch(rec, "sync", pp, prs, waits=[st["k_guard"][pp], (sCC, ag_cc)])
                rec.emit("tensor", waits=[st["lq"][i], lk, st["s_guard"][pp]])
                for c in range(16):
                    rec.emit("tensor", lambda e, ip=ip, pp=pp, c=c: e.matmul(
                        S_reg[pp], aa["qsb"][ip][:, c * 128:(c + 1) * 128],
                        aa["ksb"][pp][:, c * 256:(c + 1) * 256],
                        start=(c == 0), stop=(c == 15)),
                        inc=(sPE, 1) if c == 15 else None)
                st["k_guard"][pp] = (sPE, sPE.c)
                pe = sPE.c
                w = [(sPE, pe)]
                if pan == 0:
                    w.append(st["exp_sc"])
                if pan == npan - 1:
                    rec.emit("vector", lambda e, i=i, pan=pan, pp=pp: e.scalar_tensor_tensor(
                        aa["srow"][:, pan * 256:(pan + 1) * 256], S_reg[pp], 1.0 / 45.254833995939045,
                        aa["msk"][:, i * 256:(i + 1) * 256], ALU.mult, ALU.add),
                        waits=w + [lw_mk], inc=(sV, 1))
                else:
                    rec.emit("vector", lambda e, pan=pan, pp=pp: e.tensor_scalar_mul(
                        aa["srow"][:, pan * 256:(pan + 1) * 256], S_reg[pp], 1.0 / 45.254833995939045),
                        waits=w, inc=(sV, 1))
                st["s_guard"][pp] = (sV, sV.c)

        def aa_softmax(i):
            ip = i % 2
            npan = i + 1
            rec.emit("vector", lambda e, i=i, npan=npan, ip=ip: e.reduce_max(
                nmax_s[ip][:], aa["srow"][:, 0:npan * 256], axis=AX.X, negate=True),
                waits=[(sV, sV.c)], inc=(sV, 1))
            rec.emit("scalar", lambda e, npan=npan, ip=ip: e.activation(
                prow_s[ip][:, 0:npan * 256], aa["srow"][:, 0:npan * 256], AF.Exp,
                bias=nmax_s[ip][:, 0:1], accum_out=z_s[ip][:]),
                waits=[(sV, sV.c), st["pt_done"][ip]], inc=(sSC, 1))
            st["exp_sc"] = (sSC, sSC.c)
            st["exp_of"][i] = (sSC, sSC.c)
            rec.emit("vector", lambda e, ip=ip: e.reciprocal(rz_s[ip][:], z_s[ip][:]),
                     waits=[(sSC, sSC.c)], inc=(sV, 1))
            st["rz_v"][i] = sV.c
            if adbg is not None and i == 3:
                rec.dma("sync", adbg["srow"], aa["srow"][:], waits=[(sV, sV.c)], inc=(stO.sems[0], 16))
                rec.dma("sync", adbg["prow"], prow_s[ip][:], inc=(stO.sems[0], 16))
                rec.dma("sync", adbg["z"], z_s[ip][:], inc=(stO.sems[0], 16))
                rec.emit("scalar", waits=[(stO.sems[0], stO.sems[0].c)])
                rec.emit("vector", waits=[(stO.sems[0], stO.sems[0].c)])

        def aa_pass2(i):
            ip = i % 2
            npan = i + 1
            for pan in range(npan):
                pp = pan % 2
                prs = []
                for gsub in range(2):
                    gc = 2 * pan + gsub
                    rr = gc % 2
                    lt = gc // 2
                    prs.append((aa["vsb"][pp][:, gsub * 2048:(gsub + 1) * 2048], v_r[rr, lt]))
                lv = stVP.batch(rec, "sync", pp, prs, waits=[st["v_guard"][pp], (sCC, ag_cc)])
                for gsub in range(2):
                    b = 2 * pan + gsub
                    tp = b % 2
                    tr = TRA if tp == 0 else TRB
                    rec.emit("tensor", lambda e, b=b, tr=tr, ip=ip: e.transpose(
                        tr, prow_s[ip][:, b * 128:(b + 1) * 128], ident[:]),
                        waits=[st["exp_of"][i], (sSC, sSC.c), (sGP, 1)], inc=(sPE, 1))
                    pe = sPE.c
                    rec.emit("scalar", lambda e, tp=tp, tr=tr: e.copy(aa["pt"][tp][:], tr),
                             waits=[(sPE, pe)], inc=(sSC, 1))
                    ptsc = sSC.c
                    w = [lv, (sSC, ptsc)]
                    if b == 0:
                        w.append(st["o1_prev"])
                    rec.emit("tensor", waits=w)
                    for d4 in range(4):
                        rec.emit("tensor", lambda e, tp=tp, gsub=gsub, d4=d4, b=b, npan=npan: e.matmul(
                            P[1 + d4 // 2][:, (d4 % 2) * 512:(d4 % 2) * 512 + 512],
                            aa["pt"][tp][:],
                            aa["vsb"][(b // 2) % 2][:, gsub * 2048 + d4 * 512: gsub * 2048 + (d4 + 1) * 512],
                            start=(b == 0), stop=(b == 2 * npan - 1)),
                            inc=(sPE, 1) if d4 == 3 else None)
                    st["v_guard"][pp] = (sPE, sPE.c)
            st["pt_done"][ip] = (sPE, sPE.c)
            pv_pe = sPE.c
            rec.emit("vector", lambda e, ip=ip: e.tensor_scalar(
                aa["pvn"][:, 0:1024], P[1][:], rz_s[ip][:, 0:1], None, ALU.mult),
                waits=[(sPE, pv_pe), (sV, st["rz_v"][i])], inc=(sV, 1))
            rec.emit("vector", lambda e, ip=ip: e.tensor_scalar(
                aa["pvn"][:, 1024:2048], P[2][:], rz_s[ip][:, 0:1], None, ALU.mult),
                inc=(sV, 1))
            pvn_v = sV.c
            for c in range(16):
                tr = TRA if c % 2 == 0 else TRB
                rec.emit("tensor", lambda e, c=c, tr=tr: e.transpose(
                    tr, aa["pvn"][:, c * 128:(c + 1) * 128], ident[:]),
                    waits=[(sV, pvn_v), (sSC, sSC.c)], inc=(sPE, 1))
                pe = sPE.c
                rec.emit("scalar", lambda e, c=c, tr=tr: e.copy(
                    aa["pvt"][:, c * 128:(c + 1) * 128], tr),
                    waits=[(sPE, pe)], inc=(sSC, 1))
            pvt_sc = sSC.c
            lx = stX.batch(rec, "sync", ip, [(aa["xf"][:], src[i])], waits=[(sV, sV.c)])
            rec.emit("tensor", waits=[(sSC, pvt_sc), opw])
            for oc in range(4):
                for c in range(16):
                    rec.emit("tensor", lambda e, oc=oc, c=c: e.matmul(
                        P[1 + oc // 2][:, (oc % 2) * 512:(oc % 2) * 512 + 512],
                        aa["pvt"][:, c * 128:(c + 1) * 128],
                        aa["opb"][:, c * 2048 + oc * 512: c * 2048 + (oc + 1) * 512],
                        start=(c == 0), stop=(c == 15)),
                        inc=(sPE, 1) if c == 15 else None)
            o_pe = sPE.c
            rec.emit("vector", lambda e, i=i: e.tensor_scalar(
                aa["o1"][:], aa["xf"][:], aa["lnst"][:, 2 * i:2 * i + 1], aa["lnst"][:, 2 * i + 1:2 * i + 2],
                ALU.mult, ALU.add),
                waits=[lx, (sV, sV.c), (stO.sems[0], stO.sems[0].c), (stO.sems[1], stO.sems[1].c)],
                inc=(sV, 1))
            rec.emit("vector", lambda e: e.tensor_add(aa["o1"][:, 0:1024], aa["o1"][:, 0:1024], P[1][:]),
                     waits=[(sPE, o_pe)], inc=(sV, 1))
            rec.emit("vector", lambda e: e.tensor_add(aa["o1"][:, 1024:2048], aa["o1"][:, 1024:2048], P[2][:]),
                     inc=(sV, 1))
            st["o1_prev"] = (sV, sV.c)
            st["q_guard"][ip] = (sPE, o_pe)
            ow = stO.sems[ip]
            rec.emit("vector", waits=[(ow, ow.c)])
            stO.batch(rec, "sync", ip, [(dst[i], aa["o1"][:])], waits=[(sV, sV.c)])

        import os as _os
        if _os.environ.get("AA_PIPE") != "1":
            for i in range(NT):
                aa_pass1(i)
                aa_softmax(i)
                aa_pass2(i)
        else:
            for i in range(NT):
                aa_pass1(i)
                if i >= 1:
                    aa_pass2(i - 1)
                aa_softmax(i)
            aa_pass2(NT - 1)

    ab = {
        "xn3": V(A3, 8192, H, BF16),
        "kbr": V(A3, 12288, H, F32),
        "vbr": V(A3, 20480, H, F32),
        "obr": V(A3, 12288, H, F32),
        "x": V(A3, 0, H, F32),
        "o1": V(A3, 20480, H, F32),
        "wc": [V(A2, 0, 512, BF16), V(A2, 1024, 512, BF16)],
        "pb": [V(A2, 0, H, BF16), V(A2, 4096, H, BF16)],
        "kpt": V(A2, 4096, H, BF16),
        "vpt": V(A2, 8192, H, BF16),
        "kvacc": V(A2, 12288, 2064, F32),
        "kvsb": V(A2, 20736, 2064, BF16),
        "qwc": [V(A2, 25088, 512, BF16), V(A2, 26112, 512, BF16)],
        "owc": [V(A2, 27136, 512, BF16), V(A2, 28160, 512, BF16)],
        "rzb": V(A2, 29184, 512, F32),
        "qpT": V(A2, 31232, 16 * 512, BF16),
        "atn": [V(A4, 8192, 512, BF16), V(A4, 9216, 512, BF16)],
        "el0": V(A4, 10240, 512, F32),
        "el1": V(A4, 12288, 512, F32),
        "msk8": V(A4, 14336, 512, U8),
        "qbc": alloc("ab_qbc", [128, 16], F32),
        "ones": alloc("ab_ones", [128, 1], BF16),
        "wc4": [V(A2, 0, 2048, BF16), V(A2, 4096, 2048, BF16)],
        "tw": [V(A2, 0, 2048, BF16), V(A2, 4096, 2048, BF16)],
    }

    pst_o = [[[P[(tt * 4 + eo) // 2][:, ((tt * 4 + eo) % 2) * 512:((tt * 4 + eo) % 2) * 512 + 512]
               for eo in range(4)] for tt in range(2)] for _ttp in range(2)]

    def emit_attnb(src, dst):
        stB = Stream(nc, "stB", 2)
        stKV = Stream(nc, "stKV", 2)
        xn3T = ff["xnT"]
        pb_guard = [(stB.sems[0], 0), (stB.sems[1], 0)]
        entry = [(sPE, sPE.c), (sV, sV.c)]
        for mi, mk in enumerate(("q", "k", "v", "o")):
            for st_ in range(16):
                p = st_ % 2
                w = [pb_guard[p]] + (entry if (mi == 0 and st_ < 2) else [])
                lw = stW.batch(rec, "gpsimd", p,
                               [(ab["pb"][p][:], bw_in[mk][st_ * 128:(st_ + 1) * 128, :])],
                               waits=w)
                pb_guard[p] = stB.batch(rec, "sync", p,
                                        [(w_bf[mk][st_ * 128:(st_ + 1) * 128, :], ab["pb"][p][:])],
                                        waits=[lw])
        wbf_done = [pb_guard[0], pb_guard[1]]
        wT_sc = {"q": qwT_sc, "k": kwT_sc, "v": vwT_sc, "o": owT_sc}
        stTW = Stream(nc, "stTW", 2)
        stTS = Stream(nc, "stTS", 2)
        tw_guard = [(stTS.sems[0], 0), (stTS.sems[1], 0)]
        for mk in ("q", "k", "v", "o"):
            for c in range(16):
                p = c % 2
                w = [tw_guard[p]] + ([wbf_done[0], wbf_done[1]] if (mk == "q" and c < 2) else [])
                lw = stTW.batch(rec, "scalar", p,
                                [(ab["tw"][p][:], w_bf[mk][:, c * 128:(c + 1) * 128])],
                                waits=w, transpose=True)
                tw_guard[p] = stTS.batch(rec, "sync", p,
                                         [(wT_sc[mk][c * 128:(c + 1) * 128, :], ab["tw"][p][:])],
                                         waits=[lw])
        wt_done = [tw_guard[0], tw_guard[1]]
        lw_b = stD.batch(rec, "sync", 0, [(ab["kbr"][:], bb_in["k"].partition_broadcast(128))])
        lw_b = stD.batch(rec, "sync", 0, [(ab["vbr"][:], bb_in["v"].partition_broadcast(128))])
        rec.dma("gpsimd", ab["qbc"][:],
                bb_in["q"].rearrange("o (c p) -> o c p", p=128).transpose([0, 2, 1])[0],
                inc=(sGP, 16), allow_slow_non_contiguous=True)
        qbc_gp = sGP.c
        rec.emit("gpsimd", lambda g: g.memset(ab["ones"][:], 1.0), inc=(sGP, 1))
        ones_gp = sGP.c
        for i in range(NT):
            w = [(sV, sV.c), (sSC, sSC.c)] + (ostores() if i == 0 else [])
            lw = stX.batch(rec, "sync", i % 2, [(ab["x"][:], src[i])], waits=w)
            rec.emit("vector", waits=[lw])
            emit_ln(ab["x"], ab["xn3"])
            vr = sV.c
            for c in range(16):
                tr = TRA if c % 2 == 0 else TRB
                rec.emit("tensor", lambda e, c=c, tr=tr: e.transpose(
                    tr, ab["xn3"][:, c * 128:(c + 1) * 128], ident[:]),
                    waits=[(sV, vr), (sGP, 1), (sSC, sSC.c)], inc=(sPE, 1))
                pe = sPE.c
                rec.emit("scalar", lambda e, c=c, i=i, tr=tr: e.copy(
                    xn3T[:, c * H + i * 128: c * H + (i + 1) * 128], tr),
                    waits=[(sPE, pe)], inc=(sSC, 1))
        xn3T_sc = sSC.c

        pst = [P[t // 2][:, (t % 2) * 512:(t % 2 + 1) * 512] for t in range(8)]
        first_bd = True
        for mk, biasr, do_elu, dstp in (("k", ab["kbr"], True, kp_sc), ("v", ab["vbr"], False, vp_sc)):
            for half in range(2):
                for dch4 in range(4):
                    pe_base = sPE.c
                    for cb in range(4):
                        p = cb % 2
                        if cb == 0:
                            w = [(sPE, pe_base)]
                            if first_bd:
                                w += [wt_done[0], wt_done[1], (sSC, xn3T_sc)]
                        else:
                            w = [(sPE, pe_base + 4 * (cb - 1))] if cb >= 2 else [(sPE, pe_base)]
                        lw = stI.batch(rec, "scalar", p,
                                       [(ab["wc4"][p][:, cc * 512:(cc + 1) * 512],
                                         wT_sc[mk][(cb * 4 + cc) * 128:(cb * 4 + cc + 1) * 128, dch4 * 512:(dch4 + 1) * 512])
                                        for cc in range(4)],
                                       waits=w)
                        w2 = [lw]
                        if cb == 0:
                            w2.append((sV, sV.c))
                        rec.emit("tensor", waits=w2)
                        for cc in range(4):
                            c = cb * 4 + cc
                            for t in range(8):
                                rec.emit("tensor", lambda e, p=p, c=c, cc=cc, t=t, half=half: e.matmul(
                                    pst[t], xn3T[:, c * H + half * 1024 + t * 128: c * H + half * 1024 + (t + 1) * 128],
                                    ab["wc4"][p][:, cc * 512:(cc + 1) * 512],
                                    start=(c == 0), stop=(c == 15)),
                                    inc=(sPE, 1) if t == 7 else None)
                    first_bd = False
                    pe_done = sPE.c
                    for t in range(8):
                        ep = t % 2
                        sEv = stEvB.sems[ep]
                        rec.emit("vector", lambda e, t=t, ep=ep, biasr=biasr, dch4=dch4: e.tensor_tensor(
                            ab["el0"][:] if False else ab["el0"][:], pst[t], biasr[:, dch4 * 512:(dch4 + 1) * 512], ALU.add),
                            waits=[(sPE, pe_done), lw_b, (sEv, sEv.c), (sV, sV.c)], inc=(sV, 1))
                        if do_elu:
                            rec.emit("vector", lambda e: e.tensor_single_scalar(ab["msk8"][:], ab["el0"][:], 0.0, ALU.is_gt))
                            rec.emit("scalar", lambda e: e.activation(ab["el1"][:], ab["el0"][:], AF.Exp),
                                     waits=[(sV, sV.c)], inc=(sSC, 1))
                            rec.emit("vector", lambda e: e.tensor_scalar_add(ab["el0"][:], ab["el0"][:], 1.0),
                                     waits=[(sSC, sSC.c)])
                            rec.emit("vector", lambda e, ep=ep: e.select(aa["ev"][ep][:], ab["msk8"][:], ab["el0"][:], ab["el1"][:]),
                                     inc=(sV, 1))
                        else:
                            rec.emit("vector", lambda e, ep=ep: e.tensor_copy(aa["ev"][ep][:], ab["el0"][:]), inc=(sV, 1))
                        stEvB.batch(rec, "sync", ep,
                                    [(dstp[(half * 8 + t) * 128:(half * 8 + t + 1) * 128, dch4 * 512:(dch4 + 1) * 512],
                                      aa["ev"][ep][:])],
                                    waits=[(sV, sV.c)])
        evb_all = [(sm, sm.c) for sm in stEvB.sems]

        rec.emit("vector", lambda e: e.memset(ab["kvacc"][:], 0.0), inc=(sV, 1))
        for i in range(NT):
            p = i % 2
            lw = stKV.batch(rec, "sync", p,
                            [(ab["kpt"][:], kp_sc[i * 128:(i + 1) * 128, :])],
                            waits=(evb_all if i == 0 else [(sPE, kv_pe)]))
            lw2 = stKV.batch(rec, "sync", p, [(ab["vpt"][:], vp_sc[i * 128:(i + 1) * 128, :])])
            w = [lw, lw2, (sGP, ones_gp)]
            if i > 0:
                w.append((sV, sV.c))
            rec.emit("tensor", waits=w)
            for h in range(16):
                rec.emit("tensor", lambda e, h=h: e.matmul(
                    P[h // 8][:, (h % 8) * 128:(h % 8) * 128 + 128],
                    ab["kpt"][:, h * 128:(h + 1) * 128],
                    ab["vpt"][:, h * 128:(h + 1) * 128],
                    start=True, stop=True))
                rec.emit("tensor", lambda e, h=h: e.matmul(
                    P[2][:, h:h + 1],
                    ab["kpt"][:, h * 128:(h + 1) * 128],
                    ab["ones"][:],
                    start=True, stop=True),
                    inc=(sPE, 1) if h == 15 else None)
            kv_pe = sPE.c
            rec.emit("vector", lambda e: e.tensor_add(ab["kvacc"][:, 0:1024], ab["kvacc"][:, 0:1024], P[0][:]),
                     waits=[(sPE, kv_pe)], inc=(sV, 1))
            rec.emit("vector", lambda e: e.tensor_add(ab["kvacc"][:, 1024:2048], ab["kvacc"][:, 1024:2048], P[1][:]))
            rec.emit("vector", lambda e: e.tensor_add(ab["kvacc"][:, 2048:2064], ab["kvacc"][:, 2048:2064], P[2][:, 0:16]),
                     inc=(sV, 1))
        kv_v = sV.c

        lw = stD.batch(rec, "sync", 0, [(kv_part, ab["kvacc"][:])], waits=[(sV, kv_v)])
        rec.emit("gpsimd", lambda g: g.collective_compute(
            "AllReduce", ALU.add, replica_groups=[[0, 1], [2, 3], [4, 5], [6, 7]],
            ins=[kv_part], outs=[kv_sum]), waits=[lw], inc=(sCC, 1))
        cc = sCC.c
        lw_kv = stD.batch(rec, "gpsimd", 0, [(ab["kvsb"][:], kv_sum)], waits=[(sCC, cc)])

        obw = stD.batch(rec, "sync", 0, [(ab["obr"][:], bb_in["o"].partition_broadcast(128))],
                        waits=[(sV, sV.c)])
        QP = [P[0][:, 0:512], P[0][:, 512:1024], P[1][:, 0:512], P[1][:, 512:1024]]
        AT = [P[1][:, 0:512], P[1][:, 512:1024]]
        ZR = P[2][:, 0:4]
        RB = P[2][:, 512:1024]
        qp_guard = (sSC, 0)
        at_guard = [(sV, 0), (sV, 0)]
        o2_guard = (sV, 0)
        for tch in range(4):
            qp_guard = (sV, sV.c)
            for dch4 in range(4):
                pe_base = sPE.c
                for cb in range(4):
                    pc = cb % 2
                    w = [(sPE, pe_base + 4 * (cb - 1))] if cb >= 2 else [(sPE, pe_base)]
                    if tch == 0 and dch4 == 0 and cb == 0:
                        w.append((sSC, xn3T_sc))
                    lw = stI.batch(rec, "scalar", pc,
                                   [(ab["wc4"][pc][:, cc * 512:(cc + 1) * 512],
                                     qwT_sc[(cb * 4 + cc) * 128:(cb * 4 + cc + 1) * 128, dch4 * 512:(dch4 + 1) * 512])
                                    for cc in range(4)],
                                   waits=w)
                    rec.emit("tensor", waits=[lw] + ([qp_guard] if cb == 0 else []))
                    for cc in range(4):
                        c = cb * 4 + cc
                        for di in range(4):
                            rec.emit("tensor", lambda e, pc=pc, c=c, cc=cc, di=di, tch=tch: e.matmul(
                                QP[di], ab["wc4"][pc][:, cc * 512 + di * 128: cc * 512 + (di + 1) * 128],
                                xn3T[:, c * H + tch * 512: c * H + (tch + 1) * 512],
                                start=(c == 0), stop=(c == 15)),
                                inc=(sPE, 1) if di == 3 else None)
                pe_q = sPE.c
                for di in range(4):
                    dch = dch4 * 4 + di
                    rec.emit("vector", lambda e, di=di, dch=dch: e.tensor_scalar(
                        ab["el0"][:], QP[di], ab["qbc"][:, dch:dch + 1], None, ALU.add),
                        waits=[(sPE, pe_q), (sGP, qbc_gp), (sV, sV.c)], inc=(sV, 1))
                    rec.emit("vector", lambda e: e.tensor_single_scalar(ab["msk8"][:], ab["el0"][:], 0.0, ALU.is_gt))
                    rec.emit("scalar", lambda e: e.activation(ab["el1"][:], ab["el0"][:], AF.Exp),
                             waits=[(sV, sV.c)], inc=(sSC, 1))
                    rec.emit("vector", lambda e: e.tensor_scalar_add(ab["el0"][:], ab["el0"][:], 1.0),
                             waits=[(sSC, sSC.c)])
                    rec.emit("vector", lambda e, dch=dch: e.select(ab["qpT"][:, dch * 512:(dch + 1) * 512],
                                                                   ab["msk8"][:], ab["el0"][:], ab["el1"][:]),
                             inc=(sV, 1))
                qp_guard = (sV, sV.c)
            qp_v = sV.c
            qp_guard = (sV, qp_v)
            for h in range(16):
                hp = h % 2
                rec.emit("tensor", waits=[(sV, sV.c), lw_kv])
                rec.emit("tensor", lambda e, h=h, hp=hp: e.matmul(
                    AT[hp], ab["kvsb"][:, h * 128:(h + 1) * 128],
                    ab["qpT"][:, h * 512:(h + 1) * 512],
                    start=True, stop=True))
                rec.emit("tensor", lambda e, h=h: e.matmul(
                    P[2][0:1, 0:512],
                    ab["kvsb"][:, 2048 + h:2048 + h + 1],
                    ab["qpT"][:, h * 512:(h + 1) * 512],
                    start=True, stop=True), inc=(sPE, 1))
                pe_at = sPE.c
                rec.emit("vector", lambda e: e.tensor_scalar_add(ab["rzb"][0:1, 0:512], P[2][0:1, 0:512], EPS_B),
                         waits=[(sPE, pe_at), o2_guard], inc=(sV, 1))
                rec.emit("vector", lambda e: e.reciprocal(ab["rzb"][0:1, 0:512], ab["rzb"][0:1, 0:512]),
                         waits=[(sV, sV.c)], inc=(sV, 1))
                rz_v = sV.c
                rec.emit("tensor", lambda e: e.matmul(
                    RB, onesf[0:1, :], ab["rzb"][0:1, 0:512], start=True, stop=True),
                    waits=[(sV, rz_v)], inc=(sPE, 1))
                pe_rb = sPE.c
                rec.emit("scalar", lambda e: e.copy(ab["el1"][:], RB),
                         waits=[(sPE, pe_rb), (sSC, sSC.c)], inc=(sSC, 1))
                rec.emit("vector", lambda e, h=h, hp=hp: e.tensor_tensor(
                    ab["qpT"][:, h * 512:(h + 1) * 512], AT[hp], ab["el1"][:], ALU.mult),
                    waits=[(sSC, sSC.c)], inc=(sV, 1))
                at_guard[hp] = (sV, sV.c)
            atn_v = sV.c
            for ttp in range(2):
                pe_base = sPE.c
                for eo in range(4):
                    for cb in range(4):
                        ci = eo * 4 + cb
                        pc = ci % 2
                        w = [(sPE, pe_base)] if ci < 2 else [(sPE, pe_base + 4 * (ci - 1))]
                        lw = stI.batch(rec, "scalar", pc,
                                       [(ab["wc4"][pc][:, cc * 512:(cc + 1) * 512],
                                         owT_sc[(cb * 4 + cc) * 128:(cb * 4 + cc + 1) * 128, eo * 512:(eo + 1) * 512])
                                        for cc in range(4)],
                                       waits=w)
                        w2 = [lw, (sV, atn_v)]
                        if eo == 0 and cb == 0:
                            w2.append(o2_guard)
                        rec.emit("tensor", waits=w2)
                        for cc in range(4):
                            c = cb * 4 + cc
                            for tt in range(2):
                                rec.emit("tensor", lambda e, pc=pc, c=c, cc=cc, eo=eo, tt=tt, ttp=ttp: e.matmul(
                                    pst_o[ttp][tt][eo],
                                    ab["qpT"][:, c * 512 + ttp * 256 + tt * 128: c * 512 + ttp * 256 + (tt + 1) * 128],
                                    ab["wc4"][pc][:, cc * 512:(cc + 1) * 512],
                                    start=(c == 0), stop=(c == 15)),
                                    inc=(sPE, 1) if tt == 1 else None)
                pe_o = sPE.c
                for tt in range(2):
                    gt = tch * 4 + ttp * 2 + tt
                    ip = gt % 2
                    lx = stX.batch(rec, "sync", ip, [(ab["x"][:], src[gt])],
                                   waits=[(sV, sV.c)])
                    ow = stO.sems[ip]
                    for eo in range(4):
                        rec.emit("vector", lambda e, tt=tt, ttp=ttp, eo=eo: e.tensor_tensor(
                            ab["o1"][:, eo * 512:(eo + 1) * 512], pst_o[ttp][tt][eo],
                            ab["obr"][:, eo * 512:(eo + 1) * 512], ALU.add),
                            waits=([(sPE, pe_o), obw,
                                    (stO.sems[0], stO.sems[0].c), (stO.sems[1], stO.sems[1].c)]
                                   if eo == 0 else []),
                            inc=(sV, 1) if eo == 3 else None)
                    rec.emit("vector", lambda e: e.tensor_add(ab["o1"][:], ab["o1"][:], ab["x"][:]),
                             waits=[lx], inc=(sV, 1))
                    o2_guard = (sV, sV.c)
                    stO.batch(rec, "sync", ip, [(dst[gt], ab["o1"][:])], waits=[(sV, sV.c)])

    def emit_ffn(src, dst, deq_cc, dbg=None):
        for i in range(NT):
            w = [(sV, sV.c), (sSC, sSC.c)] + (ostores() if i == 0 else [])
            lw = stX.batch(rec, "sync", i % 2, [(ff["x"][:], src[i])], waits=w)
            rec.emit("vector", waits=[lw])
            emit_ln(ff["x"], ff["xn"])
            vr = sV.c
            if dbg is not None and i == 0:
                rec.dma("sync", dbg["xn"], ff["xn"][:], waits=[(sV, vr)], inc=(stO.sems[0], 16))
                rec.dma("sync", dbg["xc"], lnb["xc"][:], inc=(stO.sems[0], 16))
                rec.dma("sync", dbg["mu"], lnb["mu"][:], inc=(stO.sems[0], 16))
                rec.dma("sync", dbg["rs"], lnb["rs"][:], inc=(stO.sems[0], 16))
                rec.dma("sync", dbg["xin"], ff["x"][:], inc=(stO.sems[0], 16))
                rec.emit("vector", waits=[(stO.sems[0], stO.sems[0].c)])
            for c in range(16):
                tr = TRA if c % 2 == 0 else TRB
                rec.emit("tensor", lambda e, c=c, tr=tr: e.transpose(
                    tr, ff["xn"][:, c * 128:(c + 1) * 128], ident[:]),
                    waits=[(sV, vr), (sGP, 1), (sSC, sSC.c)], inc=(sPE, 1))
                pe = sPE.c
                rec.emit("scalar", lambda e, c=c, i=i, tr=tr: e.copy(
                    ff["xnT"][:, c * H + i * 128: c * H + (i + 1) * 128], tr),
                    waits=[(sPE, pe)], inc=(sSC, 1))
        xnT_sc = sSC.c

        pe_base = sPE.c
        v_evac = {}
        st_strip = {}
        for j in range(128):
            p = j % 2
            w = [(sCC, deq_cc), (sPE, pe_base)] if j < 2 else [(sPE, pe_base + 4 * j - 4)]
            lw = stW.batch(rec, "scalar", p,
                           [(ff["wg"][p][:], wg_all[j * 128:(j + 1) * 128, :]),
                            (ff["wu"][p][:], wu_all[j * 128:(j + 1) * 128, :])],
                           waits=w)
            for th in range(2):
                u = 2 * j + th
                q = u % 2
                w = [lw, (sSC, xnT_sc)]
                if u >= 2:
                    w.append((sV, v_evac[u - 2]))
                rec.emit("tensor", waits=w)
                for tc in range(2):
                    for c in range(16):
                        last = tc == 1 and c == 15
                        rec.emit("tensor", lambda e, p=p, q=q, th=th, tc=tc, c=c: e.matmul(
                            P[q][:, tc * 512:(tc + 1) * 512],
                            ff["wg"][p][:, c * 128:(c + 1) * 128],
                            ff["xnT"][:, c * H + th * 1024 + tc * 512: c * H + th * 1024 + (tc + 1) * 512],
                            start=(c == 0), stop=(c == 15)),
                            inc=(sPE, 1) if last else None)
                pe_gate = sPE.c
                for tc in range(2):
                    for c in range(16):
                        last = tc == 1 and c == 15
                        rec.emit("tensor", lambda e, p=p, q=q, th=th, tc=tc, c=c: e.matmul(
                            P[2 + q][:, tc * 512:(tc + 1) * 512],
                            ff["wu"][p][:, c * 128:(c + 1) * 128],
                            ff["xnT"][:, c * H + th * 1024 + tc * 512: c * H + th * 1024 + (tc + 1) * 512],
                            start=(c == 0), stop=(c == 15)),
                            inc=(sPE, 1) if last else None)
                pe_up = sPE.c
                rec.emit("scalar", lambda e, q=q: e.activation(ff["gl"][:], P[q][:], AF.Gelu_apprx_tanh),
                         waits=[(sPE, pe_gate)], inc=(sSC, 1))
                scd = sSC.c
                w = [(sSC, scd), (sPE, pe_up)]
                if th == 0 and j >= 2:
                    w.append(st_strip[j - 2])
                rec.emit("vector", lambda e, p=p, q=q, th=th: e.tensor_tensor(
                    ff["int"][p][:, th * 1024:(th + 1) * 1024], ff["gl"][:], P[2 + q][:], ALU.mult),
                    waits=w, inc=(sV, 1))
                v_evac[u] = sV.c
            st_strip[j] = stS.batch(rec, "sync", p,
                                    [(inter_sc[j * 128:(j + 1) * 128, :], ff["int"][p][:])],
                                    waits=[(sV, v_evac[2 * j + 1])])
        strips_all = (stS.sems[1], stS.sems[1].c)
        strips_all0 = (stS.sems[0], stS.sems[0].c)

        pst = [P[t // 2][:, (t % 2) * 512:(t % 2 + 1) * 512] for t in range(8)]
        for tg in range(2):
            for hc in range(4):
                pe_base2 = sPE.c
                v_prev = sV.c
                for g in range(32):
                    p = g % 2
                    if g < 2:
                        w = [strips_all, strips_all0, (sCC, deq_cc)] if (tg == 0 and hc == 0) \
                            else [(sPE, pe_base2)]
                    else:
                        w = [(sPE, pe_base2 + 4 * (g - 1))]
                    pairs = [(ff["ic4"][p][:, jj * 1024:(jj + 1) * 1024],
                              inter_sc[(g * 4 + jj) * 128:(g * 4 + jj + 1) * 128, tg * 1024:(tg + 1) * 1024])
                             for jj in range(4)]
                    pairs += [(ff["wd4"][p][:, jj * 512:(jj + 1) * 512],
                               wd_all[(g * 4 + jj) * 128:(g * 4 + jj + 1) * 128, hc * 512:(hc + 1) * 512])
                              for jj in range(4)]
                    lw = stI.batch(rec, "scalar", p, pairs, waits=w)
                    w = [lw]
                    if g == 0:
                        w.append((sV, v_prev))
                    rec.emit("tensor", waits=w)
                    for jj in range(4):
                        j = g * 4 + jj
                        for t in range(8):
                            rec.emit("tensor", lambda e, p=p, t=t, j=j, jj=jj: e.matmul(
                                pst[t], ff["ic4"][p][:, jj * 1024 + t * 128: jj * 1024 + (t + 1) * 128],
                                ff["wd4"][p][:, jj * 512:(jj + 1) * 512],
                                start=(j == 0), stop=(j == 127)),
                                inc=(sPE, 1) if t == 7 else None)
                pe_done = sPE.c
                for t in range(8):
                    p = t % 2
                    lw = stX.batch(rec, "sync", p,
                                   [(ff["xp"][p][:], src[tg * 8 + t][:, hc * 512:(hc + 1) * 512])],
                                   waits=[(sV, sV.c)])
                    ow = stO.sems[p]
                    rec.emit("vector", lambda e, p=p, t=t: e.tensor_tensor(
                        ff["op"][p][:], pst[t], ff["xp"][p][:], ALU.add),
                        waits=[(sPE, pe_done), lw, (ow, ow.c)], inc=(sV, 1))
                    stO.batch(rec, "sync", p,
                              [(dst[tg * 8 + t][:, hc * 512:(hc + 1) * 512], ff["op"][p][:])],
                              waits=[(sV, sV.c)])
        return

    emit_identity()

    if mode == "full":
        weave = (deq_tile_emitters("g", gpk, gsc, wg_sl)
                 + deq_tile_emitters("u", upk, usc, wu_sl)
                 + deq_tile_emitters("d", dpk, dsc, wd_sl))
        emit_attna(x_my, xb1, weave=weave)
        emit_deq_gather(wg_sl, wg_all)
        emit_deq_gather(wu_sl, wu_all)
        cc = emit_deq_gather(wd_sl, wd_all)
        emit_ffn(xb1, xb2, cc)
        emit_attnb(xb2, xb3)
        emit_ffn(xb3, out_my, cc)
        if False:
            fd = {
                "x1": ap(nc.dram_tensor("fdbg_x1", [NT, 128, H], F32, kind="ExternalOutput")),
                "x2": ap(nc.dram_tensor("fdbg_x2", [NT, 128, H], F32, kind="ExternalOutput")),
                "x3": ap(nc.dram_tensor("fdbg_x3", [NT, 128, H], F32, kind="ExternalOutput")),
            }
            w = [(sm, sm.c) for sm in stO.sems] + [(sV, sV.c), (sSC, sSC.c), (sPE, sPE.c)]
            rec.dma("sync", fd["x1"], xb1, waits=w, inc=(stO.sems[0], 16))
            rec.dma("sync", fd["x2"], xb2, inc=(stO.sems[0], 16))
            rec.dma("sync", fd["x3"], xb3, inc=(stO.sems[0], 16))
        for sm in stO.sems:
            rec.emit("sync", waits=[(sm, sm.c)])
    elif mode in ("ffn", "ffn_dbg"):
        emit_dequant("g", gpk, gsc, wg_sl)
        emit_dequant("u", upk, usc, wu_sl)
        emit_dequant("d", dpk, dsc, wd_sl)
        emit_deq_gather(wg_sl, wg_all)
        emit_deq_gather(wu_sl, wu_all)
        cc = emit_deq_gather(wd_sl, wd_all)
        dbg = None
        if mode == "ffn_dbg":
            dbg = {
                "xn": ap(nc.dram_tensor("dbg_xn", [128, H], BF16, kind="ExternalOutput")),
                "xc": ap(nc.dram_tensor("dbg_xc", [128, H], F32, kind="ExternalOutput")),
                "mu": ap(nc.dram_tensor("dbg_mu", [128, 1], F32, kind="ExternalOutput")),
                "rs": ap(nc.dram_tensor("dbg_rs", [128, 1], F32, kind="ExternalOutput")),
                "xin": ap(nc.dram_tensor("dbg_xin", [128, H], F32, kind="ExternalOutput")),
            }
        emit_ffn(x_my, out_my, cc, dbg=dbg)
        if mode == "ffn_dbg":
            dbg_wg = ap(nc.dram_tensor("dbg_wg", [NCORES * 16 * 128, 2048], BF16, kind="ExternalOutput"))
            dbg_wd = ap(nc.dram_tensor("dbg_wd", [NCORES * 16 * 128, 2048], BF16, kind="ExternalOutput"))
            dbg_int = ap(nc.dram_tensor("dbg_int", [128 * 128, 2048], BF16, kind="ExternalOutput"))
            dbg_xnT = ap(nc.dram_tensor("dbg_xnT", [128, 16 * H], BF16, kind="ExternalOutput"))
            w = [(sm, sm.c) for sm in stO.sems] + [(sV, sV.c), (sSC, sSC.c), (sPE, sPE.c)]
            rec.dma("sync", dbg_wg, wg_all, waits=w, inc=(stO.sems[0], 16))
            rec.dma("sync", dbg_wd, wd_all, inc=(stO.sems[0], 16))
            rec.dma("sync", dbg_int, inter_sc, inc=(stO.sems[0], 16))
            rec.dma("sync", dbg_xnT, ff["xnT"][:], inc=(stO.sems[0], 16))
        for sm in stO.sems:
            rec.emit("sync", waits=[(sm, sm.c)])
    elif mode in ("attnb", "attnb_dbg"):
        emit_attnb(x_my, out_my)
        if mode == "attnb_dbg":
            bd = {
                "kp": ap(nc.dram_tensor("bdbg_kp", [16 * 128, 2048], BF16, kind="ExternalOutput")),
                "vp": ap(nc.dram_tensor("bdbg_vp", [16 * 128, 2048], BF16, kind="ExternalOutput")),
                "kvp": ap(nc.dram_tensor("bdbg_kvp", [128, 2064], F32, kind="ExternalOutput")),
                "kvs": ap(nc.dram_tensor("bdbg_kvs", [128, 2064], F32, kind="ExternalOutput")),
            }
            w = [(sm, sm.c) for sm in stO.sems] + [(sV, sV.c), (sSC, sSC.c), (sPE, sPE.c)]
            rec.dma("sync", bd["kp"], kp_sc, waits=w, inc=(stO.sems[0], 16))
            rec.dma("sync", bd["vp"], vp_sc, inc=(stO.sems[0], 16))
            rec.dma("sync", bd["kvp"], kv_part, inc=(stO.sems[0], 16))
            rec.dma("sync", bd["kvs"], kv_sum, inc=(stO.sems[0], 16))
        for sm in stO.sems:
            rec.emit("sync", waits=[(sm, sm.c)])
    elif mode in ("attna", "attna_dbg"):
        adbg = None
        if mode == "attna_dbg":
            adbg = {
                "qT": ap(nc.dram_tensor("adbg_qT", [16 * 128, 2048], BF16, kind="ExternalOutput")),
                "kTa": ap(nc.dram_tensor("adbg_kTa", [2 * 16 * 128, 2048], BF16, kind="ExternalOutput")),
                "va": ap(nc.dram_tensor("adbg_va", [2 * 16 * 128, 2048], BF16, kind="ExternalOutput")),
                "srow": ap(nc.dram_tensor("adbg_srow", [128, 4096], F32, kind="ExternalOutput")),
                "prow": ap(nc.dram_tensor("adbg_prow", [128, 4096], BF16, kind="ExternalOutput")),
                "z": ap(nc.dram_tensor("adbg_z", [128, 1], F32, kind="ExternalOutput")),
            }
        emit_attna(x_my, out_my, adbg=adbg)
        for sm in stO.sems:
            rec.emit("sync", waits=[(sm, sm.c)])
    else:
        raise NotImplementedError(mode)

    rec.replay()
    return nc


def shard_inputs(inputs):
    x = np.ascontiguousarray(np.asarray(inputs["L0_attn_a_x"], dtype=np.float32))
    qkv_w = np.ascontiguousarray(np.asarray(inputs["L0_attn_a_qkv"], dtype=np.float32))
    oproj = np.ascontiguousarray(np.asarray(inputs["L0_attn_a_o_proj"], dtype=np.float32))
    gamma = np.ascontiguousarray(np.asarray(inputs["L0_attn_a_gamma"], dtype=np.float32).reshape(1, H))
    beta = np.ascontiguousarray(np.asarray(inputs["L0_attn_a_beta"], dtype=np.float32).reshape(1, H))
    gpk = np.ascontiguousarray(np.asarray(inputs["L0_ffn_gate_W_packed"], dtype=np.int32))
    gsc = np.ascontiguousarray(np.asarray(inputs["L0_ffn_gate_scales"], dtype=np.float32))
    upk = np.ascontiguousarray(np.asarray(inputs["L0_ffn_up_W_packed"], dtype=np.int32))
    usc = np.ascontiguousarray(np.asarray(inputs["L0_ffn_up_scales"], dtype=np.float32))
    dpk = np.ascontiguousarray(np.asarray(inputs["L0_ffn_down_W_packed"], dtype=np.int32))
    dsc = np.ascontiguousarray(np.asarray(inputs["L0_ffn_down_scales"], dtype=np.float32))
    wb = {k: np.ascontiguousarray(np.asarray(inputs[f"L0_attn_b_{k}_w"], dtype=np.float32))
          for k in ("q", "k", "v", "o")}
    bb = {k: np.ascontiguousarray(np.asarray(inputs[f"L0_attn_b_{k}_b"], dtype=np.float32).reshape(1, H))
          for k in ("q", "k", "v", "o")}

    in_maps = []
    for c in range(NCORES):
        b, par = c // 2, c % 2
        xt = np.ascontiguousarray(x[b].reshape(32, 128, H)[par::2])
        mk = np.zeros((NT, 128, 256), dtype=np.float32)
        for i in range(NT):
            g = 2 * i + par
            r0 = 128 * g
            np_pan = g // 2 + 1
            c0 = 256 * (np_pan - 1)
            rr = r0 + np.arange(128)[:, None]
            cc = c0 + np.arange(256)[None, :]
            mk[i] = np.where(cc <= rr, 0.0, NEG).astype(np.float32)
        nsl = slice(c * 2048, (c + 1) * 2048)
        m = {
            "x_my": xt, "masks": mk,
            "qkv": qkv_w, "oproj": oproj, "gamma": gamma, "beta": beta,
            "bqw": wb["q"], "bkw": wb["k"], "bvw": wb["v"], "bow": wb["o"],
            "bqb": bb["q"], "bkb": bb["k"], "bvb": bb["v"], "bob": bb["o"],
            "gpk": np.ascontiguousarray(gpk[nsl]), "gsc": np.ascontiguousarray(gsc[nsl]),
            "upk": np.ascontiguousarray(upk[nsl]), "usc": np.ascontiguousarray(usc[nsl]),
            "dpk": np.ascontiguousarray(dpk[:, c * 1024:(c + 1) * 1024]),
            "dsc": np.ascontiguousarray(dsc[:, c * 64:(c + 1) * 64]),
        }
        in_maps.append(m)
    return in_maps


def gather_output(results):
    out = np.zeros((B, S, H), dtype=np.float32)
    for c in range(NCORES):
        b, par = c // 2, c % 2
        out[b].reshape(32, 128, H)[par::2] = np.asarray(results[c]["out_my"], dtype=np.float32)
    return out


_NC_CACHE = {}


def kernel(**inputs):
    mode = "full"
    if mode not in _NC_CACHE:
        _NC_CACHE[mode] = build_program(mode)
    nc = _NC_CACHE[mode]
    in_maps = shard_inputs(inputs)
    res = run_bass_kernel_spmd(nc, in_maps, core_ids=list(range(NCORES)))
    return gather_output(res.results)


# version 06
# speedup vs baseline: 1.0426x; 1.0221x over previous
import numpy as np
import concourse.bass as bass
import concourse.mybir as mybir
from concourse.bass_utils import run_bass_kernel_spmd

F32 = mybir.dt.float32
BF16 = mybir.dt.bfloat16
I32 = mybir.dt.int32
U8 = mybir.dt.uint8
AF = mybir.ActivationFunctionType
ALU = mybir.AluOpType
AX = mybir.AxisListType

H = 2048
S = 4096
B = 4
NT = 16
NCORES = 8
EPS_LN = 1e-5
EPS_B = 1e-6
NEG = -1e30

ENGINES = ("sync", "tensor", "vector", "scalar", "gpsimd")


class Sem:
    def __init__(self, nc, name):
        self.h = nc.alloc_semaphore(name=name)
        self.c = 0
        self.name = name

    def inc(self, k):
        self.c += k
        return self.c


class Rec:
    def __init__(self, nc):
        self.nc = nc
        self.ops = {e: [] for e in ENGINES}

    def emit(self, eng, builder=None, waits=(), inc=None):
        waits = [(s.h, int(v)) for (s, v) in waits if v > 0]
        if inc is not None:
            incs = inc if isinstance(inc, list) else [inc]
            for s, k in incs:
                s.inc(k)
            incs = [(s.h, int(k)) for (s, k) in incs]
        else:
            incs = []

        def fn(e, waits=waits, incs=incs, builder=builder):
            for h, v in waits:
                e.wait_ge(h, v)
            if builder is not None:
                ins = builder(e)
                for h, k in incs:
                    ins.then_inc(h, k)

        self.ops[eng].append(fn)

    def dma(self, eng, out, in_, waits=(), inc=None, **kw):
        self.emit(eng, lambda e: e.dma_start(out=out, in_=in_, **kw),
                  waits=waits, inc=inc)

    def replay(self):
        with self.nc.Block() as block:
            for eng in ENGINES:
                fns = self.ops[eng]
                getattr(block, eng)(lambda engine, fns=fns: [f(engine) for f in fns])


class Stream:

    def __init__(self, nc, name, nslots=2):
        self.sems = [Sem(nc, f"{name}{i}") for i in range(nslots)]

    def batch(self, rec, eng, slot, pairs, waits=(), **kw):
        s = self.sems[slot]
        for i, (o, inp) in enumerate(pairs):
            rec.dma(eng, o, inp, waits=(waits if i == 0 else ()), inc=(s, 16), **kw)
        return (s, s.c)


def build_program(mode="full"):
    nc = bass.Bass("TRN2", target_bir_lowering=False)
    ap = lambda t: t.ap()

    x_my = ap(nc.dram_tensor("x_my", [NT, 128, H], F32, kind="ExternalInput"))
    masks = ap(nc.dram_tensor("masks", [NT, 128, 256], F32, kind="ExternalInput"))
    qkv = ap(nc.dram_tensor("qkv", [H, 3 * H], F32, kind="ExternalInput"))
    oproj = ap(nc.dram_tensor("oproj", [H, H], F32, kind="ExternalInput"))
    gamma = ap(nc.dram_tensor("gamma", [1, H], F32, kind="ExternalInput"))
    beta = ap(nc.dram_tensor("beta", [1, H], F32, kind="ExternalInput"))
    bw_in = {k: ap(nc.dram_tensor(f"b{k}w", [H, H], F32, kind="ExternalInput"))
             for k in ("q", "k", "v", "o")}
    bb_in = {k: ap(nc.dram_tensor(f"b{k}b", [1, H], F32, kind="ExternalInput"))
             for k in ("q", "k", "v", "o")}
    gpk = ap(nc.dram_tensor("gpk", [2048, 1024], I32, kind="ExternalInput"))
    gsc = ap(nc.dram_tensor("gsc", [2048, 64], F32, kind="ExternalInput"))
    upk = ap(nc.dram_tensor("upk", [2048, 1024], I32, kind="ExternalInput"))
    usc = ap(nc.dram_tensor("usc", [2048, 64], F32, kind="ExternalInput"))
    dpk = ap(nc.dram_tensor("dpk", [2048, 1024], I32, kind="ExternalInput"))
    dsc = ap(nc.dram_tensor("dsc", [2048, 64], F32, kind="ExternalInput"))

    out_my = ap(nc.dram_tensor("out_my", [NT, 128, H], F32, kind="ExternalOutput"))

    dr = lambda name, shape, dt=BF16: ap(nc.dram_tensor(name, shape, dt, kind="Internal"))
    wg_sl = dr("wg_sl", [16 * 128, 2048])
    wu_sl = dr("wu_sl", [16 * 128, 2048])
    wd_sl = dr("wd_sl", [16 * 128, 2048])
    drs = lambda name, shape, dt=BF16: ap(nc.dram_tensor(name, shape, dt, kind="Internal", addr_space="Shared"))
    wg_all = drs("wg_all", [NCORES * 16 * 128, 2048])
    wu_all = drs("wu_all", [NCORES * 16 * 128, 2048])
    wd_all = drs("wd_all", [NCORES * 16 * 128, 2048])
    inter_sc = dr("inter_sc", [128 * 128, 2048])
    xb1 = ap(nc.dram_tensor("xb1", [NT, 128, H], F32, kind="Internal"))
    xb2 = ap(nc.dram_tensor("xb2", [NT, 128, H], F32, kind="Internal"))
    xb3 = ap(nc.dram_tensor("xb3", [NT, 128, H], F32, kind="Internal"))
    qT_sc = dr("qT_sc", [16 * 128, 2048])
    kT_sc = dr("kT_sc", [16 * 128, 2048])
    v_sc = dr("v_sc", [16 * 128, 2048])
    kT_all = dr("kT_all", [2 * 16 * 128, 2048])
    v_all = dr("v_all", [2 * 16 * 128, 2048])
    w_bf = {k: dr(f"{k}w_bf", [H, H]) for k in ("q", "k", "v", "o")}
    qwT_sc = dr("qwT_sc", [H, H])
    kwT_sc = dr("kwT_sc", [H, H])
    vwT_sc = dr("vwT_sc", [H, H])
    owT_sc = dr("owT_sc", [H, H])
    kp_sc = dr("kp_sc", [16 * 128, 2048])
    vp_sc = dr("vp_sc", [16 * 128, 2048])
    kv_part = ap(nc.dram_tensor("kv_part", [128, 2064], F32, kind="Internal"))
    kv_sum = ap(nc.dram_tensor("kv_sum", [128, 2064], F32, kind="Internal"))

    rec = Rec(nc)
    sPE = Sem(nc, "sPE")
    sV = Sem(nc, "sV")
    sSC = Sem(nc, "sSC")
    sGP = Sem(nc, "sGP")
    sCC = Sem(nc, "sCC")
    stX = Stream(nc, "stX", 2)
    stW = Stream(nc, "stW", 2)
    stI = Stream(nc, "stI", 2)
    stP = Stream(nc, "stP", 2)
    stO = Stream(nc, "stO", 2)
    stD = Stream(nc, "stD", 1)
    stS = Stream(nc, "stS", 2)
    stEvB = Stream(nc, "stEvB", 2)

    def ostores():
        return [(stO.sems[0], stO.sems[0].c), (stO.sems[1], stO.sems[1].c)]

    P = [nc.alloc_psum_tensor(f"P{i}", [128, 1024], F32) for i in range(4)]
    TRA = P[3][:, 0:64].bitcast(BF16)
    TRB = P[3][:, 512:576].bitcast(BF16)

    alloc = nc.alloc_sbuf_tensor
    ident = alloc("ident", [128, 128], BF16)
    onesf = alloc("onesf", [1, 128], F32)

    A1 = alloc("A1", [128, 65536], U8)
    A2 = alloc("A2", [128, 49152], U8)
    A3 = alloc("A3", [128, 32768], U8)
    A4 = alloc("A4", [128, 16384], U8)
    NB = {F32: 4, BF16: 2, I32: 4, U8: 1}

    def V(ar, off, n, dt):
        return ar[:, off:off + n * NB[dt]].bitcast(dt)

    dq = {
        "pk": alloc("dq_pk", [128, 1024], I32),
        "sc": alloc("dq_sc", [128, 64], F32),
        "f": alloc("dq_f", [128, 1024], F32),
        "m": alloc("dq_m", [128, 1024], F32),
        "t0": alloc("dq_t0", [128, 1024], F32),
        "t1": alloc("dq_t1", [128, 1024], F32),
        "ge": alloc("dq_ge", [128, 1024], F32),
        "l4": alloc("dq_l4", [128, 1024], U8),
        "l6": alloc("dq_l6", [128, 1024], U8),
        "vals": alloc("dq_vals", [128, 2048], BF16),
        "stage": alloc("dq_stage", [128, 2048], BF16),
    }
    dq["ci"] = dq["t1"][:].bitcast(I32)
    lnb = {
        "mu": alloc("ln_mu", [128, 1], F32),
        "ss": alloc("ln_ss", [128, 1], F32),
        "rs": alloc("ln_rs", [128, 1], F32),
        "xc": alloc("ln_xc", [128, H], F32),
        "sq": V(A4, 0, H, F32),
    }

    def emit_identity():
        rec.emit("gpsimd", lambda g: g.memset(onesf[:], 1.0))
        rec.emit("gpsimd", lambda g: g.memset(ident[:], 0.0))
        rec.emit("gpsimd", lambda g: g.affine_select(
            out=ident[:], in_=ident[:], compare_op=ALU.not_equal, fill=1.0,
            base=0, pattern=[[-1, 128]], channel_multiplier=1), inc=(sGP, 1))

    def deq_tile_emitters(which, pk_d, sc_d, sl_d):
        out = []
        state = {}

        def mk(t):
            def vec_part():
                emit_deq_vec(which, pk_d, sc_d, t, state)

            def pe_part():
                emit_deq_pe(which, sl_d, t, state)
            return vec_part, pe_part
        for t in range(16):
            out.append(mk(t))
        return out

    def emit_dequant(which, pk_d, sc_d, sl_d):
        for v, p in deq_tile_emitters(which, pk_d, sc_d, sl_d):
            v()
            p()

    def emit_deq_vec(which, pk_d, sc_d, t, state):
        if True:
            lw = stP.batch(rec, "scalar", 0,
                           [(dq["pk"][:], pk_d[t * 128:(t + 1) * 128, :]),
                            (dq["sc"][:], sc_d[t * 128:(t + 1) * 128, :])],
                           waits=[(sV, sV.c), (sPE, sPE.c)])
            v = "vector"
            rec.emit(v, waits=[lw])
            for pl in (0, 1):
                if pl == 0:
                    rec.emit(v, lambda e: e.tensor_single_scalar(dq["ci"][:], dq["pk"][:], 15, ALU.bitwise_and))
                else:
                    rec.emit(v, lambda e: e.tensor_single_scalar(dq["ci"][:], dq["pk"][:], 4, ALU.logical_shift_right))
                    rec.emit(v, lambda e: e.tensor_single_scalar(dq["ci"][:], dq["ci"][:], 15, ALU.bitwise_and))
                rec.emit(v, lambda e: e.tensor_copy(dq["f"][:], dq["ci"][:]))
                rec.emit(v, lambda e: e.tensor_single_scalar(dq["ge"][:], dq["f"][:], 8.0, ALU.is_ge))
                rec.emit(v, lambda e: e.scalar_tensor_tensor(dq["m"][:], dq["ge"][:], -8.0, dq["f"][:], ALU.mult, ALU.add))
                rec.emit(v, lambda e: e.tensor_single_scalar(dq["l4"][:], dq["m"][:], 4.0, ALU.is_lt))
                rec.emit(v, lambda e: e.tensor_single_scalar(dq["l6"][:], dq["m"][:], 6.0, ALU.is_lt))
                rec.emit(v, lambda e: e.tensor_scalar(dq["t0"][:], dq["m"][:], 2.0, -8.0, ALU.mult, ALU.add))
                rec.emit(v, lambda e: e.tensor_scalar_sub(dq["t1"][:], dq["m"][:], 2.0))
                rec.emit(v, lambda e: e.select(dq["t0"][:], dq["l6"][:], dq["t1"][:], dq["t0"][:]))
                rec.emit(v, lambda e: e.tensor_scalar_mul(dq["t1"][:], dq["m"][:], 0.5))
                rec.emit(v, lambda e: e.select(dq["t0"][:], dq["l4"][:], dq["t1"][:], dq["t0"][:]))
                rec.emit(v, lambda e: e.tensor_scalar(dq["ge"][:], dq["ge"][:], -2.0, 1.0, ALU.mult, ALU.add))
                rec.emit(v, lambda e: e.tensor_mul(dq["t0"][:], dq["t0"][:], dq["ge"][:]))
                o3 = dq["vals"][:, pl::2].rearrange("p (b j) -> p b j", j=16)
                i3 = dq["t0"][:].rearrange("p (b j) -> p b j", j=16)
                s3 = dq["sc"][:, :, None].broadcast_to([128, 64, 16])
                rec.emit(v, lambda e, o3=o3, i3=i3, s3=s3: e.tensor_tensor(o3, i3, s3, ALU.mult),
                         inc=(sV, 1))
            state[t] = sV.c

    def emit_deq_pe(which, sl_d, t, state):
        if True:
            v_done = state[t]
            for c in range(16):
                tr = TRA if c % 2 == 0 else TRB
                rec.emit("tensor", lambda e, c=c, tr=tr: e.transpose(
                    tr, dq["vals"][:, c * 128:(c + 1) * 128], ident[:]),
                    waits=[(sV, v_done), (sGP, 1), (sSC, sSC.c)], inc=(sPE, 1))
                pe_now = sPE.c
                rec.emit("scalar", lambda e, c=c, tr=tr: e.copy(dq["stage"][:, c * 128:(c + 1) * 128], tr),
                         waits=[(sPE, pe_now)], inc=(sSC, 1))
            sc_done = sSC.c
            if which in ("g", "u"):
                stw = stD.batch(rec, "sync", 0, [(sl_d[t * 128:(t + 1) * 128, :], dq["stage"][:])],
                                waits=[(sSC, sc_done)])
            else:
                stw = stD.batch(rec, "sync", 0,
                                [(sl_d[c * 128:(c + 1) * 128, t * 128:(t + 1) * 128],
                                  dq["stage"][:, c * 128:(c + 1) * 128]) for c in range(16)],
                                waits=[(sSC, sc_done)])
            rec.emit("scalar", waits=[stw])

    def emit_deq_gather(sl_d, all_d):
        sD = stD.sems[0]
        rec.emit("gpsimd", lambda g, sl_d=sl_d, all_d=all_d: g.collective_compute(
            "AllGather", ALU.bypass, replica_groups=[list(range(NCORES))],
            ins=[sl_d], outs=[all_d]),
            waits=[(sD, sD.c)], inc=(sCC, 1))
        return sCC.c

    def emit_ln(x_sb, xn_out, stats_to=None, eps=EPS_LN, xc=None):
        b = lnb
        xc = xc if xc is not None else b["xc"]
        rec.emit("vector", lambda e: e.reduce_sum(b["ss"][:], x_sb[:], axis=AX.X), inc=(sV, 1))
        rec.emit("vector", lambda e: e.tensor_scalar_mul(b["mu"][:], b["ss"][:], 1.0 / H),
                 waits=[(sV, sV.c)], inc=(sV, 1))
        rec.emit("vector", lambda e, xc=xc: e.tensor_scalar(xc[:], x_sb[:], b["mu"][:, 0:1], None, ALU.subtract),
                 waits=[(sV, sV.c)], inc=(sV, 1))
        rec.emit("scalar", lambda e, xc=xc: e.activation(b["sq"][:], xc[:], AF.Square, accum_out=b["ss"][:]),
                 waits=[(sV, sV.c)], inc=(sSC, 1))
        rec.emit("vector", lambda e, eps=eps: e.tensor_scalar(b["ss"][:], b["ss"][:], 1.0 / H, float(eps), ALU.mult, ALU.add),
                 waits=[(sSC, sSC.c)], inc=(sV, 1))
        rec.emit("scalar", lambda e: e.activation(b["ss"][:], b["ss"][:], AF.Ln),
                 waits=[(sV, sV.c)], inc=(sSC, 1))
        rec.emit("vector", lambda e: e.tensor_scalar_mul(b["ss"][:], b["ss"][:], -0.5),
                 waits=[(sSC, sSC.c)], inc=(sV, 1))
        rec.emit("scalar", lambda e: e.activation(b["rs"][:], b["ss"][:], AF.Exp),
                 waits=[(sV, sV.c)], inc=(sSC, 1))
        rec.emit("vector", lambda e, xc=xc: e.tensor_scalar(xn_out[:], xc[:], b["rs"][:, 0:1], None, ALU.mult),
                 waits=[(sSC, sSC.c)], inc=(sV, 1))
        if stats_to is not None:
            rec.emit("vector", lambda e: e.tensor_scalar_add(stats_to[:, 0:1], b["rs"][:], 1.0))
            rec.emit("vector", lambda e: e.tensor_tensor(stats_to[:, 1:2], b["mu"][:], b["rs"][:], ALU.mult))
            rec.emit("vector", lambda e: e.tensor_scalar_mul(stats_to[:, 1:2], stats_to[:, 1:2], -1.0), inc=(sV, 1))

    ff = {
        "x": V(A3, 0, H, F32),
        "xn": V(A3, 8192, H, BF16),
        "xnT": V(A1, 0, 16 * H, BF16),
        "wg": [V(A2, 0, 2048, BF16), V(A2, 4096, 2048, BF16)],
        "wu": [V(A2, 8192, 2048, BF16), V(A2, 12288, 2048, BF16)],
        "gl": V(A2, 16384, 1024, F32),
        "int": [V(A2, 20480, 2048, BF16), V(A2, 24576, 2048, BF16)],
    "ic": [V(A2, 28672, 1024, BF16), V(A2, 30720, 1024, BF16)],
        "wd": [V(A2, 32768, 512, BF16), V(A2, 33792, 512, BF16)],
        "xp": [V(A2, 34816, 512, F32), V(A2, 36864, 512, F32)],
        "op": [V(A2, 38912, 512, F32), V(A2, 40960, 512, F32)],
        "ic4": [V(A2, 0, 4096, BF16), V(A2, 8192, 4096, BF16)],
        "wd4": [V(A2, 16384, 2048, BF16), V(A2, 20480, 2048, BF16)],
    }

    aa = {
        "xn1": V(A3, 0, H, F32),
        "xng": V(A3, 8192, H, BF16),
        "grep": V(A3, 12288, H, F32),
        "brep": V(A3, 20480, H, F32),
        "lnst": alloc("aa_lnst", [128, 2 * NT], F32),
        "qkvb": [V(A2, 16384, 2048, BF16), V(A2, 20480, 2048, BF16)],
        "vwb": [V(A2, 0, 16 * 512, BF16), V(A2, 16384 - 8192 + 0 * 8192 + 8192, 0, BF16)],
        "ev": [alloc("aa_ev0", [128, 512], BF16), alloc("aa_ev1", [128, 512], BF16)],
        "qsb": [V(A2, 40960, 16 * 128, BF16), V(A2, 45056, 16 * 128, BF16)],
        "ksb": [V(A2, 0, 16 * 256, BF16), V(A2, 8192, 16 * 256, BF16)],
        "vsb": [V(A2, 16384, 2 * 2048, BF16), V(A2, 24576, 2 * 2048, BF16)],
        "msk": V(A2, 32768, NT * 256, BF16),
        "srow": V(A3, 0, 4096, F32),
        "prow": V(A4, 0, 4096, BF16),
        "nmax": alloc("aa_nmax", [128, 1], F32),
        "z": alloc("aa_z", [128, 1], F32),
        "rz": alloc("aa_rz", [128, 1], F32),
        "pt": [alloc("aa_pt0", [128, 128], BF16), alloc("aa_pt1", [128, 128], BF16)],
        "pvn": V(A4, 8192, 2048, BF16),
        "pvt": V(A4, 12288, 16 * 128, BF16),
        "opb": V(A1, 0, 16 * 2048, BF16),
        "xf": V(A3, 16384, H, F32),
        "o1": V(A3, 24576, H, F32),
    }
    aa["vwb"] = [V(A2, 0, 16 * 512, BF16), V(A2, 8192, 16 * 512, BF16)]
    aa["xn2"] = aa["xn1"]

    def emit_attna(src, dst, par_dummy=None, adbg=None, weave=None):
        wv = list(weave) if weave else []
        wstate = {"vq": [], "k": 0}

        def weave_step():
            if wstate["k"] < len(wv):
                wv[wstate["k"]][0]()
                wstate["vq"].append(wv[wstate["k"]][1])
                wstate["k"] += 1
            if len(wstate["vq"]) > 1:
                wstate["vq"].pop(0)()

        def weave_flush():
            while wstate["k"] < len(wv):
                weave_step()
            while wstate["vq"]:
                wstate["vq"].pop(0)()
        xngT = ff["xnT"]
        lw_g = stD.batch(rec, "sync", 0, [(aa["grep"][:], gamma.partition_broadcast(128))])
        lw_g = stD.batch(rec, "sync", 0, [(aa["brep"][:], beta.partition_broadcast(128))])
        stMK = Stream(nc, "stMK", 1)
        lw_mk = stMK.batch(rec, "gpsimd", 0, [(aa["msk"][:].rearrange("p (i c) -> p i c", c=256), masks.transpose([1, 0, 2]))])
        stOP = Stream(nc, "stOP", 1)

        for i in range(NT):
            weave_step()
            lw = stX.batch(rec, "sync", i % 2, [(ff["x"][:], src[i])],
                           waits=[(sV, sV.c), (sSC, sSC.c)])
            rec.emit("vector", waits=[lw, lw_g])
            emit_ln(ff["x"], aa["xn1"], stats_to=aa["lnst"][:, 2 * i:2 * i + 2])
            emit_ln(aa["xn1"], aa["xn1"])
            rec.emit("vector", lambda e: e.tensor_mul(aa["xn1"][:], aa["xn1"][:], aa["grep"][:]))
            rec.emit("vector", lambda e: e.tensor_add(aa["xng"][:], aa["xn1"][:], aa["brep"][:]), inc=(sV, 1))
            vr = sV.c
            for c in range(16):
                tr = TRA if c % 2 == 0 else TRB
                rec.emit("tensor", lambda e, c=c, tr=tr: e.transpose(
                    tr, aa["xng"][:, c * 128:(c + 1) * 128], ident[:]),
                    waits=[(sV, vr), (sGP, 1), (sSC, sSC.c)], inc=(sPE, 1))
                pe = sPE.c
                rec.emit("scalar", lambda e, c=c, i=i, tr=tr: e.copy(
                    xngT[:, c * H + i * 128: c * H + (i + 1) * 128], tr),
                    waits=[(sPE, pe)], inc=(sSC, 1))
        xngT_sc = sSC.c

        qkT_sc = {0: kT_sc, 1: qT_sc}
        stEv = Stream(nc, "stEv", 2)
        ev_guard = [(sSC, 0), (sSC, 0)]
        pair_groups = [[0, 1], [2, 3], [4, 5], [6, 7]]

        def qk_half(which):
            col_base = (1 - which) * 2048
            for dch16 in range(16):
                weave_step()
                p = dch16 % 2
                src_cols = qkv[:, dch16 * 128 + col_base:][:, 0:128]
                lw = stW.batch(rec, "gpsimd", p,
                               [(aa["qkvb"][p][:].rearrange("p (c d) -> p c d", d=128),
                                 src_cols.rearrange("(c p) d -> p c d", p=128))],
                               waits=[(sPE, sPE.c)])
                sc_pre = sSC.c
                for tch in range(4):
                    bank = P[p] if tch < 2 else P[2 + p]
                    reg = bank[:, (tch % 2) * 512:(tch % 2) * 512 + 512]
                    w = [lw, (sSC, sc_pre), ev_guard[p]]
                    rec.emit("tensor", waits=w)
                    for c in range(16):
                        rec.emit("tensor", lambda e, p=p, c=c, tch=tch, reg=reg: e.matmul(
                            reg, aa["qkvb"][p][:, c * 128:(c + 1) * 128],
                            xngT[:, c * H + tch * 512: c * H + (tch + 1) * 512],
                            start=(c == 0), stop=(c == 15)),
                            inc=(sPE, 1) if c == 15 else None)
                    pe = sPE.c
                    ep = tch % 2
                    rec.emit("scalar", lambda e, reg=reg, ep=ep: e.copy(aa["ev"][ep][:], reg),
                             waits=[(sPE, pe), (stEv.sems[ep], stEv.sems[ep].c)], inc=(sSC, 1))
                    stEv.batch(rec, "sync", ep,
                               [(qkT_sc[which][dch16 * 128:(dch16 + 1) * 128, tch * 512:(tch + 1) * 512],
                                 aa["ev"][ep][:])],
                               waits=[(sSC, sSC.c)])
                ev_guard[p] = (sSC, sSC.c)

        qk_half(0)
        AG_EARLY = False
        if AG_EARLY:
            evk = [(sm, sm.c) for sm in stEv.sems]
            rec.emit("gpsimd", lambda g: g.collective_compute(
                "AllGather", ALU.bypass, replica_groups=pair_groups,
                ins=[kT_sc], outs=[kT_all]), waits=evk, inc=(sCC, 1))
            agk_cc = sCC.c

        for half in range(2):
            for dch4 in range(4):
                weave_step()
                p = dch4 % 2
                src_cols = qkv[:, 2 * H + dch4 * 512: 2 * H + (dch4 + 1) * 512]
                lw = stW.batch(rec, "gpsimd", p,
                               [(aa["vwb"][p][:].rearrange("p (c d) -> p c d", d=512),
                                 src_cols.rearrange("(c p) d -> p c d", p=128))],
                               waits=[(sPE, sPE.c)])
                rec.emit("tensor", waits=[lw, (sV, sV.c), (sSC, sSC.c)])
                for c in range(16):
                    for t in range(8):
                        rec.emit("tensor", lambda e, p=p, c=c, t=t, half=half: e.matmul(
                            P[t // 2][:, (t % 2) * 512:(t % 2) * 512 + 512],
                            xngT[:, c * H + half * 1024 + t * 128: c * H + half * 1024 + (t + 1) * 128],
                            aa["vwb"][p][:, c * 512:(c + 1) * 512],
                            start=(c == 0), stop=(c == 15)),
                            inc=(sPE, 1) if (c == 15 and t == 7) else None)
                pe = sPE.c
                for t in range(8):
                    ep = t % 2
                    rec.emit("scalar", lambda e, t=t, ep=ep: e.copy(
                        aa["ev"][ep][:], P[t // 2][:, (t % 2) * 512:(t % 2) * 512 + 512]),
                        waits=[(sPE, pe), (stEv.sems[ep], stEv.sems[ep].c)], inc=(sSC, 1))
                    stEv.batch(rec, "sync", ep,
                               [(v_sc[(half * 8 + t) * 128:(half * 8 + t + 1) * 128, dch4 * 512:(dch4 + 1) * 512],
                                 aa["ev"][ep][:])],
                               waits=[(sSC, sSC.c)])

        if AG_EARLY:
            evv = [(sm, sm.c) for sm in stEv.sems]
            rec.emit("gpsimd", lambda g: g.collective_compute(
                "AllGather", ALU.bypass, replica_groups=pair_groups,
                ins=[v_sc], outs=[v_all]), waits=evv, inc=(sCC, 1))
            agv_cc = sCC.c

        ev_guard[0] = (sSC, sSC.c)
        ev_guard[1] = (sSC, sSC.c)
        qk_half(1)
        weave_flush()
        qst = [(sm, sm.c) for sm in stEv.sems]
        if not AG_EARLY:
            rec.emit("gpsimd", lambda g: g.collective_compute(
                "AllGather", ALU.bypass, replica_groups=pair_groups,
                ins=[kT_sc], outs=[kT_all]), waits=qst, inc=(sCC, 1))
            agk_cc = sCC.c
            rec.emit("gpsimd", lambda g: g.collective_compute(
                "AllGather", ALU.bypass, replica_groups=pair_groups,
                ins=[v_sc], outs=[v_all]), inc=(sCC, 1))
            agv_cc = sCC.c

        opw = stOP.batch(rec, "gpsimd", 0,
                         [(aa["opb"][:, c * 2048:(c + 1) * 2048],
                           oproj[c * 128:(c + 1) * 128, :]) for c in range(16)],
                         waits=[(sPE, sPE.c)])

        if adbg is not None:
            rec.dma("sync", adbg["qT"], qT_sc, waits=[(sCC, agv_cc)] + qst, inc=(stO.sems[0], 16))
            rec.dma("sync", adbg["kTa"], kT_all, inc=(stO.sems[0], 16))
            rec.dma("sync", adbg["va"], v_all, inc=(stO.sems[0], 16))

        kT_r = kT_all.rearrange("(r c p) t -> r c p t", r=2, p=128)
        v_r = v_all.rearrange("(r i p) d -> r i p d", r=2, p=128)

        stQT = Stream(nc, "stQT", 2)
        stKP = Stream(nc, "stKP", 2)
        stVP = Stream(nc, "stVP", 2)
        S_reg = [P[0][:, 0:256], P[0][:, 512:768]]
        prow_s = [aa["prow"], lnb["xc"][:].bitcast(BF16)]
        nmax_s = [aa["nmax"], alloc("aa_nmax1", [128, 1], F32)]
        z_s = [aa["z"], alloc("aa_z1", [128, 1], F32)]
        rz_s = [aa["rz"], alloc("aa_rz1", [128, 1], F32)]
        st = {"s_guard": [(sSC, sSC.c), (sSC, sSC.c)], "o1_prev": (sV, 0),
              "q_guard": [(sPE, sPE.c), (sPE, sPE.c)], "k_guard": [(sPE, sPE.c), (sPE, sPE.c)],
              "v_guard": [(sPE, sPE.c), (sPE, sPE.c)], "pt_done": [(sPE, 0), (sPE, 0)],
              "exp_sc": (sSC, 0), "lq": {}, "rz_v": {}, "exp_of": {}}

        def aa_pass1(i):
            ip = i % 2
            npan = i + 1
            st["lq"][i] = stQT.batch(rec, "sync", ip,
                            [(aa["qsb"][ip][:].rearrange("p (c t) -> p c t", t=128),
                              qT_sc.rearrange("(c p) t -> p c t", p=128)[:, :, i * 128:(i + 1) * 128])],
                            waits=[st["q_guard"][ip]] + qst)
            for pan in range(npan):
                pp = pan % 2
                prs = []
                for gsub in range(2):
                    gc = 2 * pan + gsub
                    rr = gc % 2
                    lt = gc // 2
                    prs.append((aa["ksb"][pp][:].rearrange("p (c t) -> p c t", t=256)[:, :, gsub * 128:(gsub + 1) * 128],
                                kT_r[rr, :, :, lt * 128:(lt + 1) * 128].transpose([1, 0, 2])))
                lk = stKP.batch(rec, "sync", pp, prs, waits=[st["k_guard"][pp], (sCC, agv_cc)])
                rec.emit("tensor", waits=[st["lq"][i], lk, st["s_guard"][pp]])
                for c in range(16):
                    rec.emit("tensor", lambda e, ip=ip, pp=pp, c=c: e.matmul(
                        S_reg[pp], aa["qsb"][ip][:, c * 128:(c + 1) * 128],
                        aa["ksb"][pp][:, c * 256:(c + 1) * 256],
                        start=(c == 0), stop=(c == 15)),
                        inc=(sPE, 1) if c == 15 else None)
                st["k_guard"][pp] = (sPE, sPE.c)
                pe = sPE.c
                w = [(sPE, pe)]
                if pan == 0:
                    w.append(st["exp_sc"])
                if pan == npan - 1:
                    rec.emit("vector", lambda e, i=i, pan=pan, pp=pp: e.scalar_tensor_tensor(
                        aa["srow"][:, pan * 256:(pan + 1) * 256], S_reg[pp], 1.0 / 45.254833995939045,
                        aa["msk"][:, i * 256:(i + 1) * 256], ALU.mult, ALU.add),
                        waits=w + [lw_mk], inc=(sV, 1))
                else:
                    rec.emit("vector", lambda e, pan=pan, pp=pp: e.tensor_scalar_mul(
                        aa["srow"][:, pan * 256:(pan + 1) * 256], S_reg[pp], 1.0 / 45.254833995939045),
                        waits=w, inc=(sV, 1))
                st["s_guard"][pp] = (sV, sV.c)

        def aa_softmax(i):
            ip = i % 2
            npan = i + 1
            rec.emit("vector", lambda e, i=i, npan=npan, ip=ip: e.reduce_max(
                nmax_s[ip][:], aa["srow"][:, 0:npan * 256], axis=AX.X, negate=True),
                waits=[(sV, sV.c)], inc=(sV, 1))
            rec.emit("scalar", lambda e, npan=npan, ip=ip: e.activation(
                prow_s[ip][:, 0:npan * 256], aa["srow"][:, 0:npan * 256], AF.Exp,
                bias=nmax_s[ip][:, 0:1], accum_out=z_s[ip][:]),
                waits=[(sV, sV.c), st["pt_done"][ip]], inc=(sSC, 1))
            st["exp_sc"] = (sSC, sSC.c)
            st["exp_of"][i] = (sSC, sSC.c)
            rec.emit("vector", lambda e, ip=ip: e.reciprocal(rz_s[ip][:], z_s[ip][:]),
                     waits=[(sSC, sSC.c)], inc=(sV, 1))
            st["rz_v"][i] = sV.c
            if adbg is not None and i == 3:
                rec.dma("sync", adbg["srow"], aa["srow"][:], waits=[(sV, sV.c)], inc=(stO.sems[0], 16))
                rec.dma("sync", adbg["prow"], prow_s[ip][:], inc=(stO.sems[0], 16))
                rec.dma("sync", adbg["z"], z_s[ip][:], inc=(stO.sems[0], 16))
                rec.emit("scalar", waits=[(stO.sems[0], stO.sems[0].c)])
                rec.emit("vector", waits=[(stO.sems[0], stO.sems[0].c)])

        def aa_pass2(i):
            ip = i % 2
            npan = i + 1
            for pan in range(npan):
                pp = pan % 2
                prs = []
                for gsub in range(2):
                    gc = 2 * pan + gsub
                    rr = gc % 2
                    lt = gc // 2
                    prs.append((aa["vsb"][pp][:, gsub * 2048:(gsub + 1) * 2048], v_r[rr, lt]))
                lv = stVP.batch(rec, "sync", pp, prs, waits=[st["v_guard"][pp], (sCC, agv_cc)])
                for gsub in range(2):
                    b = 2 * pan + gsub
                    tp = b % 2
                    tr = TRA if tp == 0 else TRB
                    rec.emit("tensor", lambda e, b=b, tr=tr, ip=ip: e.transpose(
                        tr, prow_s[ip][:, b * 128:(b + 1) * 128], ident[:]),
                        waits=[st["exp_of"][i], (sSC, sSC.c), (sGP, 1)], inc=(sPE, 1))
                    pe = sPE.c
                    rec.emit("scalar", lambda e, tp=tp, tr=tr: e.copy(aa["pt"][tp][:], tr),
                             waits=[(sPE, pe)], inc=(sSC, 1))
                    ptsc = sSC.c
                    w = [lv, (sSC, ptsc)]
                    if b == 0:
                        w.append(st["o1_prev"])
                    rec.emit("tensor", waits=w)
                    for d4 in range(4):
                        rec.emit("tensor", lambda e, tp=tp, gsub=gsub, d4=d4, b=b, npan=npan: e.matmul(
                            P[1 + d4 // 2][:, (d4 % 2) * 512:(d4 % 2) * 512 + 512],
                            aa["pt"][tp][:],
                            aa["vsb"][(b // 2) % 2][:, gsub * 2048 + d4 * 512: gsub * 2048 + (d4 + 1) * 512],
                            start=(b == 0), stop=(b == 2 * npan - 1)),
                            inc=(sPE, 1) if d4 == 3 else None)
                    st["v_guard"][pp] = (sPE, sPE.c)
            st["pt_done"][ip] = (sPE, sPE.c)
            pv_pe = sPE.c
            rec.emit("vector", lambda e, ip=ip: e.tensor_scalar(
                aa["pvn"][:, 0:1024], P[1][:], rz_s[ip][:, 0:1], None, ALU.mult),
                waits=[(sPE, pv_pe), (sV, st["rz_v"][i])], inc=(sV, 1))
            rec.emit("vector", lambda e, ip=ip: e.tensor_scalar(
                aa["pvn"][:, 1024:2048], P[2][:], rz_s[ip][:, 0:1], None, ALU.mult),
                inc=(sV, 1))
            pvn_v = sV.c
            for c in range(16):
                tr = TRA if c % 2 == 0 else TRB
                rec.emit("tensor", lambda e, c=c, tr=tr: e.transpose(
                    tr, aa["pvn"][:, c * 128:(c + 1) * 128], ident[:]),
                    waits=[(sV, pvn_v), (sSC, sSC.c)], inc=(sPE, 1))
                pe = sPE.c
                rec.emit("scalar", lambda e, c=c, tr=tr: e.copy(
                    aa["pvt"][:, c * 128:(c + 1) * 128], tr),
                    waits=[(sPE, pe)], inc=(sSC, 1))
            pvt_sc = sSC.c
            lx = stX.batch(rec, "sync", ip, [(aa["xf"][:], src[i])], waits=[(sV, sV.c)])
            rec.emit("tensor", waits=[(sSC, pvt_sc), opw])
            for oc in range(4):
                for c in range(16):
                    rec.emit("tensor", lambda e, oc=oc, c=c: e.matmul(
                        P[1 + oc // 2][:, (oc % 2) * 512:(oc % 2) * 512 + 512],
                        aa["pvt"][:, c * 128:(c + 1) * 128],
                        aa["opb"][:, c * 2048 + oc * 512: c * 2048 + (oc + 1) * 512],
                        start=(c == 0), stop=(c == 15)),
                        inc=(sPE, 1) if c == 15 else None)
            o_pe = sPE.c
            rec.emit("vector", lambda e, i=i: e.tensor_scalar(
                aa["o1"][:], aa["xf"][:], aa["lnst"][:, 2 * i:2 * i + 1], aa["lnst"][:, 2 * i + 1:2 * i + 2],
                ALU.mult, ALU.add),
                waits=[lx, (sV, sV.c), (stO.sems[0], stO.sems[0].c), (stO.sems[1], stO.sems[1].c)],
                inc=(sV, 1))
            rec.emit("vector", lambda e: e.tensor_add(aa["o1"][:, 0:1024], aa["o1"][:, 0:1024], P[1][:]),
                     waits=[(sPE, o_pe)], inc=(sV, 1))
            rec.emit("vector", lambda e: e.tensor_add(aa["o1"][:, 1024:2048], aa["o1"][:, 1024:2048], P[2][:]),
                     inc=(sV, 1))
            st["o1_prev"] = (sV, sV.c)
            st["q_guard"][ip] = (sPE, o_pe)
            ow = stO.sems[ip]
            rec.emit("vector", waits=[(ow, ow.c)])
            stO.batch(rec, "sync", ip, [(dst[i], aa["o1"][:])], waits=[(sV, sV.c)])

        import os as _os
        if _os.environ.get("AA_PIPE") != "1":
            for i in range(NT):
                aa_pass1(i)
                aa_softmax(i)
                aa_pass2(i)
        else:
            for i in range(NT):
                aa_pass1(i)
                if i >= 1:
                    aa_pass2(i - 1)
                aa_softmax(i)
            aa_pass2(NT - 1)

    ab = {
        "xn3": V(A3, 8192, H, BF16),
        "kbr": V(A3, 12288, H, F32),
        "vbr": V(A3, 20480, H, F32),
        "obr": V(A3, 12288, H, F32),
        "x": V(A3, 0, H, F32),
        "o1": V(A3, 20480, H, F32),
        "wc": [V(A2, 0, 512, BF16), V(A2, 1024, 512, BF16)],
        "pb": [V(A2, 0, H, BF16), V(A2, 4096, H, BF16)],
        "kpt": V(A2, 4096, H, BF16),
        "vpt": V(A2, 8192, H, BF16),
        "kvacc": V(A2, 12288, 2064, F32),
        "kvsb": V(A2, 20736, 2064, BF16),
        "qwc": [V(A2, 25088, 512, BF16), V(A2, 26112, 512, BF16)],
        "owc": [V(A2, 27136, 512, BF16), V(A2, 28160, 512, BF16)],
        "rzb": V(A2, 29184, 512, F32),
        "qpT": V(A2, 31232, 16 * 512, BF16),
        "atn": [V(A4, 8192, 512, BF16), V(A4, 9216, 512, BF16)],
        "el0": V(A4, 10240, 512, F32),
        "el1": V(A4, 12288, 512, F32),
        "msk8": V(A4, 14336, 512, U8),
        "qbc": alloc("ab_qbc", [128, 16], F32),
        "ones": alloc("ab_ones", [128, 1], BF16),
        "wc4": [V(A2, 0, 2048, BF16), V(A2, 4096, 2048, BF16)],
        "tw": [V(A2, 0, 2048, BF16), V(A2, 4096, 2048, BF16)],
    }

    pst_o = [[[P[(tt * 4 + eo) // 2][:, ((tt * 4 + eo) % 2) * 512:((tt * 4 + eo) % 2) * 512 + 512]
               for eo in range(4)] for tt in range(2)] for _ttp in range(2)]

    abp = {
        "pb": [V(A3, 12288, H, BF16), V(A3, 16384, H, BF16)],
        "tw": [V(A3, 20480, H, BF16), V(A3, 24576, H, BF16)],
    }
    wT_sc = {"q": qwT_sc, "k": kwT_sc, "v": vwT_sc, "o": owT_sc}
    abprep = {"units": [], "state": None}

    def mk_attnb_prep_units(entry):
        stWc = Stream(nc, "stWc", 2)
        stBc = Stream(nc, "stBc", 2)
        guard = [(stBc.sems[0], 0), (stBc.sems[1], 0)]
        cast_done = {}
        for mi, mk4 in enumerate(("q", "k", "v", "o")):
            for st_ in range(16):
                p = st_ % 2
                w = [guard[p]] + (entry if (mi == 0 and st_ < 2) else [])
                lw = stWc.batch(rec, "gpsimd", p,
                                [(abp["pb"][p][:], bw_in[mk4][st_ * 128:(st_ + 1) * 128, :])],
                                waits=w)
                guard[p] = stBc.batch(rec, "gpsimd", p,
                                      [(w_bf[mk4][st_ * 128:(st_ + 1) * 128, :], abp["pb"][p][:])],
                                      waits=[lw])
            cast_done[mk4] = [guard[0], guard[1]]
        stTW = Stream(nc, "stTW", 2)
        stTS = Stream(nc, "stTS", 2)
        state = {"tw_guard": [(stTS.sems[0], 0), (stTS.sems[1], 0)]}

        def mk1(mk4, c):
            def fn():
                p = c % 2
                w = [state["tw_guard"][p]] + (cast_done[mk4] if c < 2 else []) \
                    + (entry if (mk4 == "q" and c < 2) else [])
                lw = stTW.batch(rec, "scalar", p,
                                [(abp["tw"][p][:], w_bf[mk4][:, c * 128:(c + 1) * 128])],
                                waits=w, transpose=True)
                state["tw_guard"][p] = stTS.batch(rec, "sync", p,
                                                  [(wT_sc[mk4][c * 128:(c + 1) * 128, :], abp["tw"][p][:])],
                                                  waits=[lw])
            return fn

        abprep["units"] = [mk1(mk4, c) for mk4 in ("q", "k", "v", "o") for c in range(16)]
        abprep["state"] = state

    def drain_prep(n=1):
        while n > 0 and abprep["units"]:
            abprep["units"].pop(0)()
            n -= 1

    def emit_attnb(src, dst):
        stB = Stream(nc, "stB", 2)
        stKV = Stream(nc, "stKV", 2)
        xn3T = ff["xnT"]
        if abprep["state"] is None:
            mk_attnb_prep_units([(sPE, sPE.c), (sV, sV.c)] + ostores())
        drain_prep(len(abprep["units"]))
        wt_done = list(abprep["state"]["tw_guard"])
        lw_b = stD.batch(rec, "sync", 0, [(ab["kbr"][:], bb_in["k"].partition_broadcast(128))],
                         waits=wt_done)
        lw_b = stD.batch(rec, "sync", 0, [(ab["vbr"][:], bb_in["v"].partition_broadcast(128))])
        rec.dma("gpsimd", ab["qbc"][:],
                bb_in["q"].rearrange("o (c p) -> o c p", p=128).transpose([0, 2, 1])[0],
                inc=(sGP, 16), allow_slow_non_contiguous=True)
        qbc_gp = sGP.c
        rec.emit("gpsimd", lambda g: g.memset(ab["ones"][:], 1.0), inc=(sGP, 1))
        ones_gp = sGP.c
        for i in range(NT):
            w = [(sV, sV.c), (sSC, sSC.c)] + (ostores() if i == 0 else [])
            lw = stX.batch(rec, "sync", i % 2, [(ab["x"][:], src[i])], waits=w)
            rec.emit("vector", waits=[lw])
            emit_ln(ab["x"], ab["xn3"])
            vr = sV.c
            for c in range(16):
                tr = TRA if c % 2 == 0 else TRB
                rec.emit("tensor", lambda e, c=c, tr=tr: e.transpose(
                    tr, ab["xn3"][:, c * 128:(c + 1) * 128], ident[:]),
                    waits=[(sV, vr), (sGP, 1), (sSC, sSC.c)], inc=(sPE, 1))
                pe = sPE.c
                rec.emit("scalar", lambda e, c=c, i=i, tr=tr: e.copy(
                    xn3T[:, c * H + i * 128: c * H + (i + 1) * 128], tr),
                    waits=[(sPE, pe)], inc=(sSC, 1))
        xn3T_sc = sSC.c

        pst = [P[t // 2][:, (t % 2) * 512:(t % 2 + 1) * 512] for t in range(8)]
        first_bd = True
        for mk, biasr, do_elu, dstp in (("k", ab["kbr"], True, kp_sc), ("v", ab["vbr"], False, vp_sc)):
            for half in range(2):
                for dch4 in range(4):
                    pe_base = sPE.c
                    for cb in range(4):
                        p = cb % 2
                        if cb == 0:
                            w = [(sPE, pe_base)]
                            if first_bd:
                                w += [wt_done[0], wt_done[1], (sSC, xn3T_sc)]
                        else:
                            w = [(sPE, pe_base + 4 * (cb - 1))] if cb >= 2 else [(sPE, pe_base)]
                        lw = stI.batch(rec, "scalar", p,
                                       [(ab["wc4"][p][:, cc * 512:(cc + 1) * 512],
                                         wT_sc[mk][(cb * 4 + cc) * 128:(cb * 4 + cc + 1) * 128, dch4 * 512:(dch4 + 1) * 512])
                                        for cc in range(4)],
                                       waits=w)
                        w2 = [lw]
                        if cb == 0:
                            w2.append((sV, sV.c))
                        rec.emit("tensor", waits=w2)
                        for cc in range(4):
                            c = cb * 4 + cc
                            for t in range(8):
                                rec.emit("tensor", lambda e, p=p, c=c, cc=cc, t=t, half=half: e.matmul(
                                    pst[t], xn3T[:, c * H + half * 1024 + t * 128: c * H + half * 1024 + (t + 1) * 128],
                                    ab["wc4"][p][:, cc * 512:(cc + 1) * 512],
                                    start=(c == 0), stop=(c == 15)),
                                    inc=(sPE, 1) if t == 7 else None)
                    first_bd = False
                    pe_done = sPE.c
                    for t in range(8):
                        ep = t % 2
                        sEv = stEvB.sems[ep]
                        rec.emit("vector", lambda e, t=t, ep=ep, biasr=biasr, dch4=dch4: e.tensor_tensor(
                            ab["el0"][:] if False else ab["el0"][:], pst[t], biasr[:, dch4 * 512:(dch4 + 1) * 512], ALU.add),
                            waits=[(sPE, pe_done), lw_b, (sEv, sEv.c), (sV, sV.c)], inc=(sV, 1))
                        if do_elu:
                            rec.emit("vector", lambda e: e.tensor_single_scalar(ab["msk8"][:], ab["el0"][:], 0.0, ALU.is_gt))
                            rec.emit("scalar", lambda e: e.activation(ab["el1"][:], ab["el0"][:], AF.Exp),
                                     waits=[(sV, sV.c)], inc=(sSC, 1))
                            rec.emit("vector", lambda e: e.tensor_scalar_add(ab["el0"][:], ab["el0"][:], 1.0),
                                     waits=[(sSC, sSC.c)])
                            rec.emit("vector", lambda e, ep=ep: e.select(aa["ev"][ep][:], ab["msk8"][:], ab["el0"][:], ab["el1"][:]),
                                     inc=(sV, 1))
                        else:
                            rec.emit("vector", lambda e, ep=ep: e.tensor_copy(aa["ev"][ep][:], ab["el0"][:]), inc=(sV, 1))
                        stEvB.batch(rec, "sync", ep,
                                    [(dstp[(half * 8 + t) * 128:(half * 8 + t + 1) * 128, dch4 * 512:(dch4 + 1) * 512],
                                      aa["ev"][ep][:])],
                                    waits=[(sV, sV.c)])
        evb_all = [(sm, sm.c) for sm in stEvB.sems]

        rec.emit("vector", lambda e: e.memset(ab["kvacc"][:], 0.0), inc=(sV, 1))
        for i in range(NT):
            p = i % 2
            lw = stKV.batch(rec, "sync", p,
                            [(ab["kpt"][:], kp_sc[i * 128:(i + 1) * 128, :])],
                            waits=(evb_all if i == 0 else [(sPE, kv_pe)]))
            lw2 = stKV.batch(rec, "sync", p, [(ab["vpt"][:], vp_sc[i * 128:(i + 1) * 128, :])])
            w = [lw, lw2, (sGP, ones_gp)]
            if i > 0:
                w.append((sV, sV.c))
            rec.emit("tensor", waits=w)
            for h in range(16):
                rec.emit("tensor", lambda e, h=h: e.matmul(
                    P[h // 8][:, (h % 8) * 128:(h % 8) * 128 + 128],
                    ab["kpt"][:, h * 128:(h + 1) * 128],
                    ab["vpt"][:, h * 128:(h + 1) * 128],
                    start=True, stop=True))
                rec.emit("tensor", lambda e, h=h: e.matmul(
                    P[2][:, h:h + 1],
                    ab["kpt"][:, h * 128:(h + 1) * 128],
                    ab["ones"][:],
                    start=True, stop=True),
                    inc=(sPE, 1) if h == 15 else None)
            kv_pe = sPE.c
            rec.emit("vector", lambda e: e.tensor_add(ab["kvacc"][:, 0:1024], ab["kvacc"][:, 0:1024], P[0][:]),
                     waits=[(sPE, kv_pe)], inc=(sV, 1))
            rec.emit("vector", lambda e: e.tensor_add(ab["kvacc"][:, 1024:2048], ab["kvacc"][:, 1024:2048], P[1][:]))
            rec.emit("vector", lambda e: e.tensor_add(ab["kvacc"][:, 2048:2064], ab["kvacc"][:, 2048:2064], P[2][:, 0:16]),
                     inc=(sV, 1))
        kv_v = sV.c

        lw = stD.batch(rec, "sync", 0, [(kv_part, ab["kvacc"][:])], waits=[(sV, kv_v)])
        rec.emit("gpsimd", lambda g: g.collective_compute(
            "AllReduce", ALU.add, replica_groups=[[0, 1], [2, 3], [4, 5], [6, 7]],
            ins=[kv_part], outs=[kv_sum]), waits=[lw], inc=(sCC, 1))
        cc = sCC.c
        lw_kv = stD.batch(rec, "gpsimd", 0, [(ab["kvsb"][:], kv_sum)], waits=[(sCC, cc)])

        obw = stD.batch(rec, "sync", 0, [(ab["obr"][:], bb_in["o"].partition_broadcast(128))],
                        waits=[(sV, sV.c)])
        QP = [P[0][:, 0:512], P[0][:, 512:1024], P[1][:, 0:512], P[1][:, 512:1024]]
        AT = [P[1][:, 0:512], P[1][:, 512:1024]]
        ZR = P[2][:, 0:4]
        RB = P[2][:, 512:1024]
        qp_guard = (sSC, 0)
        at_guard = [(sV, 0), (sV, 0)]
        o2_guard = (sV, 0)
        for tch in range(4):
            qp_guard = (sV, sV.c)
            for dch4 in range(4):
                pe_base = sPE.c
                for cb in range(4):
                    pc = cb % 2
                    w = [(sPE, pe_base + 4 * (cb - 1))] if cb >= 2 else [(sPE, pe_base)]
                    if tch == 0 and dch4 == 0 and cb == 0:
                        w.append((sSC, xn3T_sc))
                    lw = stI.batch(rec, "scalar", pc,
                                   [(ab["wc4"][pc][:, cc * 512:(cc + 1) * 512],
                                     qwT_sc[(cb * 4 + cc) * 128:(cb * 4 + cc + 1) * 128, dch4 * 512:(dch4 + 1) * 512])
                                    for cc in range(4)],
                                   waits=w)
                    rec.emit("tensor", waits=[lw] + ([qp_guard] if cb == 0 else []))
                    for cc in range(4):
                        c = cb * 4 + cc
                        for di in range(4):
                            rec.emit("tensor", lambda e, pc=pc, c=c, cc=cc, di=di, tch=tch: e.matmul(
                                QP[di], ab["wc4"][pc][:, cc * 512 + di * 128: cc * 512 + (di + 1) * 128],
                                xn3T[:, c * H + tch * 512: c * H + (tch + 1) * 512],
                                start=(c == 0), stop=(c == 15)),
                                inc=(sPE, 1) if di == 3 else None)
                pe_q = sPE.c
                for di in range(4):
                    dch = dch4 * 4 + di
                    rec.emit("vector", lambda e, di=di, dch=dch: e.tensor_scalar(
                        ab["el0"][:], QP[di], ab["qbc"][:, dch:dch + 1], None, ALU.add),
                        waits=[(sPE, pe_q), (sGP, qbc_gp), (sV, sV.c)], inc=(sV, 1))
                    rec.emit("vector", lambda e: e.tensor_single_scalar(ab["msk8"][:], ab["el0"][:], 0.0, ALU.is_gt))
                    rec.emit("scalar", lambda e: e.activation(ab["el1"][:], ab["el0"][:], AF.Exp),
                             waits=[(sV, sV.c)], inc=(sSC, 1))
                    rec.emit("vector", lambda e: e.tensor_scalar_add(ab["el0"][:], ab["el0"][:], 1.0),
                             waits=[(sSC, sSC.c)])
                    rec.emit("vector", lambda e, dch=dch: e.select(ab["qpT"][:, dch * 512:(dch + 1) * 512],
                                                                   ab["msk8"][:], ab["el0"][:], ab["el1"][:]),
                             inc=(sV, 1))
                qp_guard = (sV, sV.c)
            qp_v = sV.c
            qp_guard = (sV, qp_v)
            for h in range(16):
                hp = h % 2
                rec.emit("tensor", waits=[(sV, sV.c), lw_kv])
                rec.emit("tensor", lambda e, h=h, hp=hp: e.matmul(
                    AT[hp], ab["kvsb"][:, h * 128:(h + 1) * 128],
                    ab["qpT"][:, h * 512:(h + 1) * 512],
                    start=True, stop=True))
                rec.emit("tensor", lambda e, h=h: e.matmul(
                    P[2][0:1, 0:512],
                    ab["kvsb"][:, 2048 + h:2048 + h + 1],
                    ab["qpT"][:, h * 512:(h + 1) * 512],
                    start=True, stop=True), inc=(sPE, 1))
                pe_at = sPE.c
                rec.emit("vector", lambda e: e.tensor_scalar_add(ab["rzb"][0:1, 0:512], P[2][0:1, 0:512], EPS_B),
                         waits=[(sPE, pe_at), o2_guard], inc=(sV, 1))
                rec.emit("vector", lambda e: e.reciprocal(ab["rzb"][0:1, 0:512], ab["rzb"][0:1, 0:512]),
                         waits=[(sV, sV.c)], inc=(sV, 1))
                rz_v = sV.c
                rec.emit("tensor", lambda e: e.matmul(
                    RB, onesf[0:1, :], ab["rzb"][0:1, 0:512], start=True, stop=True),
                    waits=[(sV, rz_v)], inc=(sPE, 1))
                pe_rb = sPE.c
                rec.emit("scalar", lambda e: e.copy(ab["el1"][:], RB),
                         waits=[(sPE, pe_rb), (sSC, sSC.c)], inc=(sSC, 1))
                rec.emit("vector", lambda e, h=h, hp=hp: e.tensor_tensor(
                    ab["qpT"][:, h * 512:(h + 1) * 512], AT[hp], ab["el1"][:], ALU.mult),
                    waits=[(sSC, sSC.c)], inc=(sV, 1))
                at_guard[hp] = (sV, sV.c)
            atn_v = sV.c
            for ttp in range(2):
                pe_base = sPE.c
                for eo in range(4):
                    for cb in range(4):
                        ci = eo * 4 + cb
                        pc = ci % 2
                        w = [(sPE, pe_base)] if ci < 2 else [(sPE, pe_base + 4 * (ci - 1))]
                        lw = stI.batch(rec, "scalar", pc,
                                       [(ab["wc4"][pc][:, cc * 512:(cc + 1) * 512],
                                         owT_sc[(cb * 4 + cc) * 128:(cb * 4 + cc + 1) * 128, eo * 512:(eo + 1) * 512])
                                        for cc in range(4)],
                                       waits=w)
                        w2 = [lw, (sV, atn_v)]
                        if eo == 0 and cb == 0:
                            w2.append(o2_guard)
                        rec.emit("tensor", waits=w2)
                        for cc in range(4):
                            c = cb * 4 + cc
                            for tt in range(2):
                                rec.emit("tensor", lambda e, pc=pc, c=c, cc=cc, eo=eo, tt=tt, ttp=ttp: e.matmul(
                                    pst_o[ttp][tt][eo],
                                    ab["qpT"][:, c * 512 + ttp * 256 + tt * 128: c * 512 + ttp * 256 + (tt + 1) * 128],
                                    ab["wc4"][pc][:, cc * 512:(cc + 1) * 512],
                                    start=(c == 0), stop=(c == 15)),
                                    inc=(sPE, 1) if tt == 1 else None)
                pe_o = sPE.c
                for tt in range(2):
                    gt = tch * 4 + ttp * 2 + tt
                    ip = gt % 2
                    lx = stX.batch(rec, "sync", ip, [(ab["x"][:], src[gt])],
                                   waits=[(sV, sV.c)])
                    ow = stO.sems[ip]
                    for eo in range(4):
                        rec.emit("vector", lambda e, tt=tt, ttp=ttp, eo=eo: e.tensor_tensor(
                            ab["o1"][:, eo * 512:(eo + 1) * 512], pst_o[ttp][tt][eo],
                            ab["obr"][:, eo * 512:(eo + 1) * 512], ALU.add),
                            waits=([(sPE, pe_o), obw,
                                    (stO.sems[0], stO.sems[0].c), (stO.sems[1], stO.sems[1].c)]
                                   if eo == 0 else []),
                            inc=(sV, 1) if eo == 3 else None)
                    rec.emit("vector", lambda e: e.tensor_add(ab["o1"][:], ab["o1"][:], ab["x"][:]),
                             waits=[lx], inc=(sV, 1))
                    o2_guard = (sV, sV.c)
                    stO.batch(rec, "sync", ip, [(dst[gt], ab["o1"][:])], waits=[(sV, sV.c)])

    def emit_ffn(src, dst, deq_cc, dbg=None):
        for i in range(NT):
            w = [(sV, sV.c), (sSC, sSC.c)] + (ostores() if i == 0 else [])
            lw = stX.batch(rec, "sync", i % 2, [(ff["x"][:], src[i])], waits=w)
            rec.emit("vector", waits=[lw])
            emit_ln(ff["x"], ff["xn"])
            vr = sV.c
            if dbg is not None and i == 0:
                rec.dma("sync", dbg["xn"], ff["xn"][:], waits=[(sV, vr)], inc=(stO.sems[0], 16))
                rec.dma("sync", dbg["xc"], lnb["xc"][:], inc=(stO.sems[0], 16))
                rec.dma("sync", dbg["mu"], lnb["mu"][:], inc=(stO.sems[0], 16))
                rec.dma("sync", dbg["rs"], lnb["rs"][:], inc=(stO.sems[0], 16))
                rec.dma("sync", dbg["xin"], ff["x"][:], inc=(stO.sems[0], 16))
                rec.emit("vector", waits=[(stO.sems[0], stO.sems[0].c)])
            for c in range(16):
                tr = TRA if c % 2 == 0 else TRB
                rec.emit("tensor", lambda e, c=c, tr=tr: e.transpose(
                    tr, ff["xn"][:, c * 128:(c + 1) * 128], ident[:]),
                    waits=[(sV, vr), (sGP, 1), (sSC, sSC.c)], inc=(sPE, 1))
                pe = sPE.c
                rec.emit("scalar", lambda e, c=c, i=i, tr=tr: e.copy(
                    ff["xnT"][:, c * H + i * 128: c * H + (i + 1) * 128], tr),
                    waits=[(sPE, pe)], inc=(sSC, 1))
        xnT_sc = sSC.c

        pe_base = sPE.c
        v_evac = {}
        st_strip = {}
        for j in range(128):
            p = j % 2
            w = [(sCC, deq_cc), (sPE, pe_base)] if j < 2 else [(sPE, pe_base + 4 * j - 4)]
            lw = stW.batch(rec, "scalar", p,
                           [(ff["wg"][p][:], wg_all[j * 128:(j + 1) * 128, :]),
                            (ff["wu"][p][:], wu_all[j * 128:(j + 1) * 128, :])],
                           waits=w)
            for th in range(2):
                u = 2 * j + th
                q = u % 2
                w = [lw, (sSC, xnT_sc)]
                if u >= 2:
                    w.append((sV, v_evac[u - 2]))
                rec.emit("tensor", waits=w)
                for tc in range(2):
                    for c in range(16):
                        last = tc == 1 and c == 15
                        rec.emit("tensor", lambda e, p=p, q=q, th=th, tc=tc, c=c: e.matmul(
                            P[q][:, tc * 512:(tc + 1) * 512],
                            ff["wg"][p][:, c * 128:(c + 1) * 128],
                            ff["xnT"][:, c * H + th * 1024 + tc * 512: c * H + th * 1024 + (tc + 1) * 512],
                            start=(c == 0), stop=(c == 15)),
                            inc=(sPE, 1) if last else None)
                pe_gate = sPE.c
                for tc in range(2):
                    for c in range(16):
                        last = tc == 1 and c == 15
                        rec.emit("tensor", lambda e, p=p, q=q, th=th, tc=tc, c=c: e.matmul(
                            P[2 + q][:, tc * 512:(tc + 1) * 512],
                            ff["wu"][p][:, c * 128:(c + 1) * 128],
                            ff["xnT"][:, c * H + th * 1024 + tc * 512: c * H + th * 1024 + (tc + 1) * 512],
                            start=(c == 0), stop=(c == 15)),
                            inc=(sPE, 1) if last else None)
                pe_up = sPE.c
                rec.emit("scalar", lambda e, q=q: e.activation(ff["gl"][:], P[q][:], AF.Gelu_apprx_tanh),
                         waits=[(sPE, pe_gate)], inc=(sSC, 1))
                scd = sSC.c
                w = [(sSC, scd), (sPE, pe_up)]
                if th == 0 and j >= 2:
                    w.append(st_strip[j - 2])
                rec.emit("vector", lambda e, p=p, q=q, th=th: e.tensor_tensor(
                    ff["int"][p][:, th * 1024:(th + 1) * 1024], ff["gl"][:], P[2 + q][:], ALU.mult),
                    waits=w, inc=(sV, 1))
                v_evac[u] = sV.c
            st_strip[j] = stS.batch(rec, "sync", p,
                                    [(inter_sc[j * 128:(j + 1) * 128, :], ff["int"][p][:])],
                                    waits=[(sV, v_evac[2 * j + 1])])
            if j % 2 == 1:
                drain_prep()
        strips_all = (stS.sems[1], stS.sems[1].c)
        strips_all0 = (stS.sems[0], stS.sems[0].c)

        pst = [P[t // 2][:, (t % 2) * 512:(t % 2 + 1) * 512] for t in range(8)]
        for tg in range(2):
            for hc in range(4):
                pe_base2 = sPE.c
                v_prev = sV.c
                for g in range(32):
                    p = g % 2
                    if g < 2:
                        w = [strips_all, strips_all0, (sCC, deq_cc)] if (tg == 0 and hc == 0) \
                            else [(sPE, pe_base2)]
                    else:
                        w = [(sPE, pe_base2 + 4 * (g - 1))]
                    pairs = [(ff["ic4"][p][:, jj * 1024:(jj + 1) * 1024],
                              inter_sc[(g * 4 + jj) * 128:(g * 4 + jj + 1) * 128, tg * 1024:(tg + 1) * 1024])
                             for jj in range(4)]
                    pairs += [(ff["wd4"][p][:, jj * 512:(jj + 1) * 512],
                               wd_all[(g * 4 + jj) * 128:(g * 4 + jj + 1) * 128, hc * 512:(hc + 1) * 512])
                              for jj in range(4)]
                    lw = stI.batch(rec, "scalar", p, pairs, waits=w)
                    w = [lw]
                    if g == 0:
                        w.append((sV, v_prev))
                    rec.emit("tensor", waits=w)
                    for jj in range(4):
                        j = g * 4 + jj
                        for t in range(8):
                            rec.emit("tensor", lambda e, p=p, t=t, j=j, jj=jj: e.matmul(
                                pst[t], ff["ic4"][p][:, jj * 1024 + t * 128: jj * 1024 + (t + 1) * 128],
                                ff["wd4"][p][:, jj * 512:(jj + 1) * 512],
                                start=(j == 0), stop=(j == 127)),
                                inc=(sPE, 1) if t == 7 else None)
                pe_done = sPE.c
                for t in range(8):
                    p = t % 2
                    lw = stX.batch(rec, "sync", p,
                                   [(ff["xp"][p][:], src[tg * 8 + t][:, hc * 512:(hc + 1) * 512])],
                                   waits=[(sV, sV.c)])
                    ow = stO.sems[p]
                    rec.emit("vector", lambda e, p=p, t=t: e.tensor_tensor(
                        ff["op"][p][:], pst[t], ff["xp"][p][:], ALU.add),
                        waits=[(sPE, pe_done), lw, (ow, ow.c)], inc=(sV, 1))
                    stO.batch(rec, "sync", p,
                              [(dst[tg * 8 + t][:, hc * 512:(hc + 1) * 512], ff["op"][p][:])],
                              waits=[(sV, sV.c)])
        return

    emit_identity()

    if mode == "full":
        weave = (deq_tile_emitters("g", gpk, gsc, wg_sl)
                 + deq_tile_emitters("u", upk, usc, wu_sl)
                 + deq_tile_emitters("d", dpk, dsc, wd_sl))
        emit_attna(x_my, xb1, weave=weave)
        emit_deq_gather(wg_sl, wg_all)
        emit_deq_gather(wu_sl, wu_all)
        cc = emit_deq_gather(wd_sl, wd_all)
        mk_attnb_prep_units([(sV, sV.c)] + ostores())
        emit_ffn(xb1, xb2, cc)
        emit_attnb(xb2, xb3)
        emit_ffn(xb3, out_my, cc)
        if False:
            fd = {
                "x1": ap(nc.dram_tensor("fdbg_x1", [NT, 128, H], F32, kind="ExternalOutput")),
                "x2": ap(nc.dram_tensor("fdbg_x2", [NT, 128, H], F32, kind="ExternalOutput")),
                "x3": ap(nc.dram_tensor("fdbg_x3", [NT, 128, H], F32, kind="ExternalOutput")),
            }
            w = [(sm, sm.c) for sm in stO.sems] + [(sV, sV.c), (sSC, sSC.c), (sPE, sPE.c)]
            rec.dma("sync", fd["x1"], xb1, waits=w, inc=(stO.sems[0], 16))
            rec.dma("sync", fd["x2"], xb2, inc=(stO.sems[0], 16))
            rec.dma("sync", fd["x3"], xb3, inc=(stO.sems[0], 16))
        for sm in stO.sems:
            rec.emit("sync", waits=[(sm, sm.c)])
    elif mode in ("ffn", "ffn_dbg"):
        emit_dequant("g", gpk, gsc, wg_sl)
        emit_dequant("u", upk, usc, wu_sl)
        emit_dequant("d", dpk, dsc, wd_sl)
        emit_deq_gather(wg_sl, wg_all)
        emit_deq_gather(wu_sl, wu_all)
        cc = emit_deq_gather(wd_sl, wd_all)
        dbg = None
        if mode == "ffn_dbg":
            dbg = {
                "xn": ap(nc.dram_tensor("dbg_xn", [128, H], BF16, kind="ExternalOutput")),
                "xc": ap(nc.dram_tensor("dbg_xc", [128, H], F32, kind="ExternalOutput")),
                "mu": ap(nc.dram_tensor("dbg_mu", [128, 1], F32, kind="ExternalOutput")),
                "rs": ap(nc.dram_tensor("dbg_rs", [128, 1], F32, kind="ExternalOutput")),
                "xin": ap(nc.dram_tensor("dbg_xin", [128, H], F32, kind="ExternalOutput")),
            }
        emit_ffn(x_my, out_my, cc, dbg=dbg)
        if mode == "ffn_dbg":
            dbg_wg = ap(nc.dram_tensor("dbg_wg", [NCORES * 16 * 128, 2048], BF16, kind="ExternalOutput"))
            dbg_wd = ap(nc.dram_tensor("dbg_wd", [NCORES * 16 * 128, 2048], BF16, kind="ExternalOutput"))
            dbg_int = ap(nc.dram_tensor("dbg_int", [128 * 128, 2048], BF16, kind="ExternalOutput"))
            dbg_xnT = ap(nc.dram_tensor("dbg_xnT", [128, 16 * H], BF16, kind="ExternalOutput"))
            w = [(sm, sm.c) for sm in stO.sems] + [(sV, sV.c), (sSC, sSC.c), (sPE, sPE.c)]
            rec.dma("sync", dbg_wg, wg_all, waits=w, inc=(stO.sems[0], 16))
            rec.dma("sync", dbg_wd, wd_all, inc=(stO.sems[0], 16))
            rec.dma("sync", dbg_int, inter_sc, inc=(stO.sems[0], 16))
            rec.dma("sync", dbg_xnT, ff["xnT"][:], inc=(stO.sems[0], 16))
        for sm in stO.sems:
            rec.emit("sync", waits=[(sm, sm.c)])
    elif mode in ("attnb", "attnb_dbg"):
        emit_attnb(x_my, out_my)
        if mode == "attnb_dbg":
            bd = {
                "kp": ap(nc.dram_tensor("bdbg_kp", [16 * 128, 2048], BF16, kind="ExternalOutput")),
                "vp": ap(nc.dram_tensor("bdbg_vp", [16 * 128, 2048], BF16, kind="ExternalOutput")),
                "kvp": ap(nc.dram_tensor("bdbg_kvp", [128, 2064], F32, kind="ExternalOutput")),
                "kvs": ap(nc.dram_tensor("bdbg_kvs", [128, 2064], F32, kind="ExternalOutput")),
            }
            w = [(sm, sm.c) for sm in stO.sems] + [(sV, sV.c), (sSC, sSC.c), (sPE, sPE.c)]
            rec.dma("sync", bd["kp"], kp_sc, waits=w, inc=(stO.sems[0], 16))
            rec.dma("sync", bd["vp"], vp_sc, inc=(stO.sems[0], 16))
            rec.dma("sync", bd["kvp"], kv_part, inc=(stO.sems[0], 16))
            rec.dma("sync", bd["kvs"], kv_sum, inc=(stO.sems[0], 16))
        for sm in stO.sems:
            rec.emit("sync", waits=[(sm, sm.c)])
    elif mode in ("attna", "attna_dbg"):
        adbg = None
        if mode == "attna_dbg":
            adbg = {
                "qT": ap(nc.dram_tensor("adbg_qT", [16 * 128, 2048], BF16, kind="ExternalOutput")),
                "kTa": ap(nc.dram_tensor("adbg_kTa", [2 * 16 * 128, 2048], BF16, kind="ExternalOutput")),
                "va": ap(nc.dram_tensor("adbg_va", [2 * 16 * 128, 2048], BF16, kind="ExternalOutput")),
                "srow": ap(nc.dram_tensor("adbg_srow", [128, 4096], F32, kind="ExternalOutput")),
                "prow": ap(nc.dram_tensor("adbg_prow", [128, 4096], BF16, kind="ExternalOutput")),
                "z": ap(nc.dram_tensor("adbg_z", [128, 1], F32, kind="ExternalOutput")),
            }
        emit_attna(x_my, out_my, adbg=adbg)
        for sm in stO.sems:
            rec.emit("sync", waits=[(sm, sm.c)])
    else:
        raise NotImplementedError(mode)

    rec.replay()
    return nc


def shard_inputs(inputs):
    x = np.ascontiguousarray(np.asarray(inputs["L0_attn_a_x"], dtype=np.float32))
    qkv_w = np.ascontiguousarray(np.asarray(inputs["L0_attn_a_qkv"], dtype=np.float32))
    oproj = np.ascontiguousarray(np.asarray(inputs["L0_attn_a_o_proj"], dtype=np.float32))
    gamma = np.ascontiguousarray(np.asarray(inputs["L0_attn_a_gamma"], dtype=np.float32).reshape(1, H))
    beta = np.ascontiguousarray(np.asarray(inputs["L0_attn_a_beta"], dtype=np.float32).reshape(1, H))
    gpk = np.ascontiguousarray(np.asarray(inputs["L0_ffn_gate_W_packed"], dtype=np.int32))
    gsc = np.ascontiguousarray(np.asarray(inputs["L0_ffn_gate_scales"], dtype=np.float32))
    upk = np.ascontiguousarray(np.asarray(inputs["L0_ffn_up_W_packed"], dtype=np.int32))
    usc = np.ascontiguousarray(np.asarray(inputs["L0_ffn_up_scales"], dtype=np.float32))
    dpk = np.ascontiguousarray(np.asarray(inputs["L0_ffn_down_W_packed"], dtype=np.int32))
    dsc = np.ascontiguousarray(np.asarray(inputs["L0_ffn_down_scales"], dtype=np.float32))
    wb = {k: np.ascontiguousarray(np.asarray(inputs[f"L0_attn_b_{k}_w"], dtype=np.float32))
          for k in ("q", "k", "v", "o")}
    bb = {k: np.ascontiguousarray(np.asarray(inputs[f"L0_attn_b_{k}_b"], dtype=np.float32).reshape(1, H))
          for k in ("q", "k", "v", "o")}

    in_maps = []
    for c in range(NCORES):
        b, par = c // 2, c % 2
        xt = np.ascontiguousarray(x[b].reshape(32, 128, H)[par::2])
        mk = np.zeros((NT, 128, 256), dtype=np.float32)
        for i in range(NT):
            g = 2 * i + par
            r0 = 128 * g
            np_pan = g // 2 + 1
            c0 = 256 * (np_pan - 1)
            rr = r0 + np.arange(128)[:, None]
            cc = c0 + np.arange(256)[None, :]
            mk[i] = np.where(cc <= rr, 0.0, NEG).astype(np.float32)
        nsl = slice(c * 2048, (c + 1) * 2048)
        m = {
            "x_my": xt, "masks": mk,
            "qkv": qkv_w, "oproj": oproj, "gamma": gamma, "beta": beta,
            "bqw": wb["q"], "bkw": wb["k"], "bvw": wb["v"], "bow": wb["o"],
            "bqb": bb["q"], "bkb": bb["k"], "bvb": bb["v"], "bob": bb["o"],
            "gpk": np.ascontiguousarray(gpk[nsl]), "gsc": np.ascontiguousarray(gsc[nsl]),
            "upk": np.ascontiguousarray(upk[nsl]), "usc": np.ascontiguousarray(usc[nsl]),
            "dpk": np.ascontiguousarray(dpk[:, c * 1024:(c + 1) * 1024]),
            "dsc": np.ascontiguousarray(dsc[:, c * 64:(c + 1) * 64]),
        }
        in_maps.append(m)
    return in_maps


def gather_output(results):
    out = np.zeros((B, S, H), dtype=np.float32)
    for c in range(NCORES):
        b, par = c // 2, c % 2
        out[b].reshape(32, 128, H)[par::2] = np.asarray(results[c]["out_my"], dtype=np.float32)
    return out


_NC_CACHE = {}


def kernel(**inputs):
    mode = "full"
    if mode not in _NC_CACHE:
        _NC_CACHE[mode] = build_program(mode)
    nc = _NC_CACHE[mode]
    in_maps = shard_inputs(inputs)
    res = run_bass_kernel_spmd(nc, in_maps, core_ids=list(range(NCORES)))
    return gather_output(res.results)


# version 08
# speedup vs baseline: 1.0434x; 1.0007x over previous
import numpy as np
import concourse.bass as bass
import concourse.mybir as mybir
from concourse.bass_utils import run_bass_kernel_spmd

F32 = mybir.dt.float32
BF16 = mybir.dt.bfloat16
I32 = mybir.dt.int32
U8 = mybir.dt.uint8
AF = mybir.ActivationFunctionType
ALU = mybir.AluOpType
AX = mybir.AxisListType

H = 2048
S = 4096
B = 4
NT = 16
NCORES = 8
EPS_LN = 1e-5
EPS_B = 1e-6
NEG = -1e30

ENGINES = ("sync", "tensor", "vector", "scalar", "gpsimd")


class Sem:
    def __init__(self, nc, name):
        self.h = nc.alloc_semaphore(name=name)
        self.c = 0
        self.name = name

    def inc(self, k):
        self.c += k
        return self.c


class Rec:
    def __init__(self, nc):
        self.nc = nc
        self.ops = {e: [] for e in ENGINES}

    def emit(self, eng, builder=None, waits=(), inc=None):
        waits = [(s.h, int(v)) for (s, v) in waits if v > 0]
        if inc is not None:
            incs = inc if isinstance(inc, list) else [inc]
            for s, k in incs:
                s.inc(k)
            incs = [(s.h, int(k)) for (s, k) in incs]
        else:
            incs = []

        def fn(e, waits=waits, incs=incs, builder=builder):
            for h, v in waits:
                e.wait_ge(h, v)
            if builder is not None:
                ins = builder(e)
                for h, k in incs:
                    ins.then_inc(h, k)

        self.ops[eng].append(fn)

    def dma(self, eng, out, in_, waits=(), inc=None, **kw):
        self.emit(eng, lambda e: e.dma_start(out=out, in_=in_, **kw),
                  waits=waits, inc=inc)

    def replay(self):
        with self.nc.Block() as block:
            for eng in ENGINES:
                fns = self.ops[eng]
                getattr(block, eng)(lambda engine, fns=fns: [f(engine) for f in fns])


class Stream:

    def __init__(self, nc, name, nslots=2):
        self.sems = [Sem(nc, f"{name}{i}") for i in range(nslots)]

    def batch(self, rec, eng, slot, pairs, waits=(), **kw):
        s = self.sems[slot]
        for i, (o, inp) in enumerate(pairs):
            rec.dma(eng, o, inp, waits=(waits if i == 0 else ()), inc=(s, 16), **kw)
        return (s, s.c)


def build_program(mode="full"):
    nc = bass.Bass("TRN2", target_bir_lowering=False)
    ap = lambda t: t.ap()

    x_my = ap(nc.dram_tensor("x_my", [NT, 128, H], F32, kind="ExternalInput"))
    masks = ap(nc.dram_tensor("masks", [NT, 128, 256], F32, kind="ExternalInput"))
    qkv = ap(nc.dram_tensor("qkv", [H, 3 * H], F32, kind="ExternalInput"))
    oproj = ap(nc.dram_tensor("oproj", [H, H], F32, kind="ExternalInput"))
    gamma = ap(nc.dram_tensor("gamma", [1, H], F32, kind="ExternalInput"))
    beta = ap(nc.dram_tensor("beta", [1, H], F32, kind="ExternalInput"))
    bw_in = {k: ap(nc.dram_tensor(f"b{k}w", [H, H], F32, kind="ExternalInput"))
             for k in ("q", "k", "v", "o")}
    bb_in = {k: ap(nc.dram_tensor(f"b{k}b", [1, H], F32, kind="ExternalInput"))
             for k in ("q", "k", "v", "o")}
    gpk = ap(nc.dram_tensor("gpk", [2048, 1024], I32, kind="ExternalInput"))
    gsc = ap(nc.dram_tensor("gsc", [2048, 64], F32, kind="ExternalInput"))
    upk = ap(nc.dram_tensor("upk", [2048, 1024], I32, kind="ExternalInput"))
    usc = ap(nc.dram_tensor("usc", [2048, 64], F32, kind="ExternalInput"))
    dpk = ap(nc.dram_tensor("dpk", [2048, 1024], I32, kind="ExternalInput"))
    dsc = ap(nc.dram_tensor("dsc", [2048, 64], F32, kind="ExternalInput"))

    out_my = ap(nc.dram_tensor("out_my", [NT, 128, H], F32, kind="ExternalOutput"))

    dr = lambda name, shape, dt=BF16: ap(nc.dram_tensor(name, shape, dt, kind="Internal"))
    wg_sl = dr("wg_sl", [16 * 128, 2048])
    wu_sl = dr("wu_sl", [16 * 128, 2048])
    wd_sl = dr("wd_sl", [16 * 128, 2048])
    drs = lambda name, shape, dt=BF16: ap(nc.dram_tensor(name, shape, dt, kind="Internal", addr_space="Shared"))
    wg_all = drs("wg_all", [NCORES * 16 * 128, 2048])
    wu_all = drs("wu_all", [NCORES * 16 * 128, 2048])
    wd_all = drs("wd_all", [NCORES * 16 * 128, 2048])
    inter_sc = dr("inter_sc", [128 * 128, 2048])
    xb1 = ap(nc.dram_tensor("xb1", [NT, 128, H], F32, kind="Internal"))
    xb2 = ap(nc.dram_tensor("xb2", [NT, 128, H], F32, kind="Internal"))
    xb3 = ap(nc.dram_tensor("xb3", [NT, 128, H], F32, kind="Internal"))
    qT_sc = dr("qT_sc", [16 * 128, 2048])
    kT_sc = dr("kT_sc", [16 * 128, 2048])
    v_sc = dr("v_sc", [16 * 128, 2048])
    kT_all = dr("kT_all", [2 * 16 * 128, 2048])
    v_all = dr("v_all", [2 * 16 * 128, 2048])
    w_bf = {k: dr(f"{k}w_bf", [H, H]) for k in ("q", "k", "v", "o")}
    qwT_sc = dr("qwT_sc", [H, H])
    kwT_sc = dr("kwT_sc", [H, H])
    vwT_sc = dr("vwT_sc", [H, H])
    owT_sc = dr("owT_sc", [H, H])
    kp_sc = dr("kp_sc", [16 * 128, 2048])
    vp_sc = dr("vp_sc", [16 * 128, 2048])
    kv_part = ap(nc.dram_tensor("kv_part", [128, 2064], F32, kind="Internal"))
    kv_sum = ap(nc.dram_tensor("kv_sum", [128, 2064], F32, kind="Internal"))

    rec = Rec(nc)
    sPE = Sem(nc, "sPE")
    sV = Sem(nc, "sV")
    sSC = Sem(nc, "sSC")
    sGP = Sem(nc, "sGP")
    sCC = Sem(nc, "sCC")
    stX = Stream(nc, "stX", 2)
    stW = Stream(nc, "stW", 2)
    stI = Stream(nc, "stI", 2)
    stP = Stream(nc, "stP", 2)
    stO = Stream(nc, "stO", 2)
    stD = Stream(nc, "stD", 1)
    stS = Stream(nc, "stS", 2)
    stEvB = Stream(nc, "stEvB", 2)

    def ostores():
        return [(stO.sems[0], stO.sems[0].c), (stO.sems[1], stO.sems[1].c)]

    P = [nc.alloc_psum_tensor(f"P{i}", [128, 1024], F32) for i in range(4)]
    TRA = P[3][:, 0:64].bitcast(BF16)
    TRB = P[3][:, 512:576].bitcast(BF16)

    alloc = nc.alloc_sbuf_tensor
    ident = alloc("ident", [128, 128], BF16)
    onesf = alloc("onesf", [1, 128], F32)

    A1 = alloc("A1", [128, 65536], U8)
    A2 = alloc("A2", [128, 49152], U8)
    A3 = alloc("A3", [128, 32768], U8)
    A4 = alloc("A4", [128, 16384], U8)
    NB = {F32: 4, BF16: 2, I32: 4, U8: 1}

    def V(ar, off, n, dt):
        return ar[:, off:off + n * NB[dt]].bitcast(dt)

    dq = {
        "pk": alloc("dq_pk", [128, 1024], I32),
        "sc": alloc("dq_sc", [128, 64], F32),
        "f": alloc("dq_f", [128, 1024], F32),
        "m": alloc("dq_m", [128, 1024], F32),
        "t0": alloc("dq_t0", [128, 1024], F32),
        "t1": alloc("dq_t1", [128, 1024], F32),
        "ge": alloc("dq_ge", [128, 1024], F32),
        "l4": alloc("dq_l4", [128, 1024], U8),
        "l6": alloc("dq_l6", [128, 1024], U8),
        "vals": alloc("dq_vals", [128, 2048], BF16),
        "stage": alloc("dq_stage", [128, 2048], BF16),
    }
    dq["ci"] = dq["t1"][:].bitcast(I32)
    lnb = {
        "mu": alloc("ln_mu", [128, 1], F32),
        "ss": alloc("ln_ss", [128, 1], F32),
        "rs": alloc("ln_rs", [128, 1], F32),
        "xc": alloc("ln_xc", [128, H], F32),
        "sq": V(A4, 0, H, F32),
    }

    def emit_identity():
        rec.emit("gpsimd", lambda g: g.memset(onesf[:], 1.0), inc=(sGP, 1))
        rec.emit("gpsimd", lambda g: g.memset(ident[:], 0.0), inc=(sGP, 1))
        rec.emit("gpsimd", lambda g: g.affine_select(
            out=ident[:], in_=ident[:], compare_op=ALU.not_equal, fill=1.0,
            base=0, pattern=[[-1, 128]], channel_multiplier=1), inc=(sGP, 1))

    def deq_tile_emitters(which, pk_d, sc_d, sl_d):
        out = []
        state = {}

        def mk(t):
            def vec_part():
                emit_deq_vec(which, pk_d, sc_d, t, state)

            def pe_part():
                emit_deq_pe(which, sl_d, t, state)
            return vec_part, pe_part
        for t in range(16):
            out.append(mk(t))
        return out

    def emit_dequant(which, pk_d, sc_d, sl_d):
        for v, p in deq_tile_emitters(which, pk_d, sc_d, sl_d):
            v()
            p()

    def emit_deq_vec(which, pk_d, sc_d, t, state):
        if True:
            lw = stP.batch(rec, "scalar", 0,
                           [(dq["pk"][:], pk_d[t * 128:(t + 1) * 128, :]),
                            (dq["sc"][:], sc_d[t * 128:(t + 1) * 128, :])],
                           waits=[(sV, sV.c), (sPE, sPE.c)])
            v = "vector"
            rec.emit(v, waits=[lw])
            for pl in (0, 1):
                if pl == 0:
                    rec.emit(v, lambda e: e.tensor_single_scalar(dq["ci"][:], dq["pk"][:], 15, ALU.bitwise_and))
                else:
                    rec.emit(v, lambda e: e.tensor_single_scalar(dq["ci"][:], dq["pk"][:], 4, ALU.logical_shift_right))
                    rec.emit(v, lambda e: e.tensor_single_scalar(dq["ci"][:], dq["ci"][:], 15, ALU.bitwise_and))
                rec.emit(v, lambda e: e.tensor_copy(dq["f"][:], dq["ci"][:]))
                rec.emit(v, lambda e: e.tensor_single_scalar(dq["ge"][:], dq["f"][:], 8.0, ALU.is_ge))
                rec.emit(v, lambda e: e.scalar_tensor_tensor(dq["m"][:], dq["ge"][:], -8.0, dq["f"][:], ALU.mult, ALU.add))
                rec.emit(v, lambda e: e.tensor_single_scalar(dq["l4"][:], dq["m"][:], 4.0, ALU.is_lt))
                rec.emit(v, lambda e: e.tensor_single_scalar(dq["l6"][:], dq["m"][:], 6.0, ALU.is_lt))
                rec.emit(v, lambda e: e.tensor_scalar(dq["t0"][:], dq["m"][:], 2.0, -8.0, ALU.mult, ALU.add))
                rec.emit(v, lambda e: e.tensor_scalar_sub(dq["t1"][:], dq["m"][:], 2.0))
                rec.emit(v, lambda e: e.select(dq["t0"][:], dq["l6"][:], dq["t1"][:], dq["t0"][:]))
                rec.emit(v, lambda e: e.tensor_scalar_mul(dq["t1"][:], dq["m"][:], 0.5))
                rec.emit(v, lambda e: e.select(dq["t0"][:], dq["l4"][:], dq["t1"][:], dq["t0"][:]))
                rec.emit(v, lambda e: e.tensor_scalar(dq["ge"][:], dq["ge"][:], -2.0, 1.0, ALU.mult, ALU.add))
                rec.emit(v, lambda e: e.tensor_mul(dq["t0"][:], dq["t0"][:], dq["ge"][:]))
                o3 = dq["vals"][:, pl::2].rearrange("p (b j) -> p b j", j=16)
                i3 = dq["t0"][:].rearrange("p (b j) -> p b j", j=16)
                s3 = dq["sc"][:, :, None].broadcast_to([128, 64, 16])
                rec.emit(v, lambda e, o3=o3, i3=i3, s3=s3: e.tensor_tensor(o3, i3, s3, ALU.mult),
                         inc=(sV, 1))
            state[t] = sV.c

    def emit_deq_pe(which, sl_d, t, state):
        if True:
            v_done = state[t]
            for c in range(16):
                tr = TRA if c % 2 == 0 else TRB
                rec.emit("tensor", lambda e, c=c, tr=tr: e.transpose(
                    tr, dq["vals"][:, c * 128:(c + 1) * 128], ident[:]),
                    waits=[(sV, v_done), (sGP, 3), (sSC, sSC.c)], inc=(sPE, 1))
                pe_now = sPE.c
                rec.emit("scalar", lambda e, c=c, tr=tr: e.copy(dq["stage"][:, c * 128:(c + 1) * 128], tr),
                         waits=[(sPE, pe_now)], inc=(sSC, 1))
            sc_done = sSC.c
            if which in ("g", "u"):
                stw = stD.batch(rec, "sync", 0, [(sl_d[t * 128:(t + 1) * 128, :], dq["stage"][:])],
                                waits=[(sSC, sc_done)])
            else:
                stw = stD.batch(rec, "sync", 0,
                                [(sl_d[c * 128:(c + 1) * 128, t * 128:(t + 1) * 128],
                                  dq["stage"][:, c * 128:(c + 1) * 128]) for c in range(16)],
                                waits=[(sSC, sc_done)])
            rec.emit("scalar", waits=[stw])

    def emit_deq_gather(sl_d, all_d):
        sD = stD.sems[0]
        rec.emit("gpsimd", lambda g, sl_d=sl_d, all_d=all_d: g.collective_compute(
            "AllGather", ALU.bypass, replica_groups=[list(range(NCORES))],
            ins=[sl_d], outs=[all_d]),
            waits=[(sD, sD.c)], inc=(sCC, 1))
        return sCC.c

    def emit_ln(x_sb, xn_out, stats_to=None, eps=EPS_LN, xc=None):
        b = lnb
        xc = xc if xc is not None else b["xc"]
        rec.emit("vector", lambda e: e.reduce_sum(b["ss"][:], x_sb[:], axis=AX.X), inc=(sV, 1))
        rec.emit("vector", lambda e: e.tensor_scalar_mul(b["mu"][:], b["ss"][:], 1.0 / H),
                 waits=[(sV, sV.c)], inc=(sV, 1))
        rec.emit("vector", lambda e, xc=xc: e.tensor_scalar(xc[:], x_sb[:], b["mu"][:, 0:1], None, ALU.subtract),
                 waits=[(sV, sV.c)], inc=(sV, 1))
        rec.emit("scalar", lambda e, xc=xc: e.activation(b["sq"][:], xc[:], AF.Square, accum_out=b["ss"][:]),
                 waits=[(sV, sV.c)], inc=(sSC, 1))
        rec.emit("vector", lambda e, eps=eps: e.tensor_scalar(b["ss"][:], b["ss"][:], 1.0 / H, float(eps), ALU.mult, ALU.add),
                 waits=[(sSC, sSC.c)], inc=(sV, 1))
        rec.emit("scalar", lambda e: e.activation(b["ss"][:], b["ss"][:], AF.Ln),
                 waits=[(sV, sV.c)], inc=(sSC, 1))
        rec.emit("vector", lambda e: e.tensor_scalar_mul(b["ss"][:], b["ss"][:], -0.5),
                 waits=[(sSC, sSC.c)], inc=(sV, 1))
        rec.emit("scalar", lambda e: e.activation(b["rs"][:], b["ss"][:], AF.Exp),
                 waits=[(sV, sV.c)], inc=(sSC, 1))
        rec.emit("vector", lambda e, xc=xc: e.tensor_scalar(xn_out[:], xc[:], b["rs"][:, 0:1], None, ALU.mult),
                 waits=[(sSC, sSC.c)], inc=(sV, 1))
        if stats_to is not None:
            rec.emit("vector", lambda e: e.tensor_scalar_add(stats_to[:, 0:1], b["rs"][:], 1.0))
            rec.emit("vector", lambda e: e.tensor_tensor(stats_to[:, 1:2], b["mu"][:], b["rs"][:], ALU.mult))
            rec.emit("vector", lambda e: e.tensor_scalar_mul(stats_to[:, 1:2], stats_to[:, 1:2], -1.0), inc=(sV, 1))

    ff = {
        "x": V(A3, 0, H, F32),
        "xn": V(A3, 8192, H, BF16),
        "xnT": V(A1, 0, 16 * H, BF16),
        "wg": [V(A2, 0, 2048, BF16), V(A2, 4096, 2048, BF16)],
        "wu": [V(A2, 8192, 2048, BF16), V(A2, 12288, 2048, BF16)],
        "gl": V(A2, 16384, 1024, F32),
        "int": [V(A2, 20480, 2048, BF16), V(A2, 24576, 2048, BF16)],
    "ic": [V(A2, 28672, 1024, BF16), V(A2, 30720, 1024, BF16)],
        "wd": [V(A2, 32768, 512, BF16), V(A2, 33792, 512, BF16)],
        "xp": [V(A2, 34816, 512, F32), V(A2, 36864, 512, F32)],
        "op": [V(A2, 38912, 512, F32), V(A2, 40960, 512, F32)],
        "ic4": [V(A2, 0, 4096, BF16), V(A2, 8192, 4096, BF16)],
        "wd4": [V(A2, 16384, 2048, BF16), V(A2, 20480, 2048, BF16)],
    }

    aa = {
        "xn1": V(A3, 0, H, F32),
        "xng": V(A3, 8192, H, BF16),
        "grep": V(A3, 12288, H, F32),
        "brep": V(A3, 20480, H, F32),
        "lnst": alloc("aa_lnst", [128, 2 * NT], F32),
        "qkvb": [V(A2, 16384, 2048, BF16), V(A2, 20480, 2048, BF16)],
        "vwb": [V(A2, 0, 16 * 512, BF16), V(A2, 16384 - 8192 + 0 * 8192 + 8192, 0, BF16)],
        "ev": [alloc("aa_ev0", [128, 512], BF16), alloc("aa_ev1", [128, 512], BF16)],
        "qsb": [V(A2, 40960, 16 * 128, BF16), V(A2, 45056, 16 * 128, BF16)],
        "ksb": [V(A2, 0, 16 * 256, BF16), V(A2, 8192, 16 * 256, BF16)],
        "vsb": [V(A2, 16384, 2 * 2048, BF16), V(A2, 24576, 2 * 2048, BF16)],
        "msk": V(A2, 32768, NT * 256, BF16),
        "srow": V(A3, 0, 4096, F32),
        "prow": V(A4, 0, 4096, BF16),
        "nmax": alloc("aa_nmax", [128, 1], F32),
        "z": alloc("aa_z", [128, 1], F32),
        "rz": alloc("aa_rz", [128, 1], F32),
        "pt": [alloc("aa_pt0", [128, 128], BF16), alloc("aa_pt1", [128, 128], BF16)],
        "pvn": V(A4, 8192, 2048, BF16),
        "pvt": V(A4, 12288, 16 * 128, BF16),
        "opb": V(A1, 0, 16 * 2048, BF16),
        "xf": V(A3, 16384, H, F32),
        "o1": V(A3, 24576, H, F32),
    }
    aa["vwb"] = [V(A2, 0, 16 * 512, BF16), V(A2, 8192, 16 * 512, BF16)]
    aa["xn2"] = aa["xn1"]

    def emit_attna(src, dst, par_dummy=None, adbg=None, weave=None):
        wv = list(weave) if weave else []
        wstate = {"vq": [], "k": 0}

        def weave_step():
            if wstate["k"] < len(wv):
                wv[wstate["k"]][0]()
                wstate["vq"].append(wv[wstate["k"]][1])
                wstate["k"] += 1
            if len(wstate["vq"]) > 1:
                wstate["vq"].pop(0)()

        def weave_flush():
            while wstate["k"] < len(wv):
                weave_step()
            while wstate["vq"]:
                wstate["vq"].pop(0)()
        xngT = ff["xnT"]
        lw_g = stD.batch(rec, "sync", 0, [(aa["grep"][:], gamma.partition_broadcast(128))])
        lw_g = stD.batch(rec, "sync", 0, [(aa["brep"][:], beta.partition_broadcast(128))])
        stMK = Stream(nc, "stMK", 1)
        lw_mk = stMK.batch(rec, "gpsimd", 0, [(aa["msk"][:].rearrange("p (i c) -> p i c", c=256), masks.transpose([1, 0, 2]))])
        stOP = Stream(nc, "stOP", 1)

        for i in range(NT):
            weave_step()
            lw = stX.batch(rec, "sync", i % 2, [(ff["x"][:], src[i])],
                           waits=[(sV, sV.c), (sSC, sSC.c)])
            rec.emit("vector", waits=[lw, lw_g])
            emit_ln(ff["x"], aa["xn1"], stats_to=aa["lnst"][:, 2 * i:2 * i + 2])
            emit_ln(aa["xn1"], aa["xn1"])
            rec.emit("vector", lambda e: e.tensor_mul(aa["xn1"][:], aa["xn1"][:], aa["grep"][:]))
            rec.emit("vector", lambda e: e.tensor_add(aa["xng"][:], aa["xn1"][:], aa["brep"][:]), inc=(sV, 1))
            vr = sV.c
            for c in range(16):
                tr = TRA if c % 2 == 0 else TRB
                rec.emit("tensor", lambda e, c=c, tr=tr: e.transpose(
                    tr, aa["xng"][:, c * 128:(c + 1) * 128], ident[:]),
                    waits=[(sV, vr), (sGP, 3), (sSC, sSC.c)], inc=(sPE, 1))
                pe = sPE.c
                rec.emit("scalar", lambda e, c=c, i=i, tr=tr: e.copy(
                    xngT[:, c * H + i * 128: c * H + (i + 1) * 128], tr),
                    waits=[(sPE, pe)], inc=(sSC, 1))
        xngT_sc = sSC.c

        qkT_sc = {0: qT_sc, 1: kT_sc}
        stEv = Stream(nc, "stEv", 2)
        ev_guard = [(sSC, 0), (sSC, 0)]
        for dch in range(32):
            weave_step()
            p = dch % 2
            src_cols = qkv[:, (dch % 16) * 128 + (dch // 16) * 2048:][:, 0:128]
            lw = stW.batch(rec, "gpsimd", p,
                           [(aa["qkvb"][p][:].rearrange("p (c d) -> p c d", d=128),
                             src_cols.rearrange("(c p) d -> p c d", p=128))],
                           waits=[(sPE, sPE.c)])
            for tch in range(4):
                bank = P[p] if tch < 2 else P[2 + p]
                reg = bank[:, (tch % 2) * 512:(tch % 2) * 512 + 512]
                w = [lw, (sSC, xngT_sc), ev_guard[p]]
                rec.emit("tensor", waits=w)
                for c in range(16):
                    rec.emit("tensor", lambda e, p=p, c=c, tch=tch, reg=reg: e.matmul(
                        reg, aa["qkvb"][p][:, c * 128:(c + 1) * 128],
                        xngT[:, c * H + tch * 512: c * H + (tch + 1) * 512],
                        start=(c == 0), stop=(c == 15)),
                        inc=(sPE, 1) if c == 15 else None)
                pe = sPE.c
                ep = tch % 2
                rec.emit("scalar", lambda e, reg=reg, ep=ep: e.copy(aa["ev"][ep][:], reg),
                         waits=[(sPE, pe), (stEv.sems[ep], stEv.sems[ep].c)], inc=(sSC, 1))
                stEv.batch(rec, "sync", ep,
                           [(qkT_sc[dch // 16][(dch % 16) * 128:(dch % 16 + 1) * 128, tch * 512:(tch + 1) * 512],
                             aa["ev"][ep][:])],
                           waits=[(sSC, sSC.c)])
            ev_guard[p] = (sSC, sSC.c)

        weave_flush()
        for half in range(2):
            for dch4 in range(4):
                p = dch4 % 2
                src_cols = qkv[:, 2 * H + dch4 * 512: 2 * H + (dch4 + 1) * 512]
                lw = stW.batch(rec, "gpsimd", p,
                               [(aa["vwb"][p][:].rearrange("p (c d) -> p c d", d=512),
                                 src_cols.rearrange("(c p) d -> p c d", p=128))],
                               waits=[(sPE, sPE.c)])
                rec.emit("tensor", waits=[lw, (sV, sV.c), (sSC, sSC.c)])
                for c in range(16):
                    for t in range(8):
                        rec.emit("tensor", lambda e, p=p, c=c, t=t, half=half: e.matmul(
                            P[t // 2][:, (t % 2) * 512:(t % 2) * 512 + 512],
                            xngT[:, c * H + half * 1024 + t * 128: c * H + half * 1024 + (t + 1) * 128],
                            aa["vwb"][p][:, c * 512:(c + 1) * 512],
                            start=(c == 0), stop=(c == 15)),
                            inc=(sPE, 1) if (c == 15 and t == 7) else None)
                pe = sPE.c
                for t in range(8):
                    ep = t % 2
                    rec.emit("scalar", lambda e, t=t, ep=ep: e.copy(
                        aa["ev"][ep][:], P[t // 2][:, (t % 2) * 512:(t % 2) * 512 + 512]),
                        waits=[(sPE, pe), (stEv.sems[ep], stEv.sems[ep].c)], inc=(sSC, 1))
                    stEv.batch(rec, "sync", ep,
                               [(v_sc[(half * 8 + t) * 128:(half * 8 + t + 1) * 128, dch4 * 512:(dch4 + 1) * 512],
                                 aa["ev"][ep][:])],
                               waits=[(sSC, sSC.c)])

        opw = stOP.batch(rec, "gpsimd", 0,
                         [(aa["opb"][:, c * 2048:(c + 1) * 2048],
                           oproj[c * 128:(c + 1) * 128, :]) for c in range(16)],
                         waits=[(sPE, sPE.c)])

        pair_groups = [[0, 1], [2, 3], [4, 5], [6, 7]]
        ev_all = [(sm, sm.c) for sm in stEv.sems]
        rec.emit("gpsimd", lambda g: g.collective_compute(
            "AllGather", ALU.bypass, replica_groups=pair_groups,
            ins=[kT_sc], outs=[kT_all]), waits=ev_all, inc=(sCC, 1))
        rec.emit("gpsimd", lambda g: g.collective_compute(
            "AllGather", ALU.bypass, replica_groups=pair_groups,
            ins=[v_sc], outs=[v_all]), inc=(sCC, 1))
        ag_cc = sCC.c
        if adbg is not None:
            rec.dma("sync", adbg["qT"], qT_sc, waits=[(sCC, ag_cc)], inc=(stO.sems[0], 16))
            rec.dma("sync", adbg["kTa"], kT_all, inc=(stO.sems[0], 16))
            rec.dma("sync", adbg["va"], v_all, inc=(stO.sems[0], 16))

        kT_r = kT_all.rearrange("(r c p) t -> r c p t", r=2, p=128)
        v_r = v_all.rearrange("(r i p) d -> r i p d", r=2, p=128)

        stQT = Stream(nc, "stQT", 2)
        stKP = Stream(nc, "stKP", 2)
        stVP = Stream(nc, "stVP", 2)
        S_reg = [P[0][:, 0:256], P[0][:, 512:768]]
        prow_s = [aa["prow"], lnb["xc"][:].bitcast(BF16)]
        nmax_s = [aa["nmax"], alloc("aa_nmax1", [128, 1], F32)]
        z_s = [aa["z"], alloc("aa_z1", [128, 1], F32)]
        rz_s = [aa["rz"], alloc("aa_rz1", [128, 1], F32)]
        st = {"s_guard": [(sSC, 0), (sSC, 0)], "o1_prev": (sV, 0),
              "q_guard": [(sPE, 0), (sPE, 0)], "k_guard": [(sPE, 0), (sPE, 0)],
              "v_guard": [(sPE, 0), (sPE, 0)], "pt_done": [(sPE, 0), (sPE, 0)],
              "exp_sc": (sSC, 0), "lq": {}, "rz_v": {}, "exp_of": {}}

        def aa_pass1(i):
            ip = i % 2
            npan = i + 1
            st["lq"][i] = stQT.batch(rec, "sync", ip,
                            [(aa["qsb"][ip][:].rearrange("p (c t) -> p c t", t=128),
                              qT_sc.rearrange("(c p) t -> p c t", p=128)[:, :, i * 128:(i + 1) * 128])],
                            waits=[st["q_guard"][ip], (sCC, ag_cc)])
            for pan in range(npan):
                pp = pan % 2
                prs = []
                for gsub in range(2):
                    gc = 2 * pan + gsub
                    rr = gc % 2
                    lt = gc // 2
                    prs.append((aa["ksb"][pp][:].rearrange("p (c t) -> p c t", t=256)[:, :, gsub * 128:(gsub + 1) * 128],
                                kT_r[rr, :, :, lt * 128:(lt + 1) * 128].transpose([1, 0, 2])))
                lk = stKP.batch(rec, "sync", pp, prs, waits=[st["k_guard"][pp], (sCC, ag_cc)])
                rec.emit("tensor", waits=[st["lq"][i], lk, st["s_guard"][pp]])
                for c in range(16):
                    rec.emit("tensor", lambda e, ip=ip, pp=pp, c=c: e.matmul(
                        S_reg[pp], aa["qsb"][ip][:, c * 128:(c + 1) * 128],
                        aa["ksb"][pp][:, c * 256:(c + 1) * 256],
                        start=(c == 0), stop=(c == 15)),
                        inc=(sPE, 1) if c == 15 else None)
                st["k_guard"][pp] = (sPE, sPE.c)
                pe = sPE.c
                w = [(sPE, pe)]
                if pan == 0:
                    w.append(st["exp_sc"])
                if pan == npan - 1:
                    rec.emit("vector", lambda e, i=i, pan=pan, pp=pp: e.scalar_tensor_tensor(
                        aa["srow"][:, pan * 256:(pan + 1) * 256], S_reg[pp], 1.0 / 45.254833995939045,
                        aa["msk"][:, i * 256:(i + 1) * 256], ALU.mult, ALU.add),
                        waits=w + [lw_mk], inc=(sV, 1))
                else:
                    rec.emit("vector", lambda e, pan=pan, pp=pp: e.tensor_scalar_mul(
                        aa["srow"][:, pan * 256:(pan + 1) * 256], S_reg[pp], 1.0 / 45.254833995939045),
                        waits=w, inc=(sV, 1))
                st["s_guard"][pp] = (sV, sV.c)

        def aa_softmax(i):
            ip = i % 2
            npan = i + 1
            rec.emit("vector", lambda e, i=i, npan=npan, ip=ip: e.reduce_max(
                nmax_s[ip][:], aa["srow"][:, 0:npan * 256], axis=AX.X, negate=True),
                waits=[(sV, sV.c)], inc=(sV, 1))
            rec.emit("scalar", lambda e, npan=npan, ip=ip: e.activation(
                prow_s[ip][:, 0:npan * 256], aa["srow"][:, 0:npan * 256], AF.Exp,
                bias=nmax_s[ip][:, 0:1], accum_out=z_s[ip][:]),
                waits=[(sV, sV.c), st["pt_done"][ip]], inc=(sSC, 1))
            st["exp_sc"] = (sSC, sSC.c)
            st["exp_of"][i] = (sSC, sSC.c)
            rec.emit("vector", lambda e, ip=ip: e.reciprocal(rz_s[ip][:], z_s[ip][:]),
                     waits=[(sSC, sSC.c)], inc=(sV, 1))
            st["rz_v"][i] = sV.c
            if adbg is not None and i == 3:
                rec.dma("sync", adbg["srow"], aa["srow"][:], waits=[(sV, sV.c)], inc=(stO.sems[0], 16))
                rec.dma("sync", adbg["prow"], prow_s[ip][:], inc=(stO.sems[0], 16))
                rec.dma("sync", adbg["z"], z_s[ip][:], inc=(stO.sems[0], 16))
                rec.emit("scalar", waits=[(stO.sems[0], stO.sems[0].c)])
                rec.emit("vector", waits=[(stO.sems[0], stO.sems[0].c)])

        def aa_pass2(i):
            ip = i % 2
            npan = i + 1
            for pan in range(npan):
                pp = pan % 2
                prs = []
                for gsub in range(2):
                    gc = 2 * pan + gsub
                    rr = gc % 2
                    lt = gc // 2
                    prs.append((aa["vsb"][pp][:, gsub * 2048:(gsub + 1) * 2048], v_r[rr, lt]))
                lv = stVP.batch(rec, "sync", pp, prs, waits=[st["v_guard"][pp], (sCC, ag_cc)])
                for gsub in range(2):
                    b = 2 * pan + gsub
                    tp = b % 2
                    tr = TRA if tp == 0 else TRB
                    rec.emit("tensor", lambda e, b=b, tr=tr, ip=ip: e.transpose(
                        tr, prow_s[ip][:, b * 128:(b + 1) * 128], ident[:]),
                        waits=[st["exp_of"][i], (sSC, sSC.c), (sGP, 3)], inc=(sPE, 1))
                    pe = sPE.c
                    rec.emit("scalar", lambda e, tp=tp, tr=tr: e.copy(aa["pt"][tp][:], tr),
                             waits=[(sPE, pe)], inc=(sSC, 1))
                    ptsc = sSC.c
                    w = [lv, (sSC, ptsc)]
                    if b == 0:
                        w.append(st["o1_prev"])
                    rec.emit("tensor", waits=w)
                    for d4 in range(4):
                        rec.emit("tensor", lambda e, tp=tp, gsub=gsub, d4=d4, b=b, npan=npan: e.matmul(
                            P[1 + d4 // 2][:, (d4 % 2) * 512:(d4 % 2) * 512 + 512],
                            aa["pt"][tp][:],
                            aa["vsb"][(b // 2) % 2][:, gsub * 2048 + d4 * 512: gsub * 2048 + (d4 + 1) * 512],
                            start=(b == 0), stop=(b == 2 * npan - 1)),
                            inc=(sPE, 1) if d4 == 3 else None)
                    st["v_guard"][pp] = (sPE, sPE.c)
            st["pt_done"][ip] = (sPE, sPE.c)
            pv_pe = sPE.c
            rec.emit("vector", lambda e, ip=ip: e.tensor_scalar(
                aa["pvn"][:, 0:1024], P[1][:], rz_s[ip][:, 0:1], None, ALU.mult),
                waits=[(sPE, pv_pe), (sV, st["rz_v"][i])], inc=(sV, 1))
            rec.emit("vector", lambda e, ip=ip: e.tensor_scalar(
                aa["pvn"][:, 1024:2048], P[2][:], rz_s[ip][:, 0:1], None, ALU.mult),
                inc=(sV, 1))
            pvn_v = sV.c
            for c in range(16):
                tr = TRA if c % 2 == 0 else TRB
                rec.emit("tensor", lambda e, c=c, tr=tr: e.transpose(
                    tr, aa["pvn"][:, c * 128:(c + 1) * 128], ident[:]),
                    waits=[(sV, pvn_v), (sSC, sSC.c)], inc=(sPE, 1))
                pe = sPE.c
                rec.emit("scalar", lambda e, c=c, tr=tr: e.copy(
                    aa["pvt"][:, c * 128:(c + 1) * 128], tr),
                    waits=[(sPE, pe)], inc=(sSC, 1))
            pvt_sc = sSC.c
            lx = stX.batch(rec, "sync", ip, [(aa["xf"][:], src[i])], waits=[(sV, sV.c)])
            rec.emit("tensor", waits=[(sSC, pvt_sc), opw])
            for oc in range(4):
                for c in range(16):
                    rec.emit("tensor", lambda e, oc=oc, c=c: e.matmul(
                        P[1 + oc // 2][:, (oc % 2) * 512:(oc % 2) * 512 + 512],
                        aa["pvt"][:, c * 128:(c + 1) * 128],
                        aa["opb"][:, c * 2048 + oc * 512: c * 2048 + (oc + 1) * 512],
                        start=(c == 0), stop=(c == 15)),
                        inc=(sPE, 1) if c == 15 else None)
            o_pe = sPE.c
            rec.emit("vector", lambda e, i=i: e.tensor_scalar(
                aa["o1"][:], aa["xf"][:], aa["lnst"][:, 2 * i:2 * i + 1], aa["lnst"][:, 2 * i + 1:2 * i + 2],
                ALU.mult, ALU.add),
                waits=[lx, (sV, sV.c), (stO.sems[0], stO.sems[0].c), (stO.sems[1], stO.sems[1].c)],
                inc=(sV, 1))
            rec.emit("vector", lambda e: e.tensor_add(aa["o1"][:, 0:1024], aa["o1"][:, 0:1024], P[1][:]),
                     waits=[(sPE, o_pe)], inc=(sV, 1))
            rec.emit("vector", lambda e: e.tensor_add(aa["o1"][:, 1024:2048], aa["o1"][:, 1024:2048], P[2][:]),
                     inc=(sV, 1))
            st["o1_prev"] = (sV, sV.c)
            st["q_guard"][ip] = (sPE, o_pe)
            ow = stO.sems[ip]
            rec.emit("vector", waits=[(ow, ow.c)])
            stO.batch(rec, "sync", ip, [(dst[i], aa["o1"][:])], waits=[(sV, sV.c)])

        import os as _os
        if _os.environ.get("AA_PIPE") != "1":
            for i in range(NT):
                aa_pass1(i)
                aa_softmax(i)
                aa_pass2(i)
        else:
            for i in range(NT):
                aa_pass1(i)
                if i >= 1:
                    aa_pass2(i - 1)
                aa_softmax(i)
            aa_pass2(NT - 1)

    ab = {
        "xn3": V(A3, 8192, H, BF16),
        "kbr": V(A3, 12288, H, F32),
        "vbr": V(A3, 20480, H, F32),
        "obr": V(A3, 12288, H, F32),
        "x": V(A3, 0, H, F32),
        "o1": V(A3, 20480, H, F32),
        "wc": [V(A2, 0, 512, BF16), V(A2, 1024, 512, BF16)],
        "pb": [V(A2, 0, H, BF16), V(A2, 4096, H, BF16)],
        "kpt": V(A2, 4096, H, BF16),
        "vpt": V(A2, 8192, H, BF16),
        "kvacc": V(A2, 12288, 2064, F32),
        "kvsb": V(A2, 20736, 2064, BF16),
        "qwc": [V(A2, 25088, 512, BF16), V(A2, 26112, 512, BF16)],
        "owc": [V(A2, 27136, 512, BF16), V(A2, 28160, 512, BF16)],
        "rzb": V(A2, 29184, 512, F32),
        "qpT": V(A2, 31232, 16 * 512, BF16),
        "atn": [V(A4, 8192, 512, BF16), V(A4, 9216, 512, BF16)],
        "el0": V(A4, 10240, 512, F32),
        "el1": V(A4, 12288, 512, F32),
        "msk8": V(A4, 14336, 512, U8),
        "qbc": alloc("ab_qbc", [128, 16], F32),
        "ones": alloc("ab_ones", [128, 1], BF16),
        "wc4": [V(A2, 0, 2048, BF16), V(A2, 4096, 2048, BF16)],
        "tw": [V(A2, 0, 2048, BF16), V(A2, 4096, 2048, BF16)],
    }

    pst_o = [[[P[(tt * 4 + eo) // 2][:, ((tt * 4 + eo) % 2) * 512:((tt * 4 + eo) % 2) * 512 + 512]
               for eo in range(4)] for tt in range(2)] for _ttp in range(2)]

    abp = {
        "pb": [V(A3, 12288, H, BF16), V(A3, 16384, H, BF16)],
        "tw": [V(A3, 20480, H, BF16), V(A3, 24576, H, BF16)],
    }
    wT_g = {"q": qwT_sc, "k": kwT_sc, "v": vwT_sc, "o": owT_sc}
    abprep = {"units": [], "state": None}

    def mk_attnb_prep_units(entry):
        stWc = Stream(nc, "stWc", 2)
        stBc = Stream(nc, "stBc", 2)
        guard = [(stBc.sems[0], 0), (stBc.sems[1], 0)]
        cast_done = {}
        for mi, mk4 in enumerate(("q", "k", "v", "o")):
            for st_ in range(16):
                p = st_ % 2
                w = [guard[p]] + (entry if (mi == 0 and st_ < 2) else [])
                lw = stWc.batch(rec, "gpsimd", p,
                                [(abp["pb"][p][:], bw_in[mk4][st_ * 128:(st_ + 1) * 128, :])],
                                waits=w)
                guard[p] = stBc.batch(rec, "gpsimd", p,
                                      [(w_bf[mk4][st_ * 128:(st_ + 1) * 128, :], abp["pb"][p][:])],
                                      waits=[lw])
            cast_done[mk4] = [guard[0], guard[1]]
        stTW = Stream(nc, "stTW", 2)
        stTS = Stream(nc, "stTS", 2)
        state = {"tw_guard": [(stTS.sems[0], 0), (stTS.sems[1], 0)]}

        def mk1(mk4, c):
            def fn():
                p = c % 2
                w = [state["tw_guard"][p]] + (cast_done[mk4] if c < 2 else []) \
                    + (entry if (mk4 == "q" and c < 2) else [])
                lw = stTW.batch(rec, "scalar", p,
                                [(abp["tw"][p][:], w_bf[mk4][:, c * 128:(c + 1) * 128])],
                                waits=w, transpose=True)
                state["tw_guard"][p] = stTS.batch(rec, "sync", p,
                                                  [(wT_g[mk4][c * 128:(c + 1) * 128, :], abp["tw"][p][:])],
                                                  waits=[lw])
            return fn

        abprep["units"] = [mk1(mk4, c) for mk4 in ("q", "k", "v", "o") for c in range(16)]
        abprep["state"] = state

    def drain_prep(n=1):
        while n > 0 and abprep["units"]:
            abprep["units"].pop(0)()
            n -= 1

    def emit_attnb(src, dst):
        stB = Stream(nc, "stB", 2)
        stKV = Stream(nc, "stKV", 2)
        xn3T = ff["xnT"]
        if abprep["state"] is None:
            mk_attnb_prep_units([(sPE, sPE.c), (sV, sV.c)] + ostores())
        drain_prep(len(abprep["units"]))
        wt_done = list(abprep["state"]["tw_guard"])
        wT_sc = wT_g
        lw_b = stD.batch(rec, "sync", 0, [(ab["kbr"][:], bb_in["k"].partition_broadcast(128))],
                         waits=wt_done)
        lw_b = stD.batch(rec, "sync", 0, [(ab["vbr"][:], bb_in["v"].partition_broadcast(128))])
        rec.dma("gpsimd", ab["qbc"][:],
                bb_in["q"].rearrange("o (c p) -> o c p", p=128).transpose([0, 2, 1])[0],
                inc=(sGP, 16), allow_slow_non_contiguous=True)
        qbc_gp = sGP.c
        rec.emit("gpsimd", lambda g: g.memset(ab["ones"][:], 1.0), inc=(sGP, 1))
        ones_gp = sGP.c
        for i in range(NT):
            w = [(sV, sV.c), (sSC, sSC.c)] + (ostores() if i == 0 else [])
            lw = stX.batch(rec, "sync", i % 2, [(ab["x"][:], src[i])], waits=w)
            rec.emit("vector", waits=[lw])
            emit_ln(ab["x"], ab["xn3"])
            vr = sV.c
            for c in range(16):
                tr = TRA if c % 2 == 0 else TRB
                rec.emit("tensor", lambda e, c=c, tr=tr: e.transpose(
                    tr, ab["xn3"][:, c * 128:(c + 1) * 128], ident[:]),
                    waits=[(sV, vr), (sGP, 3), (sSC, sSC.c)], inc=(sPE, 1))
                pe = sPE.c
                rec.emit("scalar", lambda e, c=c, i=i, tr=tr: e.copy(
                    xn3T[:, c * H + i * 128: c * H + (i + 1) * 128], tr),
                    waits=[(sPE, pe)], inc=(sSC, 1))
        xn3T_sc = sSC.c

        pst = [P[t // 2][:, (t % 2) * 512:(t % 2 + 1) * 512] for t in range(8)]
        first_bd = True
        for mk, biasr, do_elu, dstp in (("k", ab["kbr"], True, kp_sc), ("v", ab["vbr"], False, vp_sc)):
            for half in range(2):
                for dch4 in range(4):
                    pe_base = sPE.c
                    for cb in range(4):
                        p = cb % 2
                        if cb == 0:
                            w = [(sPE, pe_base)]
                            if first_bd:
                                w += [wt_done[0], wt_done[1], (sSC, xn3T_sc)]
                        else:
                            w = [(sPE, pe_base + 4 * (cb - 1))] if cb >= 2 else [(sPE, pe_base)]
                        lw = stI.batch(rec, "scalar", p,
                                       [(ab["wc4"][p][:, cc * 512:(cc + 1) * 512],
                                         wT_sc[mk][(cb * 4 + cc) * 128:(cb * 4 + cc + 1) * 128, dch4 * 512:(dch4 + 1) * 512])
                                        for cc in range(4)],
                                       waits=w)
                        w2 = [lw]
                        if cb == 0:
                            w2.append((sV, sV.c))
                        rec.emit("tensor", waits=w2)
                        for cc in range(4):
                            c = cb * 4 + cc
                            for t in range(8):
                                rec.emit("tensor", lambda e, p=p, c=c, cc=cc, t=t, half=half: e.matmul(
                                    pst[t], xn3T[:, c * H + half * 1024 + t * 128: c * H + half * 1024 + (t + 1) * 128],
                                    ab["wc4"][p][:, cc * 512:(cc + 1) * 512],
                                    start=(c == 0), stop=(c == 15)),
                                    inc=(sPE, 1) if t == 7 else None)
                    first_bd = False
                    pe_done = sPE.c
                    for t in range(8):
                        ep = t % 2
                        sEv = stEvB.sems[ep]
                        rec.emit("vector", lambda e, t=t, ep=ep, biasr=biasr, dch4=dch4: e.tensor_tensor(
                            ab["el0"][:] if False else ab["el0"][:], pst[t], biasr[:, dch4 * 512:(dch4 + 1) * 512], ALU.add),
                            waits=[(sPE, pe_done), lw_b, (sEv, sEv.c), (sV, sV.c)], inc=(sV, 1))
                        if do_elu:
                            rec.emit("vector", lambda e: e.tensor_single_scalar(ab["msk8"][:], ab["el0"][:], 0.0, ALU.is_gt))
                            rec.emit("scalar", lambda e: e.activation(ab["el1"][:], ab["el0"][:], AF.Exp),
                                     waits=[(sV, sV.c)], inc=(sSC, 1))
                            rec.emit("vector", lambda e: e.tensor_scalar_add(ab["el0"][:], ab["el0"][:], 1.0),
                                     waits=[(sSC, sSC.c)])
                            rec.emit("vector", lambda e, ep=ep: e.select(aa["ev"][ep][:], ab["msk8"][:], ab["el0"][:], ab["el1"][:]),
                                     inc=(sV, 1))
                        else:
                            rec.emit("vector", lambda e, ep=ep: e.tensor_copy(aa["ev"][ep][:], ab["el0"][:]), inc=(sV, 1))
                        stEvB.batch(rec, "sync", ep,
                                    [(dstp[(half * 8 + t) * 128:(half * 8 + t + 1) * 128, dch4 * 512:(dch4 + 1) * 512],
                                      aa["ev"][ep][:])],
                                    waits=[(sV, sV.c)])
        evb_all = [(sm, sm.c) for sm in stEvB.sems]

        rec.emit("vector", lambda e: e.memset(ab["kvacc"][:], 0.0), inc=(sV, 1))
        for i in range(NT):
            p = i % 2
            lw = stKV.batch(rec, "sync", p,
                            [(ab["kpt"][:], kp_sc[i * 128:(i + 1) * 128, :])],
                            waits=(evb_all if i == 0 else [(sPE, kv_pe)]))
            lw2 = stKV.batch(rec, "sync", p, [(ab["vpt"][:], vp_sc[i * 128:(i + 1) * 128, :])])
            w = [lw, lw2, (sGP, ones_gp)]
            if i > 0:
                w.append((sV, sV.c))
            rec.emit("tensor", waits=w)
            for h in range(16):
                rec.emit("tensor", lambda e, h=h: e.matmul(
                    P[h // 8][:, (h % 8) * 128:(h % 8) * 128 + 128],
                    ab["kpt"][:, h * 128:(h + 1) * 128],
                    ab["vpt"][:, h * 128:(h + 1) * 128],
                    start=True, stop=True))
                rec.emit("tensor", lambda e, h=h: e.matmul(
                    P[2][:, h:h + 1],
                    ab["kpt"][:, h * 128:(h + 1) * 128],
                    ab["ones"][:],
                    start=True, stop=True),
                    inc=(sPE, 1) if h == 15 else None)
            kv_pe = sPE.c
            rec.emit("vector", lambda e: e.tensor_add(ab["kvacc"][:, 0:1024], ab["kvacc"][:, 0:1024], P[0][:]),
                     waits=[(sPE, kv_pe)], inc=(sV, 1))
            rec.emit("vector", lambda e: e.tensor_add(ab["kvacc"][:, 1024:2048], ab["kvacc"][:, 1024:2048], P[1][:]))
            rec.emit("vector", lambda e: e.tensor_add(ab["kvacc"][:, 2048:2064], ab["kvacc"][:, 2048:2064], P[2][:, 0:16]),
                     inc=(sV, 1))
        kv_v = sV.c

        lw = stD.batch(rec, "sync", 0, [(kv_part, ab["kvacc"][:])], waits=[(sV, kv_v)])
        rec.emit("gpsimd", lambda g: g.collective_compute(
            "AllReduce", ALU.add, replica_groups=[[0, 1], [2, 3], [4, 5], [6, 7]],
            ins=[kv_part], outs=[kv_sum]), waits=[lw], inc=(sCC, 1))
        cc = sCC.c
        lw_kv = stD.batch(rec, "gpsimd", 0, [(ab["kvsb"][:], kv_sum)], waits=[(sCC, cc)])

        obw = stD.batch(rec, "sync", 0, [(ab["obr"][:], bb_in["o"].partition_broadcast(128))],
                        waits=[(sV, sV.c)])
        QP = [P[0][:, 0:512], P[0][:, 512:1024], P[1][:, 0:512], P[1][:, 512:1024]]
        AT = [P[1][:, 0:512], P[1][:, 512:1024]]
        ZR = P[2][:, 0:4]
        RB = P[2][:, 512:1024]
        qp_guard = (sSC, 0)
        at_guard = [(sV, 0), (sV, 0)]
        o2_guard = (sV, 0)
        for tch in range(4):
            qp_guard = (sV, sV.c)
            for dch4 in range(4):
                pe_base = sPE.c
                for cb in range(4):
                    pc = cb % 2
                    w = [(sPE, pe_base + 4 * (cb - 1))] if cb >= 2 else [(sPE, pe_base)]
                    if tch == 0 and dch4 == 0 and cb == 0:
                        w.append((sSC, xn3T_sc))
                    lw = stI.batch(rec, "scalar", pc,
                                   [(ab["wc4"][pc][:, cc * 512:(cc + 1) * 512],
                                     qwT_sc[(cb * 4 + cc) * 128:(cb * 4 + cc + 1) * 128, dch4 * 512:(dch4 + 1) * 512])
                                    for cc in range(4)],
                                   waits=w)
                    rec.emit("tensor", waits=[lw] + ([qp_guard] if cb == 0 else []))
                    for cc in range(4):
                        c = cb * 4 + cc
                        for di in range(4):
                            rec.emit("tensor", lambda e, pc=pc, c=c, cc=cc, di=di, tch=tch: e.matmul(
                                QP[di], ab["wc4"][pc][:, cc * 512 + di * 128: cc * 512 + (di + 1) * 128],
                                xn3T[:, c * H + tch * 512: c * H + (tch + 1) * 512],
                                start=(c == 0), stop=(c == 15)),
                                inc=(sPE, 1) if di == 3 else None)
                pe_q = sPE.c
                for di in range(4):
                    dch = dch4 * 4 + di
                    rec.emit("vector", lambda e, di=di, dch=dch: e.tensor_scalar(
                        ab["el0"][:], QP[di], ab["qbc"][:, dch:dch + 1], None, ALU.add),
                        waits=[(sPE, pe_q), (sGP, qbc_gp), (sV, sV.c)], inc=(sV, 1))
                    rec.emit("vector", lambda e: e.tensor_single_scalar(ab["msk8"][:], ab["el0"][:], 0.0, ALU.is_gt))
                    rec.emit("scalar", lambda e: e.activation(ab["el1"][:], ab["el0"][:], AF.Exp),
                             waits=[(sV, sV.c)], inc=(sSC, 1))
                    rec.emit("vector", lambda e: e.tensor_scalar_add(ab["el0"][:], ab["el0"][:], 1.0),
                             waits=[(sSC, sSC.c)])
                    rec.emit("vector", lambda e, dch=dch: e.select(ab["qpT"][:, dch * 512:(dch + 1) * 512],
                                                                   ab["msk8"][:], ab["el0"][:], ab["el1"][:]),
                             inc=(sV, 1))
                qp_guard = (sV, sV.c)
            qp_v = sV.c
            qp_guard = (sV, qp_v)
            for h in range(16):
                hp = h % 2
                rec.emit("tensor", waits=[(sV, sV.c), lw_kv])
                rec.emit("tensor", lambda e, h=h, hp=hp: e.matmul(
                    AT[hp], ab["kvsb"][:, h * 128:(h + 1) * 128],
                    ab["qpT"][:, h * 512:(h + 1) * 512],
                    start=True, stop=True))
                rec.emit("tensor", lambda e, h=h: e.matmul(
                    P[2][0:1, 0:512],
                    ab["kvsb"][:, 2048 + h:2048 + h + 1],
                    ab["qpT"][:, h * 512:(h + 1) * 512],
                    start=True, stop=True), inc=(sPE, 1))
                pe_at = sPE.c
                rec.emit("vector", lambda e: e.tensor_scalar_add(ab["rzb"][0:1, 0:512], P[2][0:1, 0:512], EPS_B),
                         waits=[(sPE, pe_at), o2_guard], inc=(sV, 1))
                rec.emit("vector", lambda e: e.reciprocal(ab["rzb"][0:1, 0:512], ab["rzb"][0:1, 0:512]),
                         waits=[(sV, sV.c)], inc=(sV, 1))
                rz_v = sV.c
                rec.emit("tensor", lambda e: e.matmul(
                    RB, onesf[0:1, :], ab["rzb"][0:1, 0:512], start=True, stop=True),
                    waits=[(sV, rz_v), (sGP, 3)], inc=(sPE, 1))
                pe_rb = sPE.c
                rec.emit("scalar", lambda e: e.copy(ab["el1"][:], RB),
                         waits=[(sPE, pe_rb), (sSC, sSC.c)], inc=(sSC, 1))
                rec.emit("vector", lambda e, h=h, hp=hp: e.tensor_tensor(
                    ab["qpT"][:, h * 512:(h + 1) * 512], AT[hp], ab["el1"][:], ALU.mult),
                    waits=[(sSC, sSC.c)], inc=(sV, 1))
                at_guard[hp] = (sV, sV.c)
            atn_v = sV.c
            for ttp in range(2):
                pe_base = sPE.c
                for eo in range(4):
                    for cb in range(4):
                        ci = eo * 4 + cb
                        pc = ci % 2
                        w = [(sPE, pe_base)] if ci < 2 else [(sPE, pe_base + 4 * (ci - 1))]
                        lw = stI.batch(rec, "scalar", pc,
                                       [(ab["wc4"][pc][:, cc * 512:(cc + 1) * 512],
                                         owT_sc[(cb * 4 + cc) * 128:(cb * 4 + cc + 1) * 128, eo * 512:(eo + 1) * 512])
                                        for cc in range(4)],
                                       waits=w)
                        w2 = [lw, (sV, atn_v)]
                        if eo == 0 and cb == 0:
                            w2.append(o2_guard)
                        rec.emit("tensor", waits=w2)
                        for cc in range(4):
                            c = cb * 4 + cc
                            for tt in range(2):
                                rec.emit("tensor", lambda e, pc=pc, c=c, cc=cc, eo=eo, tt=tt, ttp=ttp: e.matmul(
                                    pst_o[ttp][tt][eo],
                                    ab["qpT"][:, c * 512 + ttp * 256 + tt * 128: c * 512 + ttp * 256 + (tt + 1) * 128],
                                    ab["wc4"][pc][:, cc * 512:(cc + 1) * 512],
                                    start=(c == 0), stop=(c == 15)),
                                    inc=(sPE, 1) if tt == 1 else None)
                pe_o = sPE.c
                for tt in range(2):
                    gt = tch * 4 + ttp * 2 + tt
                    ip = gt % 2
                    lx = stX.batch(rec, "sync", ip, [(ab["x"][:], src[gt])],
                                   waits=[(sV, sV.c)])
                    ow = stO.sems[ip]
                    for eo in range(4):
                        rec.emit("vector", lambda e, tt=tt, ttp=ttp, eo=eo: e.tensor_tensor(
                            ab["o1"][:, eo * 512:(eo + 1) * 512], pst_o[ttp][tt][eo],
                            ab["obr"][:, eo * 512:(eo + 1) * 512], ALU.add),
                            waits=([(sPE, pe_o), obw,
                                    (stO.sems[0], stO.sems[0].c), (stO.sems[1], stO.sems[1].c)]
                                   if eo == 0 else []),
                            inc=(sV, 1) if eo == 3 else None)
                    rec.emit("vector", lambda e: e.tensor_add(ab["o1"][:], ab["o1"][:], ab["x"][:]),
                             waits=[lx], inc=(sV, 1))
                    o2_guard = (sV, sV.c)
                    stO.batch(rec, "sync", ip, [(dst[gt], ab["o1"][:])], waits=[(sV, sV.c)])

    def emit_ffn(src, dst, deq_cc, dbg=None):
        for i in range(NT):
            w = [(sV, sV.c), (sSC, sSC.c)] + (ostores() if i == 0 else [])
            lw = stX.batch(rec, "sync", i % 2, [(ff["x"][:], src[i])], waits=w)
            rec.emit("vector", waits=[lw])
            emit_ln(ff["x"], ff["xn"])
            vr = sV.c
            if dbg is not None and i == 0:
                rec.dma("sync", dbg["xn"], ff["xn"][:], waits=[(sV, vr)], inc=(stO.sems[0], 16))
                rec.dma("sync", dbg["xc"], lnb["xc"][:], inc=(stO.sems[0], 16))
                rec.dma("sync", dbg["mu"], lnb["mu"][:], inc=(stO.sems[0], 16))
                rec.dma("sync", dbg["rs"], lnb["rs"][:], inc=(stO.sems[0], 16))
                rec.dma("sync", dbg["xin"], ff["x"][:], inc=(stO.sems[0], 16))
                rec.emit("vector", waits=[(stO.sems[0], stO.sems[0].c)])
            for c in range(16):
                tr = TRA if c % 2 == 0 else TRB
                rec.emit("tensor", lambda e, c=c, tr=tr: e.transpose(
                    tr, ff["xn"][:, c * 128:(c + 1) * 128], ident[:]),
                    waits=[(sV, vr), (sGP, 3), (sSC, sSC.c)], inc=(sPE, 1))
                pe = sPE.c
                rec.emit("scalar", lambda e, c=c, i=i, tr=tr: e.copy(
                    ff["xnT"][:, c * H + i * 128: c * H + (i + 1) * 128], tr),
                    waits=[(sPE, pe)], inc=(sSC, 1))
        xnT_sc = sSC.c

        pe_base = sPE.c
        v_evac = {}
        st_strip = {}
        for j in range(128):
            p = j % 2
            w = [(sCC, deq_cc), (sPE, pe_base)] if j < 2 else [(sPE, pe_base + 4 * j - 4)]
            lw = stW.batch(rec, "scalar", p,
                           [(ff["wg"][p][:], wg_all[j * 128:(j + 1) * 128, :]),
                            (ff["wu"][p][:], wu_all[j * 128:(j + 1) * 128, :])],
                           waits=w)
            for th in range(2):
                u = 2 * j + th
                q = u % 2
                w = [lw, (sSC, xnT_sc)]
                if u >= 2:
                    w.append((sV, v_evac[u - 2]))
                rec.emit("tensor", waits=w)
                for tc in range(2):
                    for c in range(16):
                        last = tc == 1 and c == 15
                        rec.emit("tensor", lambda e, p=p, q=q, th=th, tc=tc, c=c: e.matmul(
                            P[q][:, tc * 512:(tc + 1) * 512],
                            ff["wg"][p][:, c * 128:(c + 1) * 128],
                            ff["xnT"][:, c * H + th * 1024 + tc * 512: c * H + th * 1024 + (tc + 1) * 512],
                            start=(c == 0), stop=(c == 15)),
                            inc=(sPE, 1) if last else None)
                pe_gate = sPE.c
                for tc in range(2):
                    for c in range(16):
                        last = tc == 1 and c == 15
                        rec.emit("tensor", lambda e, p=p, q=q, th=th, tc=tc, c=c: e.matmul(
                            P[2 + q][:, tc * 512:(tc + 1) * 512],
                            ff["wu"][p][:, c * 128:(c + 1) * 128],
                            ff["xnT"][:, c * H + th * 1024 + tc * 512: c * H + th * 1024 + (tc + 1) * 512],
                            start=(c == 0), stop=(c == 15)),
                            inc=(sPE, 1) if last else None)
                pe_up = sPE.c
                rec.emit("scalar", lambda e, q=q: e.activation(ff["gl"][:], P[q][:], AF.Gelu_apprx_tanh),
                         waits=[(sPE, pe_gate)], inc=(sSC, 1))
                scd = sSC.c
                w = [(sSC, scd), (sPE, pe_up)]
                if th == 0 and j >= 2:
                    w.append(st_strip[j - 2])
                rec.emit("vector", lambda e, p=p, q=q, th=th: e.tensor_tensor(
                    ff["int"][p][:, th * 1024:(th + 1) * 1024], ff["gl"][:], P[2 + q][:], ALU.mult),
                    waits=w, inc=(sV, 1))
                v_evac[u] = sV.c
            st_strip[j] = stS.batch(rec, "sync", p,
                                    [(inter_sc[j * 128:(j + 1) * 128, :], ff["int"][p][:])],
                                    waits=[(sV, v_evac[2 * j + 1])])
            if j % 2 == 1:
                drain_prep()
        strips_all = (stS.sems[1], stS.sems[1].c)
        strips_all0 = (stS.sems[0], stS.sems[0].c)

        pst = [P[t // 2][:, (t % 2) * 512:(t % 2 + 1) * 512] for t in range(8)]
        for tg in range(2):
            for hc in range(4):
                pe_base2 = sPE.c
                v_prev = sV.c
                for g in range(32):
                    p = g % 2
                    if g < 2:
                        w = [strips_all, strips_all0, (sCC, deq_cc)] if (tg == 0 and hc == 0) \
                            else [(sPE, pe_base2)]
                    else:
                        w = [(sPE, pe_base2 + 4 * (g - 1))]
                    pairs = [(ff["ic4"][p][:, jj * 1024:(jj + 1) * 1024],
                              inter_sc[(g * 4 + jj) * 128:(g * 4 + jj + 1) * 128, tg * 1024:(tg + 1) * 1024])
                             for jj in range(4)]
                    pairs += [(ff["wd4"][p][:, jj * 512:(jj + 1) * 512],
                               wd_all[(g * 4 + jj) * 128:(g * 4 + jj + 1) * 128, hc * 512:(hc + 1) * 512])
                              for jj in range(4)]
                    lw = stI.batch(rec, "scalar", p, pairs, waits=w)
                    w = [lw]
                    if g == 0:
                        w.append((sV, v_prev))
                    rec.emit("tensor", waits=w)
                    for jj in range(4):
                        j = g * 4 + jj
                        for t in range(8):
                            rec.emit("tensor", lambda e, p=p, t=t, j=j, jj=jj: e.matmul(
                                pst[t], ff["ic4"][p][:, jj * 1024 + t * 128: jj * 1024 + (t + 1) * 128],
                                ff["wd4"][p][:, jj * 512:(jj + 1) * 512],
                                start=(j == 0), stop=(j == 127)),
                                inc=(sPE, 1) if t == 7 else None)
                pe_done = sPE.c
                for t in range(8):
                    p = t % 2
                    lw = stX.batch(rec, "sync", p,
                                   [(ff["xp"][p][:], src[tg * 8 + t][:, hc * 512:(hc + 1) * 512])],
                                   waits=[(sV, sV.c)])
                    ow = stO.sems[p]
                    rec.emit("vector", lambda e, p=p, t=t: e.tensor_tensor(
                        ff["op"][p][:], pst[t], ff["xp"][p][:], ALU.add),
                        waits=[(sPE, pe_done), lw, (ow, ow.c)], inc=(sV, 1))
                    stO.batch(rec, "sync", p,
                              [(dst[tg * 8 + t][:, hc * 512:(hc + 1) * 512], ff["op"][p][:])],
                              waits=[(sV, sV.c)])
        return

    emit_identity()

    if mode == "full":
        weave = (deq_tile_emitters("g", gpk, gsc, wg_sl)
                 + deq_tile_emitters("u", upk, usc, wu_sl)
                 + deq_tile_emitters("d", dpk, dsc, wd_sl))
        emit_attna(x_my, xb1, weave=weave)
        emit_deq_gather(wg_sl, wg_all)
        emit_deq_gather(wu_sl, wu_all)
        cc = emit_deq_gather(wd_sl, wd_all)
        mk_attnb_prep_units([(sV, sV.c)] + ostores())
        emit_ffn(xb1, xb2, cc)
        emit_attnb(xb2, xb3)
        emit_ffn(xb3, out_my, cc)
        if False:
            fd = {
                "x1": ap(nc.dram_tensor("fdbg_x1", [NT, 128, H], F32, kind="ExternalOutput")),
                "x2": ap(nc.dram_tensor("fdbg_x2", [NT, 128, H], F32, kind="ExternalOutput")),
                "x3": ap(nc.dram_tensor("fdbg_x3", [NT, 128, H], F32, kind="ExternalOutput")),
            }
            w = [(sm, sm.c) for sm in stO.sems] + [(sV, sV.c), (sSC, sSC.c), (sPE, sPE.c)]
            rec.dma("sync", fd["x1"], xb1, waits=w, inc=(stO.sems[0], 16))
            rec.dma("sync", fd["x2"], xb2, inc=(stO.sems[0], 16))
            rec.dma("sync", fd["x3"], xb3, inc=(stO.sems[0], 16))
        for sm in stO.sems:
            rec.emit("sync", waits=[(sm, sm.c)])
    elif mode in ("ffn", "ffn_dbg"):
        emit_dequant("g", gpk, gsc, wg_sl)
        emit_dequant("u", upk, usc, wu_sl)
        emit_dequant("d", dpk, dsc, wd_sl)
        emit_deq_gather(wg_sl, wg_all)
        emit_deq_gather(wu_sl, wu_all)
        cc = emit_deq_gather(wd_sl, wd_all)
        dbg = None
        if mode == "ffn_dbg":
            dbg = {
                "xn": ap(nc.dram_tensor("dbg_xn", [128, H], BF16, kind="ExternalOutput")),
                "xc": ap(nc.dram_tensor("dbg_xc", [128, H], F32, kind="ExternalOutput")),
                "mu": ap(nc.dram_tensor("dbg_mu", [128, 1], F32, kind="ExternalOutput")),
                "rs": ap(nc.dram_tensor("dbg_rs", [128, 1], F32, kind="ExternalOutput")),
                "xin": ap(nc.dram_tensor("dbg_xin", [128, H], F32, kind="ExternalOutput")),
            }
        emit_ffn(x_my, out_my, cc, dbg=dbg)
        if mode == "ffn_dbg":
            dbg_wg = ap(nc.dram_tensor("dbg_wg", [NCORES * 16 * 128, 2048], BF16, kind="ExternalOutput"))
            dbg_wd = ap(nc.dram_tensor("dbg_wd", [NCORES * 16 * 128, 2048], BF16, kind="ExternalOutput"))
            dbg_int = ap(nc.dram_tensor("dbg_int", [128 * 128, 2048], BF16, kind="ExternalOutput"))
            dbg_xnT = ap(nc.dram_tensor("dbg_xnT", [128, 16 * H], BF16, kind="ExternalOutput"))
            w = [(sm, sm.c) for sm in stO.sems] + [(sV, sV.c), (sSC, sSC.c), (sPE, sPE.c)]
            rec.dma("sync", dbg_wg, wg_all, waits=w, inc=(stO.sems[0], 16))
            rec.dma("sync", dbg_wd, wd_all, inc=(stO.sems[0], 16))
            rec.dma("sync", dbg_int, inter_sc, inc=(stO.sems[0], 16))
            rec.dma("sync", dbg_xnT, ff["xnT"][:], inc=(stO.sems[0], 16))
        for sm in stO.sems:
            rec.emit("sync", waits=[(sm, sm.c)])
    elif mode in ("attnb", "attnb_dbg"):
        emit_attnb(x_my, out_my)
        if mode == "attnb_dbg":
            bd = {
                "kp": ap(nc.dram_tensor("bdbg_kp", [16 * 128, 2048], BF16, kind="ExternalOutput")),
                "vp": ap(nc.dram_tensor("bdbg_vp", [16 * 128, 2048], BF16, kind="ExternalOutput")),
                "kvp": ap(nc.dram_tensor("bdbg_kvp", [128, 2064], F32, kind="ExternalOutput")),
                "kvs": ap(nc.dram_tensor("bdbg_kvs", [128, 2064], F32, kind="ExternalOutput")),
            }
            w = [(sm, sm.c) for sm in stO.sems] + [(sV, sV.c), (sSC, sSC.c), (sPE, sPE.c)]
            rec.dma("sync", bd["kp"], kp_sc, waits=w, inc=(stO.sems[0], 16))
            rec.dma("sync", bd["vp"], vp_sc, inc=(stO.sems[0], 16))
            rec.dma("sync", bd["kvp"], kv_part, inc=(stO.sems[0], 16))
            rec.dma("sync", bd["kvs"], kv_sum, inc=(stO.sems[0], 16))
        for sm in stO.sems:
            rec.emit("sync", waits=[(sm, sm.c)])
    elif mode in ("attna", "attna_dbg"):
        adbg = None
        if mode == "attna_dbg":
            adbg = {
                "qT": ap(nc.dram_tensor("adbg_qT", [16 * 128, 2048], BF16, kind="ExternalOutput")),
                "kTa": ap(nc.dram_tensor("adbg_kTa", [2 * 16 * 128, 2048], BF16, kind="ExternalOutput")),
                "va": ap(nc.dram_tensor("adbg_va", [2 * 16 * 128, 2048], BF16, kind="ExternalOutput")),
                "srow": ap(nc.dram_tensor("adbg_srow", [128, 4096], F32, kind="ExternalOutput")),
                "prow": ap(nc.dram_tensor("adbg_prow", [128, 4096], BF16, kind="ExternalOutput")),
                "z": ap(nc.dram_tensor("adbg_z", [128, 1], F32, kind="ExternalOutput")),
            }
        emit_attna(x_my, out_my, adbg=adbg)
        for sm in stO.sems:
            rec.emit("sync", waits=[(sm, sm.c)])
    else:
        raise NotImplementedError(mode)

    rec.replay()
    return nc


def shard_inputs(inputs):
    x = np.ascontiguousarray(np.asarray(inputs["L0_attn_a_x"], dtype=np.float32))
    qkv_w = np.ascontiguousarray(np.asarray(inputs["L0_attn_a_qkv"], dtype=np.float32))
    oproj = np.ascontiguousarray(np.asarray(inputs["L0_attn_a_o_proj"], dtype=np.float32))
    gamma = np.ascontiguousarray(np.asarray(inputs["L0_attn_a_gamma"], dtype=np.float32).reshape(1, H))
    beta = np.ascontiguousarray(np.asarray(inputs["L0_attn_a_beta"], dtype=np.float32).reshape(1, H))
    gpk = np.ascontiguousarray(np.asarray(inputs["L0_ffn_gate_W_packed"], dtype=np.int32))
    gsc = np.ascontiguousarray(np.asarray(inputs["L0_ffn_gate_scales"], dtype=np.float32))
    upk = np.ascontiguousarray(np.asarray(inputs["L0_ffn_up_W_packed"], dtype=np.int32))
    usc = np.ascontiguousarray(np.asarray(inputs["L0_ffn_up_scales"], dtype=np.float32))
    dpk = np.ascontiguousarray(np.asarray(inputs["L0_ffn_down_W_packed"], dtype=np.int32))
    dsc = np.ascontiguousarray(np.asarray(inputs["L0_ffn_down_scales"], dtype=np.float32))
    wb = {k: np.ascontiguousarray(np.asarray(inputs[f"L0_attn_b_{k}_w"], dtype=np.float32))
          for k in ("q", "k", "v", "o")}
    bb = {k: np.ascontiguousarray(np.asarray(inputs[f"L0_attn_b_{k}_b"], dtype=np.float32).reshape(1, H))
          for k in ("q", "k", "v", "o")}

    in_maps = []
    for c in range(NCORES):
        b, par = c // 2, c % 2
        xt = np.ascontiguousarray(x[b].reshape(32, 128, H)[par::2])
        mk = np.zeros((NT, 128, 256), dtype=np.float32)
        for i in range(NT):
            g = 2 * i + par
            r0 = 128 * g
            np_pan = g // 2 + 1
            c0 = 256 * (np_pan - 1)
            rr = r0 + np.arange(128)[:, None]
            cc = c0 + np.arange(256)[None, :]
            mk[i] = np.where(cc <= rr, 0.0, NEG).astype(np.float32)
        nsl = slice(c * 2048, (c + 1) * 2048)
        m = {
            "x_my": xt, "masks": mk,
            "qkv": qkv_w, "oproj": oproj, "gamma": gamma, "beta": beta,
            "bqw": wb["q"], "bkw": wb["k"], "bvw": wb["v"], "bow": wb["o"],
            "bqb": bb["q"], "bkb": bb["k"], "bvb": bb["v"], "bob": bb["o"],
            "gpk": np.ascontiguousarray(gpk[nsl]), "gsc": np.ascontiguousarray(gsc[nsl]),
            "upk": np.ascontiguousarray(upk[nsl]), "usc": np.ascontiguousarray(usc[nsl]),
            "dpk": np.ascontiguousarray(dpk[:, c * 1024:(c + 1) * 1024]),
            "dsc": np.ascontiguousarray(dsc[:, c * 64:(c + 1) * 64]),
        }
        in_maps.append(m)
    return in_maps


def gather_output(results):
    out = np.zeros((B, S, H), dtype=np.float32)
    for c in range(NCORES):
        b, par = c // 2, c % 2
        out[b].reshape(32, 128, H)[par::2] = np.asarray(results[c]["out_my"], dtype=np.float32)
    return out


_NC_CACHE = {}


def kernel(**inputs):
    mode = "full"
    if mode not in _NC_CACHE:
        _NC_CACHE[mode] = build_program(mode)
    nc = _NC_CACHE[mode]
    in_maps = shard_inputs(inputs)
    res = run_bass_kernel_spmd(nc, in_maps, core_ids=list(range(NCORES)))
    return gather_output(res.results)
